# version 01
# baseline (speedup 1.0000x reference)
import contextlib
import numpy as np
import ml_dtypes
import concourse.bass as bass
import concourse.mybir as mybir
from concourse.bass_utils import run_bass_kernel_spmd

F32 = mybir.dt.float32
BF16 = mybir.dt.bfloat16
AF = mybir.ActivationFunctionType
ALU = mybir.AluOpType
AX = mybir.AxisListType

B, L, D, E, NST, R = 2, 8192, 1024, 2048, 16, 64
ES = 512
NC = 8
LQ = L // 4
EPS = 1e-5


def _w(eng, sem, v):
    if v > 0:
        eng.wait_ge(sem, v)


def build_stage1():
    LC = 512
    NCH = L // LC
    NT = LC // 128

    nc = bass.Bass("TRN2", target_bir_lowering=False)
    x = nc.dram_tensor("x", [L, D], F32, kind="ExternalInput").ap()
    w_in = nc.dram_tensor("w_in", [1024, D], F32, kind="ExternalInput").ap()
    conv_w = nc.dram_tensor("conv_w", [ES, 4], F32, kind="ExternalInput").ap()
    conv_b = nc.dram_tensor("conv_b", [128, 4], F32, kind="ExternalInput").ap()
    w_x = nc.dram_tensor("w_x", [96, ES], F32, kind="ExternalInput").ap()
    u_out = nc.dram_tensor("u_out", [ES, L], F32, kind="ExternalOutput").ap()
    g2_out = nc.dram_tensor("g2_out", [ES, L], F32, kind="ExternalOutput").ap()
    pt_out = nc.dram_tensor("pt_out", [96, L], F32, kind="ExternalOutput").ap()
    xn_out = nc.dram_tensor("xn_out", [D, L], BF16, kind="ExternalOutput").ap()

    u_out_r = u_out.rearrange("(dj p) l -> p dj l", p=128)
    g2_out_r = g2_out.rearrange("(dj p) l -> p dj l", p=128)
    xn_out_r = xn_out.rearrange("(j p) l -> p j l", p=128)

    with contextlib.ExitStack() as _es:
        io_s = _es.enter_context(nc.sbuf_tensor("io_s", [128, 128], F32))
        id_s = _es.enter_context(nc.sbuf_tensor("id_s", [128, 128], F32))
        winT = _es.enter_context(nc.sbuf_tensor("winT", [128, 8 * 8 * 128], BF16))
        wxT = _es.enter_context(nc.sbuf_tensor("wxT", [128, 4 * 96], BF16))
        cw_s = _es.enter_context(nc.sbuf_tensor("cw_s", [128, 16], F32))
        cb_s = _es.enter_context(nc.sbuf_tensor("cb_s", [128, 4], F32))
        x_sb = _es.enter_context(nc.sbuf_tensor("x_sb", [128, 2, NT, 1024], F32))
        bst_s = _es.enter_context(nc.sbuf_tensor("bst_s", [128, NT * 12], F32))
        mv_s = _es.enter_context(nc.sbuf_tensor("mv_s", [128, NT * 2], F32))
        rstd_s = _es.enter_context(nc.sbuf_tensor("rstd_s", [128, NT], F32))
        xnt_s = _es.enter_context(nc.sbuf_tensor("xnt_s", [128, NT, 1024], F32))
        xnf_s = _es.enter_context(nc.sbuf_tensor("xnf_s", [128, 2, 8, LC], BF16))
        xz_s = _es.enter_context(nc.sbuf_tensor("xz_s", [128, 8, LC], F32))
        pad_s = _es.enter_context(nc.sbuf_tensor("pad_s", [128, 4, LC + 3], F32))
        cva_s = _es.enter_context(nc.sbuf_tensor("cva_s", [128, LC], F32))
        cvb_s = _es.enter_context(nc.sbuf_tensor("cvb_s", [128, LC], F32))
        conv_s = _es.enter_context(nc.sbuf_tensor("conv_s", [128, 4, LC], F32))
        sig_s = _es.enter_context(nc.sbuf_tensor("sig_s", [128, 8, LC], F32))
        u_s = _es.enter_context(nc.sbuf_tensor("u_s", [128, 2, 4, LC], F32))
        ub_s = _es.enter_context(nc.sbuf_tensor("ub_s", [128, 4, LC], BF16))
        g2_s = _es.enter_context(nc.sbuf_tensor("g2_s", [128, 2, 4, LC], F32))
        pt_s = _es.enter_context(nc.sbuf_tensor("pt_s", [96, 2, LC], F32))
        wst_s = _es.enter_context(nc.sbuf_tensor("wst_s", [96, 512], F32))
        cst_s = _es.enter_context(nc.sbuf_tensor("cst_s", [128, 1], F32))
        ps_tp = _es.enter_context(nc.psum_tensor("ps_tp", [128, 2, 512], F32))
        ps_xz = _es.enter_context(nc.psum_tensor("ps_xz", [128, 2, 512], F32))
        ps_pt = _es.enter_context(nc.psum_tensor("ps_pt", [96, 512], F32))
        s_prep = _es.enter_context(nc.semaphore("s_prep"))
        s_io = _es.enter_context(nc.semaphore("s_io"))
        s_x = _es.enter_context(nc.semaphore("s_x"))
        s_stats = _es.enter_context(nc.semaphore("s_stats"))
        s_rsl = _es.enter_context(nc.semaphore("s_rsl"))
        s_aff = _es.enter_context(nc.semaphore("s_aff"))
        s_tp = _es.enter_context(nc.semaphore("s_tp"))
        s_tpe = _es.enter_context(nc.semaphore("s_tpe"))
        s_xz = _es.enter_context(nc.semaphore("s_xz"))
        s_xze = _es.enter_context(nc.semaphore("s_xze"))
        s_conv = _es.enter_context(nc.semaphore("s_conv"))
        s_sig = _es.enter_context(nc.semaphore("s_sig"))
        s_mul = _es.enter_context(nc.semaphore("s_mul"))
        s_pt = _es.enter_context(nc.semaphore("s_pt"))
        s_pte = _es.enter_context(nc.semaphore("s_pte"))
        s_out = _es.enter_context(nc.semaphore("s_out"))
        block = _es.enter_context(nc.Block())
        nc.const_aps.register(F32, EPS, cst_s[:, 0:1])

        @block.sync
        def _(sync):
            sync.dma_start(out=x_sb.ap().rearrange("p b t f -> p (b t) f"),
                           in_=w_in.rearrange("(mi p) k -> p mi k", p=128)).then_inc(s_prep, 16)
            sync.dma_start(out=wst_s[:], in_=w_x).then_inc(s_prep, 16)
            sync.dma_start(out=cw_s.ap().rearrange("p (dj k) -> p dj k", k=4),
                           in_=conv_w.rearrange("(dj p) k -> p dj k", p=128)).then_inc(s_prep, 16)
            sync.dma_start(out=cb_s[:], in_=conv_b).then_inc(s_prep, 16)
            for c in range(NCH):
                if c < 2:
                    sync.wait_ge(s_tp, 64)
                else:
                    sync.wait_ge(s_aff, c - 1)
                sync.dma_start(out=x_sb[:, c % 2],
                               in_=x[c * LC:(c + 1) * LC, :].rearrange("(i p) f -> p i f", p=128)
                               ).then_inc(s_x, 16)

        @block.gpsimd
        def _(gpsimd):
            gpsimd.iota(out=io_s[:], pattern=[[1, 128]], base=0, channel_multiplier=-1,
                        allow_small_or_imprecise_dtypes=True).then_inc(s_io, 1)
            for c in range(NCH):
                gpsimd.wait_ge(s_mul, c + 1)
                gpsimd.wait_ge(s_pte, c + 1)
                sl = slice(c * LC, (c + 1) * LC)
                gpsimd.dma_start(out=u_out_r[:, :, sl], in_=u_s[:, c % 2]).then_inc(s_out, 16)
                gpsimd.dma_start(out=g2_out_r[:, :, sl], in_=g2_s[:, c % 2]).then_inc(s_out, 16)
                gpsimd.dma_start(out=pt_out[:, sl], in_=pt_s[:, c % 2]).then_inc(s_out, 16)
                gpsimd.dma_start(out=xn_out_r[:, :, sl], in_=xnf_s[:, c % 2]).then_inc(s_out, 16)
            gpsimd.wait_ge(s_out, NCH * 64)

        @block.vector
        def _(vector):
            vector.wait_ge(s_io, 1)
            vector.tensor_scalar(out=id_s[:], in0=io_s[:], scalar1=0.0, op0=ALU.is_equal,
                                 scalar2=None).then_inc(s_io, 1)
            vector.memset(cst_s[:], EPS)
            vector.memset(pad_s.ap().rearrange("p a b -> p (a b)"), 0.0)
            for c in range(NCH):
                vector.wait_ge(s_x, 16 * (c + 1))
                for i in range(NT):
                    vector.bn_stats(out=bst_s[:, i * 12:i * 12 + 6], in_=x_sb[:, c % 2, i, 0:512])
                    vector.bn_stats(out=bst_s[:, i * 12 + 6:i * 12 + 12], in_=x_sb[:, c % 2, i, 512:1024])
                for i in range(NT):
                    ii = vector.bn_aggr(out=mv_s[:, 2 * i:2 * i + 2], in_=bst_s[:, i * 12:(i + 1) * 12])
                ii.then_inc(s_stats, 1)
                vector.wait_ge(s_rsl, c + 1)
                _w(vector, s_tp, 68 + 8 * c if c > 0 else 0)
                for i in range(NT):
                    ii = vector.tensor_scalar(out=xnt_s[:, i, :], in0=x_sb[:, c % 2, i, :],
                                              scalar1=mv_s[:, 2 * i:2 * i + 1], op0=ALU.subtract,
                                              scalar2=rstd_s[:, i:i + 1], op1=ALU.mult)
                ii.then_inc(s_aff, 1)
                vector.wait_ge(s_xze, 8 * c + 4)
                for dj in range(4):
                    vector.tensor_copy(out=pad_s[:, dj, 3:3 + LC], in_=xz_s[:, dj, :])
                for dj in range(4):
                    vector.tensor_scalar(out=cva_s[:], in0=pad_s[:, dj, 0:LC],
                                         scalar1=cw_s[:, 4 * dj:4 * dj + 1], op0=ALU.mult,
                                         scalar2=cb_s[:, dj:dj + 1], op1=ALU.add)
                    vector.scalar_tensor_tensor(out=cvb_s[:], in0=pad_s[:, dj, 1:1 + LC],
                                                scalar=cw_s[:, 4 * dj + 1:4 * dj + 2],
                                                in1=cva_s[:], op0=ALU.mult, op1=ALU.add)
                    vector.scalar_tensor_tensor(out=cva_s[:], in0=pad_s[:, dj, 2:2 + LC],
                                                scalar=cw_s[:, 4 * dj + 2:4 * dj + 3],
                                                in1=cvb_s[:], op0=ALU.mult, op1=ALU.add)
                    vector.scalar_tensor_tensor(out=conv_s[:, dj, :], in0=pad_s[:, dj, 3:3 + LC],
                                                scalar=cw_s[:, 4 * dj + 3:4 * dj + 4],
                                                in1=cva_s[:], op0=ALU.mult, op1=ALU.add)
                    ii = vector.tensor_copy(out=pad_s[:, dj, 0:3], in_=pad_s[:, dj, LC:LC + 3])
                ii.then_inc(s_conv, 1)
                vector.wait_ge(s_sig, c + 1)
                _w(vector, s_out, 64 * (c - 1))
                for dj in range(4):
                    vector.tensor_mul(out=u_s[:, c % 2, dj, :], in0=conv_s[:, dj, :], in1=sig_s[:, dj, :])
                _w(vector, s_pt, c)
                for dj in range(4):
                    vector.tensor_copy(out=ub_s[:, dj, :], in_=u_s[:, c % 2, dj, :])
                for dj in range(4):
                    ii = vector.tensor_mul(out=g2_s[:, c % 2, dj, :], in0=xz_s[:, 4 + dj, :],
                                           in1=sig_s[:, 4 + dj, :])
                ii.then_inc(s_mul, 1)

        @block.tensor
        def _(tensor):
            tensor.wait_ge(s_prep, 64)
            tensor.wait_ge(s_io, 2)
            xst = x_sb.ap().rearrange("p b t f -> p (b t f)")
            for mi in range(8):
                for kj in range(8):
                    idx = mi * 8 + kj
                    _w(tensor, s_tpe, idx - 1)
                    tensor.transpose(out=ps_tp[:, idx % 2, 0:128],
                                     in_=xst[:, mi * 1024 + kj * 128: mi * 1024 + (kj + 1) * 128],
                                     identity=id_s[:]).then_inc(s_tp, 1)
            for dj in range(4):
                idx = 64 + dj
                _w(tensor, s_tpe, idx - 1)
                tensor.transpose(out=ps_tp[:, idx % 2, 0:96], in_=wst_s[:, dj * 128:(dj + 1) * 128],
                                 identity=id_s[0:96, 0:96]).then_inc(s_tp, 1)
            for c in range(NCH):
                base = 68 + 8 * c
                tensor.wait_ge(s_aff, c + 1)
                for j in range(8):
                    _w(tensor, s_tpe, base + j - 1)
                    for i in range(NT):
                        ii = tensor.transpose(out=ps_tp[:, j % 2, i * 128:(i + 1) * 128],
                                              in_=xnt_s[:, i, j * 128:(j + 1) * 128],
                                              identity=id_s[:])
                    ii.then_inc(s_tp, 1)
                tensor.wait_ge(s_tpe, base + 8)
                for mi in range(8):
                    gidx = 8 * c + mi
                    _w(tensor, s_xze, gidx - 1)
                    for kj in range(8):
                        ii = tensor.matmul(out=ps_xz[:, mi % 2, :],
                                           lhsT=winT[:, (kj * 8 + mi) * 128:(kj * 8 + mi + 1) * 128],
                                           rhs=xnf_s[:, c % 2, kj, :],
                                           start=(kj == 0), stop=(kj == 7))
                    ii.then_inc(s_xz, 1)
                tensor.wait_ge(s_mul, c + 1)
                _w(tensor, s_pte, c)
                for dj in range(4):
                    ii = tensor.matmul(out=ps_pt[:, :], lhsT=wxT[:, dj * 96:(dj + 1) * 96],
                                       rhs=ub_s[:, dj, :], start=(dj == 0), stop=(dj == 3))
                ii.then_inc(s_pt, 1)

        @block.scalar
        def _(scalar):
            for idx in range(64):
                mi, kj = idx // 8, idx % 8
                scalar.wait_ge(s_tp, idx + 1)
                scalar.copy(out=winT[:, (kj * 8 + mi) * 128:(kj * 8 + mi + 1) * 128],
                            in_=ps_tp[:, idx % 2, 0:128]).then_inc(s_tpe, 1)
            for dj in range(4):
                idx = 64 + dj
                scalar.wait_ge(s_tp, idx + 1)
                scalar.copy(out=wxT[:, dj * 96:(dj + 1) * 96],
                            in_=ps_tp[:, idx % 2, 0:96]).then_inc(s_tpe, 1)
            for c in range(NCH):
                base = 68 + 8 * c
                scalar.wait_ge(s_stats, c + 1)
                scalar.activation(out=rstd_s[:],
                                  in_=mv_s.ap().rearrange("p (t two) -> p t two", two=2)[:, :, 1:2],
                                  func=AF.Ln, bias=EPS)
                scalar.activation(out=rstd_s[:], in_=rstd_s[:], func=AF.Exp, scale=-0.5).then_inc(s_rsl, 1)
                _w(scalar, s_xz, 8 * (c - 1))
                _w(scalar, s_out, 64 * (c - 1))
                for j in range(8):
                    scalar.wait_ge(s_tp, base + j + 1)
                    scalar.copy(out=xnf_s[:, c % 2, j, :], in_=ps_tp[:, j % 2, :]).then_inc(s_tpe, 1)
                _w(scalar, s_mul, c)
                for mi in range(8):
                    gidx = 8 * c + mi
                    scalar.wait_ge(s_xz, gidx + 1)
                    scalar.copy(out=xz_s[:, mi, :], in_=ps_xz[:, mi % 2, :]).then_inc(s_xze, 1)
                scalar.wait_ge(s_conv, c + 1)
                for dj in range(4):
                    scalar.activation(out=sig_s[:, dj, :], in_=conv_s[:, dj, :], func=AF.Sigmoid)
                for dj in range(4):
                    ii = scalar.activation(out=sig_s[:, 4 + dj, :], in_=xz_s[:, 4 + dj, :],
                                           func=AF.Sigmoid)
                ii.then_inc(s_sig, 1)
                scalar.wait_ge(s_pt, c + 1)
                _w(scalar, s_out, 64 * (c - 1))
                scalar.copy(out=pt_s[:, c % 2, :], in_=ps_pt[:, :]).then_inc(s_pte, 1)
    return nc


def build_stage2():
    LS = 256
    NCH = L // LS

    nc = bass.Bass("TRN2", target_bir_lowering=False)
    u_in = nc.dram_tensor("u_in", [ES, L], F32, kind="ExternalInput").ap()
    g2_in = nc.dram_tensor("g2_in", [ES, L], F32, kind="ExternalInput").ap()
    pts = [nc.dram_tensor(f"pt{i}", [96, L], F32, kind="ExternalInput").ap() for i in range(4)]
    w_dt = nc.dram_tensor("w_dt", [ES, R], F32, kind="ExternalInput").ap()
    dt_bias = nc.dram_tensor("dt_bias", [128, 4], F32, kind="ExternalInput").ap()
    a_log = nc.dram_tensor("a_log", [ES, NST], F32, kind="ExternalInput").ap()
    d_skip = nc.dram_tensor("d_skip", [128, 4], F32, kind="ExternalInput").ap()
    y2_out = nc.dram_tensor("y2_out", [ES, L], BF16, kind="ExternalOutput").ap()

    u_in_r = u_in.rearrange("(dj p) l -> p dj l", p=128)
    g2_in_r = g2_in.rearrange("(dj p) l -> p dj l", p=128)
    y2_out_r = y2_out.rearrange("(dj p) l -> p dj l", p=128)

    with contextlib.ExitStack() as _es:
        io16 = _es.enter_context(nc.sbuf_tensor("io16", [16, 128], F32))
        io_s = _es.enter_context(nc.sbuf_tensor("io_s", [128, 128], F32))
        id_s = _es.enter_context(nc.sbuf_tensor("id_s", [128, 128], F32))
        en_s = _es.enter_context(nc.sbuf_tensor("en_s", [16, 16 * 128], F32))
        wdt_st = _es.enter_context(nc.sbuf_tensor("wdt_st", [128, 4, R], F32))
        wdtT = _es.enter_context(nc.sbuf_tensor("wdtT", [64, 4, 128], BF16))
        a_sb = _es.enter_context(nc.sbuf_tensor("a_sb", [128, 4, NST], F32))
        dtb_s = _es.enter_context(nc.sbuf_tensor("dtb_s", [128, 4], F32))
        dsk_s = _es.enter_context(nc.sbuf_tensor("dsk_s", [128, 4], F32))
        state = _es.enter_context(nc.sbuf_tensor("state", [128, 4, NST], F32))
        u_sb = _es.enter_context(nc.sbuf_tensor("u_sb", [128, 2, 4, LS], F32))
        g2_sb = _es.enter_context(nc.sbuf_tensor("g2_sb", [128, 2, 4, LS], F32))
        dtl_sb = _es.enter_context(nc.sbuf_tensor("dtl_sb", [64, 2, 4, LS], F32))
        bp_sb = _es.enter_context(nc.sbuf_tensor("bp_sb", [16, 2, 4, LS], F32))
        cp_sb = _es.enter_context(nc.sbuf_tensor("cp_sb", [16, 2, 4, LS], F32))
        dts_sb = _es.enter_context(nc.sbuf_tensor("dts_sb", [64, LS], F32))
        bps_sb = _es.enter_context(nc.sbuf_tensor("bps_sb", [16, 2, LS], F32))
        cps_sb = _es.enter_context(nc.sbuf_tensor("cps_sb", [16, 2, LS], F32))
        dtb16 = _es.enter_context(nc.sbuf_tensor("dtb16", [64, LS], BF16))
        z_sb = _es.enter_context(nc.sbuf_tensor("z_sb", [128, 4, LS], F32))
        nz_sb = _es.enter_context(nc.sbuf_tensor("nz_sb", [128, 4, LS], F32))
        sp_sb = _es.enter_context(nc.sbuf_tensor("sp_sb", [128, 4, LS], F32))
        del_sb = _es.enter_context(nc.sbuf_tensor("del_sb", [128, 4, LS], F32))
        du_sb = _es.enter_context(nc.sbuf_tensor("du_sb", [128, 4, LS], F32))
        da_sb = _es.enter_context(nc.sbuf_tensor("da_sb", [128, 2, LS], F32))
        dbu_sb = _es.enter_context(nc.sbuf_tensor("dbu_sb", [128, LS], F32))
        h_sb = _es.enter_context(nc.sbuf_tensor("h_sb", [128, LS], F32))
        hc_sb = _es.enter_context(nc.sbuf_tensor("hc_sb", [128, LS], F32))
        yac_sb = _es.enter_context(nc.sbuf_tensor("yac_sb", [128, 4, 2, LS], F32))
        ydu_sb = _es.enter_context(nc.sbuf_tensor("ydu_sb", [128, LS], F32))
        y2_sb = _es.enter_context(nc.sbuf_tensor("y2_sb", [128, 2, 4, LS], BF16))
        cst1_s = _es.enter_context(nc.sbuf_tensor("cst1_s", [128, 1], F32))
        ps_dp = _es.enter_context(nc.psum_tensor("ps_dp", [128, 2, 512], F32))
        ps_rep = _es.enter_context(nc.psum_tensor("ps_rep", [128, 2, 2, LS], F32))
        ps_w = _es.enter_context(nc.psum_tensor("ps_w", [64, 128], F32))
        s_prep = _es.enter_context(nc.semaphore("s_prep"))
        s_io = _es.enter_context(nc.semaphore("s_io"))
        s_ax = _es.enter_context(nc.semaphore("s_ax"))
        s_ld = _es.enter_context(nc.semaphore("s_ld"))
        s_in_used = _es.enter_context(nc.semaphore("s_in_used"))
        s_sum = _es.enter_context(nc.semaphore("s_sum"))
        s_dp = _es.enter_context(nc.semaphore("s_dp"))
        s_dpu = _es.enter_context(nc.semaphore("s_dpu"))
        s_nz = _es.enter_context(nc.semaphore("s_nz"))
        s_sp = _es.enter_context(nc.semaphore("s_sp"))
        s_del = _es.enter_context(nc.semaphore("s_del"))
        s_rep = _es.enter_context(nc.semaphore("s_rep"))
        s_repu = _es.enter_context(nc.semaphore("s_repu"))
        s_da = _es.enter_context(nc.semaphore("s_da"))
        s_dau = _es.enter_context(nc.semaphore("s_dau"))
        s_y2 = _es.enter_context(nc.semaphore("s_y2"))
        s_st = _es.enter_context(nc.semaphore("s_st"))
        block = _es.enter_context(nc.Block())
        nc.const_aps.register(F32, 1.0, cst1_s[:, 0:1])

        @block.sync
        def _(sync):
            sync.dma_start(out=wdt_st[:],
                           in_=w_dt.rearrange("(dj p) r -> p dj r", p=128)).then_inc(s_prep, 16)
            sync.dma_start(out=a_sb[:],
                           in_=a_log.rearrange("(dj p) n -> p dj n", p=128)).then_inc(s_prep, 16)
            sync.dma_start(out=dtb_s[:], in_=dt_bias).then_inc(s_prep, 16)
            sync.dma_start(out=dsk_s[:], in_=d_skip).then_inc(s_prep, 16)
            for c in range(NCH):
                _w(sync, s_in_used, c - 1)
                sl = slice(c * LS, (c + 1) * LS)
                sync.dma_start(out=u_sb[:, c % 2], in_=u_in_r[:, :, sl]).then_inc(s_ld, 16)
                sync.dma_start(out=g2_sb[:, c % 2], in_=g2_in_r[:, :, sl]).then_inc(s_ld, 16)
                for i in range(4):
                    sync.dma_start(out=dtl_sb[:, c % 2, i, :], in_=pts[i][0:64, sl]).then_inc(s_ld, 16)
                    sync.dma_start(out=bp_sb[:, c % 2, i, :], in_=pts[i][64:80, sl]).then_inc(s_ld, 16)
                    sync.dma_start(out=cp_sb[:, c % 2, i, :], in_=pts[i][80:96, sl]).then_inc(s_ld, 16)

        @block.gpsimd
        def _(gpsimd):
            gpsimd.iota(out=io16[:], pattern=[[0, 128]], base=0, channel_multiplier=1,
                        allow_small_or_imprecise_dtypes=True)
            gpsimd.iota(out=io_s[:], pattern=[[1, 128]], base=0, channel_multiplier=-1,
                        allow_small_or_imprecise_dtypes=True).then_inc(s_io, 1)
            for c in range(NCH):
                gpsimd.wait_ge(s_y2, c + 1)
                gpsimd.dma_start(out=y2_out_r[:, :, c * LS:(c + 1) * LS],
                                 in_=y2_sb[:, c % 2]).then_inc(s_st, 16)
            gpsimd.wait_ge(s_st, NCH * 16)

        @block.tensor
        def _(tensor):
            tensor.wait_ge(s_prep, 64)
            tensor.wait_ge(s_io, 2)
            for dj in range(4):
                _w(tensor, s_dpu, dj)
                tensor.transpose(out=ps_w[:, :], in_=wdt_st[:, dj, :], identity=id_s[:]
                                 ).then_inc(s_dp, 1)
            for c in range(NCH):
                tensor.wait_ge(s_sum, 2 * c + 1)
                for dj in range(4):
                    gidx = 4 * c + dj
                    _w(tensor, s_dpu, 4 + gidx - 1)
                    tensor.matmul(out=ps_dp[:, dj % 2, 0:LS], lhsT=wdtT[:, dj, :], rhs=dtb16[:],
                                  start=True, stop=True).then_inc(s_dp, 1)
                tensor.wait_ge(s_sum, 2 * c + 2)
                for n in range(NST):
                    ridx = NST * c + n
                    _w(tensor, s_repu, ridx - 1)
                    tensor.matmul(out=ps_rep[:, n % 2, 0, :], lhsT=en_s[:, n * 128:(n + 1) * 128],
                                  rhs=bps_sb[:, c % 2, :], start=True, stop=True)
                    tensor.matmul(out=ps_rep[:, n % 2, 1, :], lhsT=en_s[:, n * 128:(n + 1) * 128],
                                  rhs=cps_sb[:, c % 2, :], start=True, stop=True).then_inc(s_rep, 1)

        @block.scalar
        def _(scalar):
            scalar.wait_ge(s_prep, 64)
            scalar.activation(out=a_sb.ap().rearrange("p a b -> p (a b)"),
                              in_=a_sb.ap().rearrange("p a b -> p (a b)"),
                              func=AF.Exp).then_inc(s_ax, 1)
            for dj in range(4):
                scalar.wait_ge(s_dp, dj + 1)
                scalar.copy(out=wdtT[:, dj, :], in_=ps_w[:, :]).then_inc(s_dpu, 1)
            for c in range(NCH):
                scalar.wait_ge(s_nz, c + 1)
                for dj in range(4):
                    scalar.activation(out=sp_sb[:, dj, :], in_=nz_sb[:, dj, :], func=AF.Exp)
                for dj in range(4):
                    ii = scalar.activation(out=sp_sb[:, dj, :], in_=sp_sb[:, dj, :], func=AF.Ln,
                                           bias=1.0)
                ii.then_inc(s_sp, 1)
                scalar.wait_ge(s_del, c + 1)
                for n in range(NST):
                    for dj in range(4):
                        idx = (c * NST + n) * 4 + dj
                        _w(scalar, s_dau, idx - 1)
                        scalar.activation(out=da_sb[:, idx % 2, :], in_=del_sb[:, dj, :],
                                          func=AF.Exp, scale=a_sb[:, dj, n:n + 1]).then_inc(s_da, 1)

        @block.vector
        def _(vector):
            vector.wait_ge(s_io, 1)
            vector.tensor_scalar(out=id_s[:], in0=io_s[:], scalar1=0.0, op0=ALU.is_equal,
                                 scalar2=None)
            for n in range(NST):
                ii = vector.tensor_scalar(out=en_s[:, n * 128:(n + 1) * 128], in0=io16[:],
                                          scalar1=float(n), op0=ALU.is_equal, scalar2=None)
            ii.then_inc(s_io, 1)
            vector.memset(cst1_s[:], 1.0)
            vector.memset(state.ap().rearrange("p a b -> p (a b)"), 0.0)
            vector.wait_ge(s_ax, 1)
            vector.tensor_scalar(out=a_sb.ap().rearrange("p a b -> p (a b)"),
                                 in0=a_sb.ap().rearrange("p a b -> p (a b)"),
                                 scalar1=-1.0, op0=ALU.mult, scalar2=None)
            for c in range(NCH):
                vector.wait_ge(s_ld, 16 * 14 * (c + 1))
                _w(vector, s_dp, 4 + 4 * c)
                vector.tensor_add(out=dts_sb[:], in0=dtl_sb[:, c % 2, 0, :], in1=dtl_sb[:, c % 2, 1, :])
                vector.tensor_add(out=dts_sb[:], in0=dts_sb[:], in1=dtl_sb[:, c % 2, 2, :])
                vector.tensor_add(out=dts_sb[:], in0=dts_sb[:], in1=dtl_sb[:, c % 2, 3, :])
                vector.tensor_copy(out=dtb16[:], in_=dts_sb[:]).then_inc(s_sum, 1)
                _w(vector, s_rep, NST * (c - 1))
                vector.tensor_add(out=bps_sb[:, c % 2, :], in0=bp_sb[:, c % 2, 0, :], in1=bp_sb[:, c % 2, 1, :])
                vector.tensor_add(out=bps_sb[:, c % 2, :], in0=bps_sb[:, c % 2, :], in1=bp_sb[:, c % 2, 2, :])
                vector.tensor_add(out=bps_sb[:, c % 2, :], in0=bps_sb[:, c % 2, :], in1=bp_sb[:, c % 2, 3, :])
                vector.tensor_add(out=cps_sb[:, c % 2, :], in0=cp_sb[:, c % 2, 0, :], in1=cp_sb[:, c % 2, 1, :])
                vector.tensor_add(out=cps_sb[:, c % 2, :], in0=cps_sb[:, c % 2, :], in1=cp_sb[:, c % 2, 2, :])
                vector.tensor_add(out=cps_sb[:, c % 2, :], in0=cps_sb[:, c % 2, :], in1=cp_sb[:, c % 2, 3, :]
                                  ).then_inc(s_sum, 1)
                for dj in range(4):
                    vector.wait_ge(s_dp, 4 + 4 * c + dj + 1)
                    vector.tensor_scalar(out=z_sb[:, dj, :], in0=ps_dp[:, dj % 2, 0:LS],
                                         scalar1=dtb_s[:, dj:dj + 1], op0=ALU.add,
                                         scalar2=None).then_inc(s_dpu, 1)
                for dj in range(4):
                    ii = vector.scalar_tensor_tensor(out=nz_sb[:, dj, :], in0=z_sb[:, dj, :],
                                                     scalar=-1.0, in1=z_sb[:, dj, :],
                                                     op0=ALU.mult, op1=ALU.min)
                ii.then_inc(s_nz, 1)
                vector.wait_ge(s_sp, c + 1)
                for dj in range(4):
                    ii = vector.scalar_tensor_tensor(out=del_sb[:, dj, :], in0=z_sb[:, dj, :],
                                                     scalar=0.0, in1=sp_sb[:, dj, :],
                                                     op0=ALU.max, op1=ALU.add)
                ii.then_inc(s_del, 1)
                for dj in range(4):
                    vector.tensor_mul(out=du_sb[:, dj, :], in0=del_sb[:, dj, :], in1=u_sb[:, c % 2, dj, :])
                for n in range(NST):
                    vector.wait_ge(s_rep, NST * c + n + 1)
                    for dj in range(4):
                        idx = (c * NST + n) * 4 + dj
                        vector.wait_ge(s_da, idx + 1)
                        vector.tensor_mul(out=dbu_sb[:], in0=du_sb[:, dj, :], in1=ps_rep[:, n % 2, 0, :])
                        vector.tensor_tensor_scan(out=h_sb[:], data0=da_sb[:, idx % 2, :], data1=dbu_sb[:],
                                                  initial=state[:, dj, n:n + 1],
                                                  op0=ALU.mult, op1=ALU.add).then_inc(s_dau, 1)
                        vector.tensor_copy(out=state[:, dj, n:n + 1], in_=h_sb[:, LS - 1:LS])
                        ii = vector.tensor_mul(out=hc_sb[:], in0=h_sb[:], in1=ps_rep[:, n % 2, 1, :])
                        if n == 0:
                            vector.tensor_copy(out=yac_sb[:, dj, 0, :], in_=hc_sb[:])
                        else:
                            vector.tensor_add(out=yac_sb[:, dj, n % 2, :],
                                              in0=yac_sb[:, dj, (n - 1) % 2, :], in1=hc_sb[:])
                        if dj == 3:
                            ii.then_inc(s_repu, 1)
                _w(vector, s_st, (c - 1) * 16)
                for dj in range(4):
                    vector.scalar_tensor_tensor(out=ydu_sb[:], in0=u_sb[:, c % 2, dj, :],
                                                scalar=dsk_s[:, dj:dj + 1],
                                                in1=yac_sb[:, dj, (NST - 1) % 2, :],
                                                op0=ALU.mult, op1=ALU.add)
                    ii = vector.tensor_mul(out=y2_sb[:, c % 2, dj, :], in0=ydu_sb[:],
                                           in1=g2_sb[:, c % 2, dj, :])
                ii.then_inc(s_y2, 1)
                vector.tensor_copy(out=dbu_sb[:, 0:1], in_=dbu_sb[:, 0:1]).then_inc(s_in_used, 1)
    return nc


def build_stage3():
    LC = 512
    NCH = LQ // LC

    nc = bass.Bass("TRN2", target_bir_lowering=False)
    y2 = nc.dram_tensor("y2", [E, LQ], BF16, kind="ExternalInput").ap()
    xn = nc.dram_tensor("xn", [D, LQ], BF16, kind="ExternalInput").ap()
    xq = nc.dram_tensor("xq", [LQ, D], F32, kind="ExternalInput").ap()
    w_out = nc.dram_tensor("w_out", [D, E], F32, kind="ExternalInput").ap()
    cb = nc.dram_tensor("cb", [D, D], F32, kind="ExternalInput").ap()
    scb = nc.dram_tensor("scb", [128, 8], F32, kind="ExternalInput").ap()
    bias = nc.dram_tensor("bias", [128, 8], F32, kind="ExternalInput").ap()
    out = nc.dram_tensor("out", [LQ, D], F32, kind="ExternalOutput").ap()

    y2_r = y2.rearrange("(kj p) l -> p kj l", p=128)
    xn_r = xn.rearrange("(kj p) l -> p kj l", p=128)

    with contextlib.ExitStack() as _es:
        io_s = _es.enter_context(nc.sbuf_tensor("io_s", [128, 128], F32))
        id_s = _es.enter_context(nc.sbuf_tensor("id_s", [128, 128], F32))
        wst = _es.enter_context(nc.sbuf_tensor("wst", [128, 2, 2048], F32))
        woT = _es.enter_context(nc.sbuf_tensor("woT", [128, 16 * 8 * 128], BF16))
        cbT = _es.enter_context(nc.sbuf_tensor("cbT", [128, 8 * 8 * 128], BF16))
        scb_s = _es.enter_context(nc.sbuf_tensor("scb_s", [128, 8], F32))
        bias_s = _es.enter_context(nc.sbuf_tensor("bias_s", [128, 8], F32))
        y2c = _es.enter_context(nc.sbuf_tensor("y2c", [128, 2, 16, LC], BF16))
        xnc = _es.enter_context(nc.sbuf_tensor("xnc", [128, 2, 8, LC], BF16))
        xq_s = _es.enter_context(nc.sbuf_tensor("xq_s", [128, 2, 4, 1024], F32))
        ffn_s = _es.enter_context(nc.sbuf_tensor("ffn_s", [128, 2, LC], F32))
        af_s = _es.enter_context(nc.sbuf_tensor("af_s", [128, 8, LC], F32))
        out_s = _es.enter_context(nc.sbuf_tensor("out_s", [128, 4, 1024], F32))
        ps_a = _es.enter_context(nc.psum_tensor("ps_a", [128, 2, 512], F32))
        ps_f = _es.enter_context(nc.psum_tensor("ps_f", [128, 2, 512], F32))
        ps_t = _es.enter_context(nc.psum_tensor("ps_t", [128, 4, 512], F32))
        s_w = _es.enter_context(nc.semaphore("s_w"))
        s_io = _es.enter_context(nc.semaphore("s_io"))
        s_ld = _es.enter_context(nc.semaphore("s_ld"))
        s_xq = _es.enter_context(nc.semaphore("s_xq"))
        s_tp = _es.enter_context(nc.semaphore("s_tp"))
        s_ev = _es.enter_context(nc.semaphore("s_ev"))
        s_mm = _es.enter_context(nc.semaphore("s_mm"))
        s_ff = _es.enter_context(nc.semaphore("s_ff"))
        s_sum = _es.enter_context(nc.semaphore("s_sum"))
        s_tt = _es.enter_context(nc.semaphore("s_tt"))
        s_add = _es.enter_context(nc.semaphore("s_add"))
        s_st = _es.enter_context(nc.semaphore("s_st"))
        block = _es.enter_context(nc.Block())
        @block.sync
        def _(sync):
            sync.dma_start(out=scb_s[:], in_=scb).then_inc(s_ld, 16)
            sync.dma_start(out=bias_s[:], in_=bias).then_inc(s_ld, 16)
            for q in range(4):
                _w(sync, s_tp, 32 * q)
                sync.dma_start(out=wst[:],
                               in_=w_out[256 * q:256 * (q + 1), :].rearrange("(mi p) k -> p mi k", p=128)
                               ).then_inc(s_w, 16)
            for q in range(4):
                sync.wait_ge(s_tp, 128 + 16 * q)
                sync.dma_start(out=wst[:, :, 0:1024],
                               in_=cb[256 * q:256 * (q + 1), :].rearrange("(mi p) k -> p mi k", p=128)
                               ).then_inc(s_w, 16)
            for c in range(NCH):
                _w(sync, s_mm, 8 * (c - 1))
                sl = slice(c * LC, (c + 1) * LC)
                sync.dma_start(out=y2c[:, c % 2], in_=y2_r[:, :, sl]).then_inc(s_ld, 16)
                sync.dma_start(out=xnc[:, c % 2], in_=xn_r[:, :, sl]).then_inc(s_ld, 16)
                _w(sync, s_add, 4 * (c - 1))
                sync.dma_start(out=xq_s[:, c % 2],
                               in_=xq[c * LC:(c + 1) * LC, :].rearrange("(i p) f -> p i f", p=128)
                               ).then_inc(s_xq, 16)

        @block.gpsimd
        def _(gpsimd):
            gpsimd.iota(out=io_s[:], pattern=[[1, 128]], base=0, channel_multiplier=-1,
                        allow_small_or_imprecise_dtypes=True).then_inc(s_io, 1)
            for c in range(NCH):
                gpsimd.wait_ge(s_add, 4 * (c + 1))
                gpsimd.dma_start(out=out[c * LC:(c + 1) * LC, :].rearrange("(i p) f -> p i f", p=128),
                                 in_=out_s[:, :, :]).then_inc(s_st, 16)
            gpsimd.wait_ge(s_st, NCH * 16)

        @block.tensor
        def _(tensor):
            tensor.wait_ge(s_io, 2)
            for q in range(4):
                tensor.wait_ge(s_w, 16 * (q + 1))
                for m2 in range(2):
                    mi = 2 * q + m2
                    for kj in range(16):
                        idx = (mi * 16 + kj)
                        _w(tensor, s_ev, idx - 1)
                        tensor.transpose(out=ps_t[:, idx % 2, 0:128],
                                         in_=wst[:, m2, kj * 128:(kj + 1) * 128],
                                         identity=id_s[:]).then_inc(s_tp, 1)
            for q in range(4):
                tensor.wait_ge(s_w, 64 + 16 * (q + 1))
                for m2 in range(2):
                    mi = 2 * q + m2
                    for kj in range(8):
                        idx = 128 + mi * 8 + kj
                        _w(tensor, s_ev, idx - 1)
                        tensor.transpose(out=ps_t[:, idx % 2, 0:128],
                                         in_=wst[:, m2, kj * 128:(kj + 1) * 128],
                                         identity=id_s[:]).then_inc(s_tp, 1)
            tensor.wait_ge(s_ev, 192)
            for c in range(NCH):
                tensor.wait_ge(s_ld, 32 + 32 * (c + 1))
                for mi in range(8):
                    gidx = 8 * c + mi
                    _w(tensor, s_sum, gidx - 1)
                    _w(tensor, s_ff, gidx - 1)
                    for kj in range(16):
                        tensor.matmul(out=ps_a[:, mi % 2, :],
                                      lhsT=woT[:, (kj * 8 + mi) * 128:(kj * 8 + mi + 1) * 128],
                                      rhs=y2c[:, c % 2, kj, :],
                                      start=(kj == 0), stop=(kj == 15))
                    for kj in range(8):
                        ii = tensor.matmul(out=ps_f[:, mi % 2, :],
                                           lhsT=cbT[:, (kj * 8 + mi) * 128:(kj * 8 + mi + 1) * 128],
                                           rhs=xnc[:, c % 2, kj, :],
                                           start=(kj == 0), stop=(kj == 7))
                    ii.then_inc(s_mm, 1)
                tensor.wait_ge(s_sum, 8 * (c + 1))
                for i in range(4):
                    tidx = 4 * c + i
                    _w(tensor, s_add, tidx - 1)
                    for mi in range(8):
                        bank = (2 * i + mi // 4) % 4
                        ii = tensor.transpose(out=ps_t[:, bank, (mi % 4) * 128:(mi % 4 + 1) * 128],
                                              in_=af_s[:, mi, i * 128:(i + 1) * 128],
                                              identity=id_s[:])
                    ii.then_inc(s_tt, 1)

        @block.scalar
        def _(scalar):
            for idx in range(128):
                mi, kj = idx // 16, idx % 16
                scalar.wait_ge(s_tp, idx + 1)
                scalar.copy(out=woT[:, (kj * 8 + mi) * 128:(kj * 8 + mi + 1) * 128],
                            in_=ps_t[:, idx % 2, 0:128]).then_inc(s_ev, 1)
            for idx in range(128, 192):
                mi, kj = (idx - 128) // 8, (idx - 128) % 8
                scalar.wait_ge(s_tp, idx + 1)
                scalar.copy(out=cbT[:, (kj * 8 + mi) * 128:(kj * 8 + mi + 1) * 128],
                            in_=ps_t[:, idx % 2, 0:128]).then_inc(s_ev, 1)
            scalar.wait_ge(s_ld, 32)
            scalar.mul(out=scb_s[:], in_=scb_s[:], mul=1.0 / 127.0)
            for c in range(NCH):
                for mi in range(8):
                    gidx = 8 * c + mi
                    scalar.wait_ge(s_mm, gidx + 1)
                    _w(scalar, s_sum, gidx - 1)
                    scalar.activation(out=ffn_s[:, mi % 2, :], in_=ps_f[:, mi % 2, :],
                                      func=AF.Identity, scale=scb_s[:, mi:mi + 1]).then_inc(s_ff, 1)

        @block.vector
        def _(vector):
            vector.wait_ge(s_io, 1)
            vector.tensor_scalar(out=id_s[:], in0=io_s[:], scalar1=0.0, op0=ALU.is_equal,
                                 scalar2=None).then_inc(s_io, 1)
            for c in range(NCH):
                for mi in range(8):
                    gidx = 8 * c + mi
                    vector.wait_ge(s_ff, gidx + 1)
                    vector.scalar_tensor_tensor(out=af_s[:, mi, :], in0=ps_a[:, mi % 2, :],
                                                scalar=bias_s[:, mi:mi + 1], in1=ffn_s[:, mi % 2, :],
                                                op0=ALU.add, op1=ALU.add).then_inc(s_sum, 1)
                _w(vector, s_st, 16 * c)
                vector.wait_ge(s_xq, 16 * (c + 1))
                for i in range(4):
                    tidx = 4 * c + i
                    vector.wait_ge(s_tt, tidx + 1)
                    vector.tensor_add(out=out_s[:, i, 0:512],
                                      in0=ps_t[:, (2 * i) % 4, :], in1=xq_s[:, c % 2, i, 0:512])
                    vector.tensor_add(out=out_s[:, i, 512:1024],
                                      in0=ps_t[:, (2 * i + 1) % 4, :], in1=xq_s[:, c % 2, i, 512:1024]
                                      ).then_inc(s_add, 1)
    return nc


def kernel(L0_attn_hidden_states, L0_attn_W_in, L0_attn_conv_w, L0_attn_conv_b,
           L0_attn_W_x, L0_attn_W_dt, L0_attn_dt_bias, L0_attn_A_log,
           L0_attn_D_skip, L0_attn_W_out, L0_ffn_CB, L0_ffn_SCB, L0_ffn_bias):
    x = np.ascontiguousarray(np.asarray(L0_attn_hidden_states, dtype=np.float32))
    W_in = np.asarray(L0_attn_W_in, dtype=np.float32)
    conv_w = np.asarray(L0_attn_conv_w, dtype=np.float32)
    conv_b = np.asarray(L0_attn_conv_b, dtype=np.float32)
    W_x = np.asarray(L0_attn_W_x, dtype=np.float32)
    W_dt = np.asarray(L0_attn_W_dt, dtype=np.float32)
    dt_bias = np.asarray(L0_attn_dt_bias, dtype=np.float32)
    A_log = np.asarray(L0_attn_A_log, dtype=np.float32)
    D_skip = np.asarray(L0_attn_D_skip, dtype=np.float32)
    W_out = np.asarray(L0_attn_W_out, dtype=np.float32)
    CB = np.asarray(L0_ffn_CB, dtype=np.float32)
    SCB = np.asarray(L0_ffn_SCB, dtype=np.float32)
    fbias = np.asarray(L0_ffn_bias, dtype=np.float32)
    C = np.ascontiguousarray

    nc1 = build_stage1()
    maps1 = []
    for cid in range(NC):
        b, s = cid // 4, cid % 4
        sl = slice(ES * s, ES * (s + 1))
        maps1.append(dict(
            x=C(x[b]),
            w_in=C(np.concatenate([W_in[ES * s:ES * (s + 1)], W_in[E + ES * s:E + ES * (s + 1)]], axis=0)),
            conv_w=C(conv_w[sl]), conv_b=C(conv_b[sl].reshape(4, 128).T), w_x=C(W_x[:, sl]),
        ))
    r1 = run_bass_kernel_spmd(nc1, maps1, core_ids=list(range(NC))).results

    nc2 = build_stage2()
    maps2 = []
    for cid in range(NC):
        b, s = cid // 4, cid % 4
        sl = slice(ES * s, ES * (s + 1))
        m = dict(u_in=r1[cid]["u_out"], g2_in=r1[cid]["g2_out"],
                 w_dt=C(W_dt[sl]), dt_bias=C(dt_bias[sl].reshape(4, 128).T), a_log=C(A_log[sl]),
                 d_skip=C(D_skip[sl].reshape(4, 128).T))
        for i in range(4):
            m[f"pt{i}"] = r1[4 * b + i]["pt_out"]
        maps2.append(m)
    r2 = run_bass_kernel_spmd(nc2, maps2, core_ids=list(range(NC))).results

    nc3 = build_stage3()
    maps3 = []
    y2full = [np.concatenate([r2[4 * b + i]["y2_out"] for i in range(4)], axis=0) for b in range(B)]
    for cid in range(NC):
        b, q = cid // 4, cid % 4
        qsl = slice(LQ * q, LQ * (q + 1))
        maps3.append(dict(
            y2=C(y2full[b][:, qsl]),
            xn=C(r1[4 * b]["xn_out"][:, qsl]),
            xq=C(x[b, qsl]),
            w_out=W_out, cb=CB, scb=C(SCB.reshape(8, 128).T), bias=C(fbias.reshape(8, 128).T),
        ))
    r3 = run_bass_kernel_spmd(nc3, maps3, core_ids=list(range(NC))).results

    out = np.empty((B, L, D), dtype=np.float32)
    for cid in range(NC):
        b, q = cid // 4, cid % 4
        out[b, LQ * q:LQ * (q + 1)] = r3[cid]["out"]
    return out


# version 07
# speedup vs baseline: 1.1570x; 1.1570x over previous
import contextlib
import numpy as np
import ml_dtypes
import concourse.bass as bass
import concourse.mybir as mybir
from concourse.bass_utils import run_bass_kernel_spmd

F32 = mybir.dt.float32
BF16 = mybir.dt.bfloat16
AF = mybir.ActivationFunctionType
ALU = mybir.AluOpType
AX = mybir.AxisListType

B, L, D, E, NST, R = 2, 8192, 1024, 2048, 16, 64
ES = 512
NC = 8
EPS = 1e-5

LC1 = 256
N1 = L // LC1
NT1 = LC1 // 128
LS = 256
N2 = L // LS
LC3 = 256
N3 = L // LC3
FS = 256


def _w(eng, sem, v):
    if v > 0:
        eng.wait_ge(sem, v)


def build_fused():
    nc = bass.Bass("TRN2", target_bir_lowering=False, detect_race_conditions=False)
    x = nc.dram_tensor("x", [L, D], F32, kind="ExternalInput").ap()
    w_in = nc.dram_tensor("w_in", [1024, D], F32, kind="ExternalInput").ap()
    conv_w = nc.dram_tensor("conv_w", [ES, 4], F32, kind="ExternalInput").ap()
    conv_b = nc.dram_tensor("conv_b", [128, 4], F32, kind="ExternalInput").ap()
    w_x = nc.dram_tensor("w_x", [96, ES], F32, kind="ExternalInput").ap()
    w_dt = nc.dram_tensor("w_dt", [ES, R], F32, kind="ExternalInput").ap()
    dt_bias = nc.dram_tensor("dt_bias", [128, 4], F32, kind="ExternalInput").ap()
    a_log = nc.dram_tensor("a_log", [ES, NST], F32, kind="ExternalInput").ap()
    d_skip = nc.dram_tensor("d_skip", [128, 4], F32, kind="ExternalInput").ap()
    w_out_s = nc.dram_tensor("w_out_s", [FS, E], F32, kind="ExternalInput").ap()
    cb_sl = nc.dram_tensor("cb_sl", [FS, D], F32, kind="ExternalInput").ap()
    scb_v = nc.dram_tensor("scb_v", [128, 2], F32, kind="ExternalInput").ap()
    bias_v = nc.dram_tensor("bias_v", [128, 2], F32, kind="ExternalInput").ap()
    xcol = nc.dram_tensor("xcol", [L, FS], F32, kind="ExternalInput").ap()
    out = nc.dram_tensor("out", [L, FS], F32, kind="ExternalOutput").ap()

    u_sc = nc.dram_tensor("u_sc", [ES, L], F32).ap()
    g2_sc = nc.dram_tensor("g2_sc", [ES, L], F32).ap()
    xn_sc = nc.dram_tensor("xn_sc", [D, L], BF16).ap()
    pt_in = nc.dram_tensor("pt_in", [96, L], F32).ap()
    pt_sum = nc.dram_tensor("pt_sum", [96, L], F32).ap()
    y2_sla = nc.dram_tensor("y2_sla", [ES, L // 2], BF16).ap()
    y2_slb = nc.dram_tensor("y2_slb", [ES, L // 2], BF16).ap()
    y2_fa = nc.dram_tensor("y2_fa", [4 * ES, L // 2], BF16).ap()
    y2_fb = nc.dram_tensor("y2_fb", [4 * ES, L // 2], BF16).ap()

    u_sc_r = u_sc.rearrange("(dj p) l -> p dj l", p=128)
    g2_sc_r = g2_sc.rearrange("(dj p) l -> p dj l", p=128)
    xn_sc_r = xn_sc.rearrange("(j p) l -> p j l", p=128)
    y2_sla_r = y2_sla.rearrange("(dj p) l -> p dj l", p=128)
    y2_slb_r = y2_slb.rearrange("(dj p) l -> p dj l", p=128)
    y2_fa_r = y2_fa.rearrange("(kj p) l -> p kj l", p=128)
    y2_fb_r = y2_fb.rearrange("(kj p) l -> p kj l", p=128)

    RG = [[0, 1, 2, 3], [4, 5, 6, 7]]

    with contextlib.ExitStack() as _es:
        sb = lambda name, shape, dt=F32: _es.enter_context(nc.sbuf_tensor(name, shape, dt))
        ps = lambda name, shape: _es.enter_context(nc.psum_tensor(name, shape, F32))
        sem = lambda name: _es.enter_context(nc.semaphore(name))

        io_s = sb("io_s", [128, 128])
        id_s = sb("id_s", [128, 128])
        io16 = sb("io16", [16, 128])
        cst_s = sb("cst_s", [128, 2])

        winT = sb("winT", [128, 8 * 8 * 128], BF16)
        wxT = sb("wxT", [128, 4 * 96], BF16)
        cw_s = sb("cw_s", [128, 16])
        cb_s = sb("cb_s", [128, 4])
        x_sb = sb("x_sb", [128, 2, NT1, 1024])
        bst_s = sb("bst_s", [128, 2, 12])
        mva_s = sb("mva_s", [128, 64, 2])
        rst_s = sb("rst_s", [128, 64])
        xnt_s = sb("xnt_s", [128, NT1, 1024])
        xnf_s = sb("xnf_s", [128, 2, 8, LC1], BF16)
        xz_s = sb("xz_s", [128, 8, LC1])
        pad_s = sb("pad_s", [128, 4, LC1 + 3])
        cva_s = sb("cva_s", [128, LC1])
        cvb_s = sb("cvb_s", [128, LC1])
        conv_c = sb("conv_c", [128, 4, LC1])
        sig_s = sb("sig_s", [128, 8, LC1])
        u1_s = sb("u1_s", [128, 2, 4, LC1])
        ub_s = sb("ub_s", [128, 4, LC1], BF16)
        g1_s = sb("g1_s", [128, 2, 4, LC1])
        pt_s = sb("pt_s", [96, 2, LC1])
        wst2_s = sb("wst2_s", [96, 512])

        wdt_st = sb("wdt_st", [128, 4, R])
        wdtT = sb("wdtT", [64, 4, 128], BF16)
        a_sb = sb("a_sb", [128, 4, NST])
        dtb_s = sb("dtb_s", [128, 4])
        dsk_s = sb("dsk_s", [128, 4])
        state = sb("state", [128, 4, NST])
        u_sb = sb("u_sb", [128, 2, 4, LS])
        g2_sb = sb("g2_sb", [128, 2, 4, LS])
        dtl_sb = sb("dtl_sb", [64, 2, LS])
        bps_sb = sb("bps_sb", [16, 2, LS])
        cps_sb = sb("cps_sb", [16, 2, LS])
        dtb16 = sb("dtb16", [64, 2, LS], BF16)
        z_sb = sb("z_sb", [128, 4, LS])
        nz_sb = sb("nz_sb", [128, 4, LS])
        sp_sb = sb("sp_sb", [128, 4, LS])
        del_sb = sb("del_sb", [128, 4, LS])
        du_sb = sb("du_sb", [128, 4, LS])
        da_sb = sb("da_sb", [128, 2, LS])
        dbu_sb = sb("dbu_sb", [128, LS], BF16)
        h_sb = sb("h_sb", [128, LS], BF16)
        hc_sb = sb("hc_sb", [128, 2, 4, LS], BF16)
        du_bf = sb("du_bf", [128, 4, LS], BF16)
        rep_bf = sb("rep_bf", [128, 2, 2, LS], BF16)
        en_bf = sb("en_bf", [16, 16 * 128], BF16)
        bps_bf = sb("bps_bf", [16, LS], BF16)
        cps_bf = sb("cps_bf", [16, LS], BF16)
        id_bf = sb("id_bf", [128, 128], BF16)
        ydu_sb = sb("ydu_sb", [128, LS])
        y2_sb = sb("y2_sb", [128, 2, 4, LS], BF16)

        woT = winT.ap()[:, 0:16 * 2 * 128]
        cbT = winT.ap()[:, 16 * 2 * 128:16 * 2 * 128 + 8 * 2 * 128]
        scb_s = sb("scb_s", [128, 2])
        bias_s = sb("bias_s", [128, 2])
        y2c_s = sb("y2c_s", [128, 2, 16, LC3], BF16)
        xnc_s = sb("xnc_s", [128, 2, 8, LC3], BF16)
        xcol_s = sb("xcol_s", [128, 2, 2, FS])
        ffn_s = sb("ffn_s", [128, 2, LC3])
        af_s = sb("af_s", [128, 2, LC3])
        out_s = sb("out_s", [128, 2, FS])

        ps_tp = ps("ps_tp", [128, 2, 512])
        ps_xz = ps("ps_xz", [128, 2, 512])
        ps_dp = ps("ps_dp", [128, 2, 512])
        ps_pt = ps_dp.ap()[0:96, 0, :]
        ps_rep = ps("ps_rep", [128, 2, 2, LS])
        ps_w2 = ps_xz.ap()[0:64, 0, 0:128]

        s_io = sem("s_io")
        s_wdma = sem("s_wdma")
        s_px = [sem("s_px0"), sem("s_px1")]
        s_pst = sem("s_pst")
        s_rsl = sem("s_rsl")
        s1_x = [sem("s1_x0"), sem("s1_x1")]
        s1_aff = sem("s1_aff")
        s1_tp = sem("s1_tp")
        s1_tpe = sem("s1_tpe")
        s1_xz = sem("s1_xz")
        s1_xze = sem("s1_xze")
        s1_conv = sem("s1_conv")
        s1_sig = sem("s1_sig")
        s1_mul = sem("s1_mul")
        s1_pt = sem("s1_pt")
        s1_pte = sem("s1_pte")
        s1_out = [sem("s1_out0"), sem("s1_out1")]
        s_cc = sem("s_cc")
        s2_wdma = sem("s2_wdma")
        s2_ax = sem("s2_ax")
        s2_ld = [sem("s2_ld0"), sem("s2_ld1")]
        s2_used = sem("s2_used")
        s2_sum = sem("s2_sum")
        s2_dp = sem("s2_dp")
        s2_dpu = sem("s2_dpu")
        s2_nz = sem("s2_nz")
        s2_sp = sem("s2_sp")
        s2_del = sem("s2_del")
        s2_rep = sem("s2_rep")
        s2_repe = sem("s2_repe")
        s2_rbu = sem("s2_rbu")
        s2_hc = sem("s2_hc")
        s2_hcu = sem("s2_hcu")
        s2_yd = sem("s2_yd")
        s2_ydu = sem("s2_ydu")
        s2_da = sem("s2_da")
        s2_dau = sem("s2_dau")
        s2_y2 = sem("s2_y2")
        s2_st = [sem("s2_st0"), sem("s2_st1")]
        s3_w = sem("s3_w")
        s3_ld = [sem("s3_ld0"), sem("s3_ld1")]
        s3_xc = [sem("s3_xc0"), sem("s3_xc1")]
        s3_tp = sem("s3_tp")
        s3_ev = sem("s3_ev")
        s3_mm = sem("s3_mm")
        s3_ff = sem("s3_ff")
        s3_sum = sem("s3_sum")
        s3_tt = sem("s3_tt")
        s3_add = sem("s3_add")
        s3_st = sem("s3_st")
        block = _es.enter_context(nc.Block())

        nc.const_aps.register(F32, EPS, cst_s[:, 0:1])
        nc.const_aps.register(F32, 1.0, cst_s[:, 1:2])

        @block.sync
        def _(sync):
            sync.dma_start(out=x_sb.ap().rearrange("p b t f -> p (b t) f"),
                           in_=w_in[0:512, :].rearrange("(mi p) k -> p mi k", p=128)).then_inc(s_wdma, 16)
            sync.dma_start(out=wst2_s[:], in_=w_x).then_inc(s_wdma, 16)
            sync.dma_start(out=cw_s.ap().rearrange("p (dj k) -> p dj k", k=4),
                           in_=conv_w.rearrange("(dj p) k -> p dj k", p=128)).then_inc(s_wdma, 16)
            sync.dma_start(out=cb_s[:], in_=conv_b).then_inc(s_wdma, 16)
            sync.wait_ge(s1_tp, 32)
            sync.dma_start(out=x_sb.ap().rearrange("p b t f -> p (b t) f"),
                           in_=w_in[512:1024, :].rearrange("(mi p) k -> p mi k", p=128)).then_inc(s_wdma, 16)
            sync.dma_start(out=wdt_st[:], in_=w_dt.rearrange("(dj p) r -> p dj r", p=128)).then_inc(s2_wdma, 16)
            sync.dma_start(out=a_sb[:], in_=a_log.rearrange("(dj p) n -> p dj n", p=128)).then_inc(s2_wdma, 16)
            sync.dma_start(out=dtb_s[:], in_=dt_bias).then_inc(s2_wdma, 16)
            sync.dma_start(out=dsk_s[:], in_=d_skip).then_inc(s2_wdma, 16)
            sync.dma_start(out=scb_s[:], in_=scb_v).then_inc(s2_wdma, 16)
            sync.dma_start(out=bias_s[:], in_=bias_v).then_inc(s2_wdma, 16)
            for c in range(N1):
                if c < 2:
                    sync.wait_ge(s1_tp, 68)
                else:
                    _w(sync, s_pst, c - 1)
                sync.dma_start(out=x_sb[:, c % 2],
                               in_=x[c * LC1:(c + 1) * LC1, :].rearrange("(i p) f -> p i f", p=128)
                               ).then_inc(s_px[c % 2], 16)
            for c in range(N1):
                if c < 2:
                    sync.wait_ge(s_pst, N1)
                else:
                    sync.wait_ge(s1_aff, c - 1)
                sync.dma_start(out=x_sb[:, c % 2],
                               in_=x[c * LC1:(c + 1) * LC1, :].rearrange("(i p) f -> p i f", p=128)
                               ).then_inc(s1_x[c % 2], 16)
            sync.wait_ge(s_cc, 1)
            for c in range(N2):
                _w(sync, s2_used, c - 1)
                sl = slice(c * LS, (c + 1) * LS)
                sync.dma_start(out=u_sb[:, c % 2], in_=u_sc_r[:, :, sl]).then_inc(s2_ld[c % 2], 16)
                sync.dma_start(out=g2_sb[:, c % 2], in_=g2_sc_r[:, :, sl]).then_inc(s2_ld[c % 2], 16)
                sync.dma_start(out=dtl_sb[:, c % 2, :], in_=pt_sum[0:64, sl]).then_inc(s2_ld[c % 2], 16)
                sync.dma_start(out=bps_sb[:, c % 2, :], in_=pt_sum[64:80, sl]).then_inc(s2_ld[c % 2], 16)
                sync.dma_start(out=cps_sb[:, c % 2, :], in_=pt_sum[80:96, sl]).then_inc(s2_ld[c % 2], 16)
            sync.wait_ge(s1_aff, N1)
            xstf = x_sb.ap().rearrange("p b t f -> p (b t f)")
            for m2 in range(2):
                sync.dma_start(out=xstf[:, m2 * 2048:(m2 + 1) * 2048],
                               in_=w_out_s[m2 * 128:(m2 + 1) * 128, :]).then_inc(s3_w, 16)
            sync.wait_ge(s3_tp, 32)
            for m2 in range(2):
                sync.dma_start(out=xstf[:, m2 * 1024:(m2 + 1) * 1024],
                               in_=cb_sl[m2 * 128:(m2 + 1) * 128, :]).then_inc(s3_w, 16)
            sync.wait_ge(s_cc, 2)
            for c in range(N3):
                if c == N3 // 2:
                    sync.wait_ge(s_cc, 3)
                _w(sync, s3_mm, 2 * (c - 1))
                if c < N3 // 2:
                    sync.dma_start(out=y2c_s[:, c % 2],
                                   in_=y2_fa_r[:, :, c * LC3:(c + 1) * LC3]).then_inc(s3_ld[c % 2], 16)
                else:
                    cc = c - N3 // 2
                    sync.dma_start(out=y2c_s[:, c % 2],
                                   in_=y2_fb_r[:, :, cc * LC3:(cc + 1) * LC3]).then_inc(s3_ld[c % 2], 16)
                sl = slice(c * LC3, (c + 1) * LC3)
                sync.dma_start(out=xnc_s[:, c % 2], in_=xn_sc_r[:, :, sl]).then_inc(s3_ld[c % 2], 16)
                _w(sync, s3_add, 2 * (c - 1))
                sync.dma_start(out=xcol_s[:, c % 2],
                               in_=xcol[sl, :].rearrange("(i p) f -> p i f", p=128)).then_inc(s3_xc[c % 2], 16)

        @block.gpsimd
        def _(gpsimd):
            gpsimd.iota(out=io_s[:], pattern=[[1, 128]], base=0, channel_multiplier=-1,
                        allow_small_or_imprecise_dtypes=True)
            gpsimd.iota(out=io16[:], pattern=[[0, 128]], base=0, channel_multiplier=1,
                        allow_small_or_imprecise_dtypes=True).then_inc(s_io, 1)
            for c in range(N1):
                gpsimd.wait_ge(s1_mul, c + 1)
                gpsimd.wait_ge(s1_pte, c + 1)
                sl = slice(c * LC1, (c + 1) * LC1)
                gpsimd.dma_start(out=u_sc_r[:, :, sl], in_=u1_s[:, c % 2]).then_inc(s1_out[c % 2], 16)
                gpsimd.dma_start(out=g2_sc_r[:, :, sl], in_=g1_s[:, c % 2]).then_inc(s1_out[c % 2], 16)
                gpsimd.dma_start(out=pt_in[:, sl], in_=pt_s[:, c % 2]).then_inc(s1_out[c % 2], 16)
                gpsimd.dma_start(out=xn_sc_r[:, :, sl], in_=xnf_s[:, c % 2]).then_inc(s1_out[c % 2], 16)
            gpsimd.wait_ge(s1_out[0], N1 // 2 * 64)
            gpsimd.wait_ge(s1_out[1], N1 // 2 * 64)
            gpsimd.collective_compute("AllReduce", ALU.add, replica_groups=RG,
                                      ins=[pt_in.opt()], outs=[pt_sum.opt()]).then_inc(s_cc, 1)
            for c in range(N2 // 2):
                gpsimd.wait_ge(s2_y2, c + 1)
                gpsimd.dma_start(out=y2_sla_r[:, :, c * LS:(c + 1) * LS],
                                 in_=y2_sb[:, c % 2]).then_inc(s2_st[c % 2], 16)
            gpsimd.wait_ge(s2_st[0], N2 // 4 * 16)
            gpsimd.wait_ge(s2_st[1], N2 // 4 * 16)
            gpsimd.collective_compute("AllGather", ALU.bypass, replica_groups=RG,
                                      ins=[y2_sla.opt()], outs=[y2_fa.opt()]).then_inc(s_cc, 1)
            for c in range(N2 // 2, N2):
                gpsimd.wait_ge(s2_y2, c + 1)
                cc = c - N2 // 2
                gpsimd.dma_start(out=y2_slb_r[:, :, cc * LS:(cc + 1) * LS],
                                 in_=y2_sb[:, c % 2]).then_inc(s2_st[c % 2], 16)
            gpsimd.wait_ge(s2_st[0], N2 // 2 * 16)
            gpsimd.wait_ge(s2_st[1], N2 // 2 * 16)
            gpsimd.collective_compute("AllGather", ALU.bypass, replica_groups=RG,
                                      ins=[y2_slb.opt()], outs=[y2_fb.opt()]).then_inc(s_cc, 1)
            for c in range(N3):
                gpsimd.wait_ge(s3_add, 2 * (c + 1))
                gpsimd.dma_start(out=out[c * LC3:(c + 1) * LC3, :].rearrange("(i p) f -> p i f", p=128),
                                 in_=out_s[:, :, :]).then_inc(s3_st, 16)
            gpsimd.wait_ge(s3_st, N3 * 16)

        @block.tensor
        def _(tensor):
            tensor.wait_ge(s_io, 2)
            xst = x_sb.ap().rearrange("p b t f -> p (b t f)")
            tensor.wait_ge(s_wdma, 64)
            for half in range(2):
                if half == 1:
                    tensor.wait_ge(s_wdma, 80)
                for mi4 in range(4):
                    mi = half * 4 + mi4
                    for kj in range(8):
                        idx = mi * 8 + kj
                        _w(tensor, s1_tpe, idx - 1)
                        tensor.transpose(out=ps_tp[:, idx % 2, 0:128],
                                         in_=xst[:, mi4 * 1024 + kj * 128: mi4 * 1024 + (kj + 1) * 128],
                                         identity=id_s[:]).then_inc(s1_tp, 1)
            for dj in range(4):
                idx = 64 + dj
                _w(tensor, s1_tpe, idx - 1)
                tensor.transpose(out=ps_tp[:, idx % 2, 0:96], in_=wst2_s[:, dj * 128:(dj + 1) * 128],
                                 identity=id_s[0:96, 0:96]).then_inc(s1_tp, 1)
            for c in range(N1):
                base = 68 + 8 * c
                tensor.wait_ge(s1_aff, c + 1)
                for j in range(8):
                    _w(tensor, s1_tpe, base + j - 1)
                    for i in range(NT1):
                        ii = tensor.transpose(out=ps_tp[:, j % 2, i * 128:(i + 1) * 128],
                                              in_=xnt_s[:, i, j * 128:(j + 1) * 128],
                                              identity=id_s[:])
                    ii.then_inc(s1_tp, 1)
                tensor.wait_ge(s1_tpe, base + 8)
                for mi in range(8):
                    gidx = 8 * c + mi
                    _w(tensor, s1_xze, gidx - 1)
                    for kj in range(8):
                        ii = tensor.matmul(out=ps_xz[:, mi % 2, 0:LC1],
                                           lhsT=winT[:, (kj * 8 + mi) * 128:(kj * 8 + mi + 1) * 128],
                                           rhs=xnf_s[:, c % 2, kj, :],
                                           start=(kj == 0), stop=(kj == 7))
                    ii.then_inc(s1_xz, 1)
                tensor.wait_ge(s1_mul, c + 1)
                _w(tensor, s1_pte, c)
                for dj in range(4):
                    ii = tensor.matmul(out=ps_pt[:, 0:LC1], lhsT=wxT[:, dj * 96:(dj + 1) * 96],
                                       rhs=ub_s[:, dj, :], start=(dj == 0), stop=(dj == 3))
                ii.then_inc(s1_pt, 1)
            tensor.wait_ge(s2_wdma, 96)
            tensor.wait_ge(s1_xze, 8 * N1)
            for dj in range(4):
                _w(tensor, s2_dpu, dj)
                tensor.transpose(out=ps_w2[:, :], in_=wdt_st[:, dj, :], identity=id_s[:]
                                 ).then_inc(s2_dp, 1)
            ps_y4 = [ps_tp.ap()[:, 0, 0:LS], ps_tp.ap()[:, 1, 0:LS],
                     ps_xz.ap()[:, 0, 0:LS], ps_xz.ap()[:, 1, 0:LS]]
            for c in range(N2):
                tensor.wait_ge(s2_sum, 2 * c + 1)
                for dj in range(4):
                    gidx = 4 * c + dj
                    _w(tensor, s2_dpu, 4 + gidx - 1)
                    tensor.matmul(out=ps_dp[:, dj % 2, 0:LS], lhsT=wdtT[:, dj, :],
                                  rhs=dtb16[:, c % 2, :], start=True, stop=True).then_inc(s2_dp, 1)
                tensor.wait_ge(s2_sum, 2 * c + 2)
                _w(tensor, s2_ydu, c)

                def rep(n):
                    ridx = NST * c + n
                    _w(tensor, s2_repe, ridx - 1)
                    tensor.matmul(out=ps_rep[:, n % 2, 0, :], lhsT=en_bf[:, n * 128:(n + 1) * 128],
                                  rhs=bps_bf[:], start=True, stop=True)
                    tensor.matmul(out=ps_rep[:, n % 2, 1, :], lhsT=en_bf[:, n * 128:(n + 1) * 128],
                                  rhs=cps_bf[:], start=True, stop=True).then_inc(s2_rep, 1)
                rep(0)
                rep(1)
                for n in range(NST):
                    if n + 2 < NST:
                        rep(n + 2)
                    for dj in range(4):
                        tensor.wait_ge(s2_hc, 64 * c + n * 4 + dj + 1)
                        ii = tensor.matmul(out=ps_y4[dj], lhsT=id_bf[:],
                                           rhs=hc_sb[:, n % 2, dj, :],
                                           start=(n == 0), stop=(n == NST - 1))
                    ii.then_inc(s2_hcu, 1)
                tensor.nop().then_inc(s2_yd, 1)
            tensor.wait_ge(s2_ydu, N2)
            tensor.wait_ge(s3_w, 32)
            for m2 in range(2):
                for kj in range(16):
                    idx = m2 * 16 + kj
                    _w(tensor, s3_ev, idx - 1)
                    tensor.transpose(out=ps_tp[:, idx % 2, 0:128],
                                     in_=xst[:, m2 * 2048 + kj * 128: m2 * 2048 + (kj + 1) * 128],
                                     identity=id_s[:]).then_inc(s3_tp, 1)
            tensor.wait_ge(s3_w, 64)
            for m2 in range(2):
                for kj in range(8):
                    idx = 32 + m2 * 8 + kj
                    _w(tensor, s3_ev, idx - 1)
                    tensor.transpose(out=ps_tp[:, idx % 2, 0:128],
                                     in_=xst[:, m2 * 1024 + kj * 128: m2 * 1024 + (kj + 1) * 128],
                                     identity=id_s[:]).then_inc(s3_tp, 1)
            tensor.wait_ge(s3_ev, 48)
            for c in range(N3):
                tensor.wait_ge(s3_ld[c % 2], 32 * (c // 2 + 1))
                for fi in range(2):
                    gidx = 2 * c + fi
                    _w(tensor, s3_sum, gidx - 1)
                    _w(tensor, s3_ff, gidx - 1)
                    for kj in range(16):
                        tensor.matmul(out=ps_xz[:, fi % 2, 0:LC3],
                                      lhsT=woT[:, (kj * 2 + fi) * 128:(kj * 2 + fi + 1) * 128],
                                      rhs=y2c_s[:, c % 2, kj, :],
                                      start=(kj == 0), stop=(kj == 15))
                    for kj in range(8):
                        ii = tensor.matmul(out=ps_dp[:, fi % 2, 0:LC3],
                                           lhsT=cbT[:, (kj * 2 + fi) * 128:(kj * 2 + fi + 1) * 128],
                                           rhs=xnc_s[:, c % 2, kj, :],
                                           start=(kj == 0), stop=(kj == 7))
                    ii.then_inc(s3_mm, 1)
                tensor.wait_ge(s3_sum, 2 * (c + 1))
                for i in range(2):
                    tidx = 2 * c + i
                    _w(tensor, s3_add, tidx - 1)
                    for fi in range(2):
                        ii = tensor.transpose(out=ps_tp[:, i % 2, fi * 128:(fi + 1) * 128],
                                              in_=af_s[:, fi, i * 128:(i + 1) * 128],
                                              identity=id_s[:])
                    ii.then_inc(s3_tt, 1)

        @block.scalar
        def _(scalar):
            for idx in range(64):
                mi, kj = idx // 8, idx % 8
                scalar.wait_ge(s1_tp, idx + 1)
                scalar.copy(out=winT[:, (kj * 8 + mi) * 128:(kj * 8 + mi + 1) * 128],
                            in_=ps_tp[:, idx % 2, 0:128]).then_inc(s1_tpe, 1)
            for dj in range(4):
                idx = 64 + dj
                scalar.wait_ge(s1_tp, idx + 1)
                scalar.copy(out=wxT[:, dj * 96:(dj + 1) * 96],
                            in_=ps_tp[:, idx % 2, 0:96]).then_inc(s1_tpe, 1)
            scalar.wait_ge(s_pst, N1)
            scalar.activation(out=rst_s[:], in_=mva_s[:, :, 1:2], func=AF.Ln, bias=EPS)
            scalar.activation(out=rst_s[:], in_=rst_s[:], func=AF.Exp, scale=-0.5).then_inc(s_rsl, 1)
            for c in range(N1):
                base = 68 + 8 * c
                _w(scalar, s1_xz, 8 * (c - 1))
                if c >= 2:
                    scalar.wait_ge(s1_out[c % 2], 64 * (c // 2))
                for j in range(8):
                    scalar.wait_ge(s1_tp, base + j + 1)
                    scalar.copy(out=xnf_s[:, c % 2, j, :], in_=ps_tp[:, j % 2, 0:LC1]).then_inc(s1_tpe, 1)
                _w(scalar, s1_mul, c)
                for mi in range(8):
                    gidx = 8 * c + mi
                    scalar.wait_ge(s1_xz, gidx + 1)
                    scalar.copy(out=xz_s[:, mi, :], in_=ps_xz[:, mi % 2, 0:LC1]).then_inc(s1_xze, 1)
                scalar.wait_ge(s1_conv, c + 1)
                for dj in range(4):
                    scalar.activation(out=sig_s[:, dj, :], in_=conv_c[:, dj, :], func=AF.Sigmoid)
                for dj in range(4):
                    ii = scalar.activation(out=sig_s[:, 4 + dj, :], in_=xz_s[:, 4 + dj, :],
                                           func=AF.Sigmoid)
                ii.then_inc(s1_sig, 1)
                scalar.wait_ge(s1_pt, c + 1)
                scalar.copy(out=pt_s[:, c % 2, :], in_=ps_pt[:, 0:LC1]).then_inc(s1_pte, 1)
            scalar.wait_ge(s2_wdma, 96)
            scalar.activation(out=a_sb.ap().rearrange("p a b -> p (a b)"),
                              in_=a_sb.ap().rearrange("p a b -> p (a b)"),
                              func=AF.Exp).then_inc(s2_ax, 1)
            for dj in range(4):
                scalar.wait_ge(s2_dp, dj + 1)
                scalar.copy(out=wdtT[:, dj, :], in_=ps_w2[:, :]).then_inc(s2_dpu, 1)
            for c in range(N2):
                scalar.wait_ge(s2_nz, c + 1)
                for dj in range(4):
                    scalar.activation(out=sp_sb[:, dj, :], in_=nz_sb[:, dj, :], func=AF.Exp)
                for dj in range(4):
                    ii = scalar.activation(out=sp_sb[:, dj, :], in_=sp_sb[:, dj, :], func=AF.Ln,
                                           bias=1.0)
                ii.then_inc(s2_sp, 1)
                scalar.wait_ge(s2_del, c + 1)
                for n in range(NST):
                    ridx = NST * c + n
                    scalar.wait_ge(s2_rep, ridx + 1)
                    _w(scalar, s2_rbu, ridx - 1)
                    scalar.copy(out=rep_bf[:, n % 2, 0, :], in_=ps_rep[:, n % 2, 0, :])
                    scalar.copy(out=rep_bf[:, n % 2, 1, :], in_=ps_rep[:, n % 2, 1, :]).then_inc(s2_repe, 1)
                    for dj in range(4):
                        idx = (c * NST + n) * 4 + dj
                        _w(scalar, s2_dau, idx - 1)
                        scalar.activation(out=da_sb[:, idx % 2, :], in_=del_sb[:, dj, :],
                                          func=AF.Exp, scale=a_sb[:, dj, n:n + 1]).then_inc(s2_da, 1)
            for idx in range(32):
                m2, kj = idx // 16, idx % 16
                scalar.wait_ge(s3_tp, idx + 1)
                scalar.copy(out=woT[:, (kj * 2 + m2) * 128:(kj * 2 + m2 + 1) * 128],
                            in_=ps_tp[:, idx % 2, 0:128]).then_inc(s3_ev, 1)
            for idx in range(32, 48):
                m2, kj = (idx - 32) // 8, (idx - 32) % 8
                scalar.wait_ge(s3_tp, idx + 1)
                scalar.copy(out=cbT[:, (kj * 2 + m2) * 128:(kj * 2 + m2 + 1) * 128],
                            in_=ps_tp[:, idx % 2, 0:128]).then_inc(s3_ev, 1)
            scalar.wait_ge(s2_wdma, 96)
            scalar.mul(out=scb_s[:], in_=scb_s[:], mul=1.0 / 127.0)
            for c in range(N3):
                for fi in range(2):
                    gidx = 2 * c + fi
                    scalar.wait_ge(s3_mm, gidx + 1)
                    _w(scalar, s3_sum, gidx - 1)
                    scalar.activation(out=ffn_s[:, fi % 2, :], in_=ps_dp[:, fi % 2, 0:LC3],
                                      func=AF.Identity, scale=scb_s[:, fi:fi + 1]).then_inc(s3_ff, 1)

        @block.vector
        def _(vector):
            vector.wait_ge(s_io, 1)
            vector.tensor_scalar(out=id_s[:], in0=io_s[:], scalar1=0.0, op0=ALU.is_equal,
                                 scalar2=None)
            for n in range(NST):
                ii = vector.tensor_scalar(out=en_bf[:, n * 128:(n + 1) * 128], in0=io16[:],
                                          scalar1=float(n), op0=ALU.is_equal, scalar2=None)
            ii.then_inc(s_io, 1)
            vector.tensor_copy(out=id_bf[:], in_=id_s[:])
            vector.memset(cst_s[:, 0:1], EPS)
            vector.memset(cst_s[:, 1:2], 1.0)
            vector.memset(pad_s.ap().rearrange("p a b -> p (a b)"), 0.0)
            vector.memset(state.ap().rearrange("p a b -> p (a b)"), 0.0)
            for c in range(N1):
                vector.wait_ge(s_px[c % 2], 16 * (c // 2 + 1))
                for i in range(NT1):
                    vector.bn_stats(out=bst_s[:, i, 0:6], in_=x_sb[:, c % 2, i, 0:512])
                    vector.bn_stats(out=bst_s[:, i, 6:12], in_=x_sb[:, c % 2, i, 512:1024])
                for i in range(NT1):
                    tg = c * NT1 + i
                    ii = vector.bn_aggr(out=mva_s[:, tg, :],
                                        in_=bst_s.ap()[:, i].rearrange("p (a b) -> p a b", b=3))
                ii.then_inc(s_pst, 1)
            for c in range(N1):
                vector.wait_ge(s1_x[c % 2], 16 * (c // 2 + 1))
                vector.wait_ge(s_rsl, 1)
                _w(vector, s1_tp, 68 + 8 * c if c > 0 else 0)
                for i in range(NT1):
                    tg = c * NT1 + i
                    ii = vector.tensor_scalar(out=xnt_s[:, i, :], in0=x_sb[:, c % 2, i, :],
                                              scalar1=mva_s[:, tg, 0:1], op0=ALU.subtract,
                                              scalar2=rst_s[:, tg:tg + 1], op1=ALU.mult)
                ii.then_inc(s1_aff, 1)
                vector.wait_ge(s1_xze, 8 * c + 4)
                for dj in range(4):
                    vector.tensor_copy(out=pad_s[:, dj, 3:3 + LC1], in_=xz_s[:, dj, :])
                for dj in range(4):
                    vector.tensor_scalar(out=cva_s[:], in0=pad_s[:, dj, 0:LC1],
                                         scalar1=cw_s[:, 4 * dj:4 * dj + 1], op0=ALU.mult,
                                         scalar2=cb_s[:, dj:dj + 1], op1=ALU.add)
                    vector.scalar_tensor_tensor(out=cvb_s[:], in0=pad_s[:, dj, 1:1 + LC1],
                                                scalar=cw_s[:, 4 * dj + 1:4 * dj + 2],
                                                in1=cva_s[:], op0=ALU.mult, op1=ALU.add)
                    vector.scalar_tensor_tensor(out=cva_s[:], in0=pad_s[:, dj, 2:2 + LC1],
                                                scalar=cw_s[:, 4 * dj + 2:4 * dj + 3],
                                                in1=cvb_s[:], op0=ALU.mult, op1=ALU.add)
                    vector.scalar_tensor_tensor(out=conv_c[:, dj, :], in0=pad_s[:, dj, 3:3 + LC1],
                                                scalar=cw_s[:, 4 * dj + 3:4 * dj + 4],
                                                in1=cva_s[:], op0=ALU.mult, op1=ALU.add)
                    ii = vector.tensor_copy(out=pad_s[:, dj, 0:3], in_=pad_s[:, dj, LC1:LC1 + 3])
                ii.then_inc(s1_conv, 1)
                vector.wait_ge(s1_sig, c + 1)
                if c >= 2:
                    vector.wait_ge(s1_out[c % 2], 64 * (c // 2))
                for dj in range(4):
                    vector.tensor_mul(out=u1_s[:, c % 2, dj, :], in0=conv_c[:, dj, :], in1=sig_s[:, dj, :])
                _w(vector, s1_pt, c)
                for dj in range(4):
                    vector.tensor_copy(out=ub_s[:, dj, :], in_=u1_s[:, c % 2, dj, :])
                for dj in range(4):
                    ii = vector.tensor_mul(out=g1_s[:, c % 2, dj, :], in0=xz_s[:, 4 + dj, :],
                                           in1=sig_s[:, 4 + dj, :])
                ii.then_inc(s1_mul, 1)
            vector.wait_ge(s2_ax, 1)
            vector.tensor_scalar(out=a_sb.ap().rearrange("p a b -> p (a b)"),
                                 in0=a_sb.ap().rearrange("p a b -> p (a b)"),
                                 scalar1=-1.0, op0=ALU.mult, scalar2=None)
            for c in range(N2):
                vector.wait_ge(s2_ld[c % 2], 80 * (c // 2 + 1))
                _w(vector, s2_dp, 4 + 4 * c)
                vector.tensor_copy(out=dtb16[:, c % 2, :], in_=dtl_sb[:, c % 2, :]).then_inc(s2_sum, 1)
                vector.tensor_copy(out=bps_bf[:], in_=bps_sb[:, c % 2, :])
                vector.tensor_copy(out=cps_bf[:], in_=cps_sb[:, c % 2, :]).then_inc(s2_sum, 1)
                for dj in range(4):
                    vector.wait_ge(s2_dp, 4 + 4 * c + dj + 1)
                    vector.tensor_scalar(out=z_sb[:, dj, :], in0=ps_dp[:, dj % 2, 0:LS],
                                         scalar1=dtb_s[:, dj:dj + 1], op0=ALU.add,
                                         scalar2=None).then_inc(s2_dpu, 1)
                for dj in range(4):
                    ii = vector.scalar_tensor_tensor(out=nz_sb[:, dj, :], in0=z_sb[:, dj, :],
                                                     scalar=-1.0, in1=z_sb[:, dj, :],
                                                     op0=ALU.mult, op1=ALU.min)
                ii.then_inc(s2_nz, 1)
                vector.wait_ge(s2_sp, c + 1)
                for dj in range(4):
                    ii = vector.scalar_tensor_tensor(out=del_sb[:, dj, :], in0=z_sb[:, dj, :],
                                                     scalar=0.0, in1=sp_sb[:, dj, :],
                                                     op0=ALU.max, op1=ALU.add)
                ii.then_inc(s2_del, 1)
                for dj in range(4):
                    vector.tensor_mul(out=du_sb[:, dj, :], in0=del_sb[:, dj, :], in1=u_sb[:, c % 2, dj, :])
                for dj in range(4):
                    vector.tensor_copy(out=du_bf[:, dj, :], in_=du_sb[:, dj, :])
                for n in range(NST):
                    vector.wait_ge(s2_repe, NST * c + n + 1)
                    for dj in range(4):
                        idx = (c * NST + n) * 4 + dj
                        vector.wait_ge(s2_da, idx + 1)
                        vector.tensor_mul(out=dbu_sb[:], in0=du_bf[:, dj, :], in1=rep_bf[:, n % 2, 0, :])
                        vector.tensor_tensor_scan(out=h_sb[:], data0=da_sb[:, idx % 2, :], data1=dbu_sb[:],
                                                  initial=state[:, dj, n:n + 1],
                                                  op0=ALU.mult, op1=ALU.add).then_inc(s2_dau, 1)
                        vector.tensor_copy(out=state[:, dj, n:n + 1], in_=h_sb[:, LS - 1:LS])
                        if n >= 2:
                            vector.wait_ge(s2_hcu, 16 * c + n - 1)
                        vector.tensor_mul(out=hc_sb[:, n % 2, dj, :], in0=h_sb[:],
                                           in1=rep_bf[:, n % 2, 1, :]).then_inc(s2_hc, 1)
                        if dj == 3:
                            vector.tensor_copy(out=dbu_sb[:, 0:1],
                                               in_=dbu_sb[:, 0:1]).then_inc(s2_rbu, 1)
                if c >= 2:
                    vector.wait_ge(s2_st[c % 2], 16 * (c // 2))
                vector.wait_ge(s2_yd, c + 1)
                ps_y4v = [ps_tp.ap()[:, 0, 0:LS], ps_tp.ap()[:, 1, 0:LS],
                          ps_xz.ap()[:, 0, 0:LS], ps_xz.ap()[:, 1, 0:LS]]
                for dj in range(4):
                    vector.scalar_tensor_tensor(out=ydu_sb[:], in0=u_sb[:, c % 2, dj, :],
                                                scalar=dsk_s[:, dj:dj + 1],
                                                in1=ps_y4v[dj],
                                                op0=ALU.mult, op1=ALU.add)
                    ii = vector.tensor_mul(out=y2_sb[:, c % 2, dj, :], in0=ydu_sb[:],
                                           in1=g2_sb[:, c % 2, dj, :])
                ii.then_inc(s2_y2, 1)
                vector.nop().then_inc(s2_ydu, 1)
                vector.tensor_copy(out=dbu_sb[:, 0:1], in_=dbu_sb[:, 0:1]).then_inc(s2_used, 1)
            for c in range(N3):
                for fi in range(2):
                    gidx = 2 * c + fi
                    vector.wait_ge(s3_ff, gidx + 1)
                    vector.scalar_tensor_tensor(out=af_s[:, fi, :], in0=ps_xz[:, fi % 2, 0:LC3],
                                                scalar=bias_s[:, fi:fi + 1], in1=ffn_s[:, fi % 2, :],
                                                op0=ALU.add, op1=ALU.add).then_inc(s3_sum, 1)
                _w(vector, s3_st, 16 * c)
                vector.wait_ge(s3_xc[c % 2], 16 * (c // 2 + 1))
                for i in range(2):
                    tidx = 2 * c + i
                    vector.wait_ge(s3_tt, tidx + 1)
                    vector.tensor_add(out=out_s[:, i, :],
                                      in0=ps_tp[:, i % 2, 0:FS], in1=xcol_s[:, c % 2, i, :]
                                      ).then_inc(s3_add, 1)
    return nc


def kernel(L0_attn_hidden_states, L0_attn_W_in, L0_attn_conv_w, L0_attn_conv_b,
           L0_attn_W_x, L0_attn_W_dt, L0_attn_dt_bias, L0_attn_A_log,
           L0_attn_D_skip, L0_attn_W_out, L0_ffn_CB, L0_ffn_SCB, L0_ffn_bias):
    x = np.ascontiguousarray(np.asarray(L0_attn_hidden_states, dtype=np.float32))
    W_in = np.asarray(L0_attn_W_in, dtype=np.float32)
    conv_w = np.asarray(L0_attn_conv_w, dtype=np.float32)
    conv_b = np.asarray(L0_attn_conv_b, dtype=np.float32)
    W_x = np.asarray(L0_attn_W_x, dtype=np.float32)
    W_dt = np.asarray(L0_attn_W_dt, dtype=np.float32)
    dt_bias = np.asarray(L0_attn_dt_bias, dtype=np.float32)
    A_log = np.asarray(L0_attn_A_log, dtype=np.float32)
    D_skip = np.asarray(L0_attn_D_skip, dtype=np.float32)
    W_out = np.asarray(L0_attn_W_out, dtype=np.float32)
    CB = np.asarray(L0_ffn_CB, dtype=np.float32)
    SCB = np.asarray(L0_ffn_SCB, dtype=np.float32)
    fbias = np.asarray(L0_ffn_bias, dtype=np.float32)
    C = np.ascontiguousarray

    nc = build_fused()
    maps = []
    for cid in range(NC):
        b, s = cid // 4, cid % 4
        sl = slice(ES * s, ES * (s + 1))
        fs = slice(FS * s, FS * (s + 1))
        maps.append(dict(
            x=C(x[b]),
            w_in=C(np.concatenate([W_in[ES * s:ES * (s + 1)], W_in[E + ES * s:E + ES * (s + 1)]], axis=0)),
            conv_w=C(conv_w[sl]),
            conv_b=C(conv_b[sl].reshape(4, 128).T),
            w_x=C(W_x[:, sl]),
            w_dt=C(W_dt[sl]),
            dt_bias=C(dt_bias[sl].reshape(4, 128).T),
            a_log=C(A_log[sl]),
            d_skip=C(D_skip[sl].reshape(4, 128).T),
            w_out_s=C(W_out[fs]),
            cb_sl=C(CB[fs]),
            scb_v=C(SCB[fs].reshape(2, 128).T),
            bias_v=C(fbias[fs].reshape(2, 128).T),
            xcol=C(x[b][:, fs]),
        ))
    res = run_bass_kernel_spmd(nc, maps, core_ids=list(range(NC))).results

    out = np.empty((B, L, D), dtype=np.float32)
    for cid in range(NC):
        b, s = cid // 4, cid % 4
        out[b, :, FS * s:FS * (s + 1)] = res[cid]["out"]
    return out


# version 11
# speedup vs baseline: 1.3276x; 1.1475x over previous
import contextlib
import numpy as np
import ml_dtypes
import concourse.bass as bass
import concourse.mybir as mybir
from concourse.bass_utils import run_bass_kernel_spmd

F32 = mybir.dt.float32
BF16 = mybir.dt.bfloat16
AF = mybir.ActivationFunctionType
ALU = mybir.AluOpType
AX = mybir.AxisListType

B, L, D, E, NST, R = 2, 8192, 1024, 2048, 16, 64
ES = 512
NC = 8
EPS = 1e-5

LC1 = 256
N1 = L // LC1
NT1 = LC1 // 128
LS = 512
N2 = L // LS
LC3 = 256
N3 = L // LC3
FS = 256


def _w(eng, sem, v):
    if v > 0:
        eng.wait_ge(sem, v)


def build_fused():
    nc = bass.Bass("TRN2", target_bir_lowering=False, detect_race_conditions=False)
    x = nc.dram_tensor("x", [L, D], F32, kind="ExternalInput").ap()
    w_in = nc.dram_tensor("w_in", [1024, D], F32, kind="ExternalInput").ap()
    conv_w = nc.dram_tensor("conv_w", [ES, 4], F32, kind="ExternalInput").ap()
    conv_b = nc.dram_tensor("conv_b", [128, 4], F32, kind="ExternalInput").ap()
    w_x = nc.dram_tensor("w_x", [96, ES], F32, kind="ExternalInput").ap()
    w_dt = nc.dram_tensor("w_dt", [ES, R], F32, kind="ExternalInput").ap()
    dt_bias = nc.dram_tensor("dt_bias", [128, 4], F32, kind="ExternalInput").ap()
    a_log = nc.dram_tensor("a_log", [ES, NST], F32, kind="ExternalInput").ap()
    d_skip = nc.dram_tensor("d_skip", [128, 4], F32, kind="ExternalInput").ap()
    w_out_s = nc.dram_tensor("w_out_s", [FS, E], F32, kind="ExternalInput").ap()
    cb_sl = nc.dram_tensor("cb_sl", [FS, D], F32, kind="ExternalInput").ap()
    scb_v = nc.dram_tensor("scb_v", [128, 2], F32, kind="ExternalInput").ap()
    bias_v = nc.dram_tensor("bias_v", [128, 2], F32, kind="ExternalInput").ap()
    xcol = nc.dram_tensor("xcol", [L, FS], F32, kind="ExternalInput").ap()
    out = nc.dram_tensor("out", [L, FS], F32, kind="ExternalOutput").ap()

    u_sc = nc.dram_tensor("u_sc", [ES, L], F32).ap()
    g2_sc = nc.dram_tensor("g2_sc", [ES, L], F32).ap()
    xn_sc = nc.dram_tensor("xn_sc", [D, L], BF16).ap()
    pt_ina = nc.dram_tensor("pt_ina", [96, L // 2], F32).ap()
    pt_inb = nc.dram_tensor("pt_inb", [96, L // 2], F32).ap()
    pt_suma = nc.dram_tensor("pt_suma", [96, L // 2], F32).ap()
    pt_sumb = nc.dram_tensor("pt_sumb", [96, L // 2], F32).ap()
    y2_sla = nc.dram_tensor("y2_sla", [ES, L // 2], BF16).ap()
    y2_slb = nc.dram_tensor("y2_slb", [ES, L // 2], BF16).ap()
    y2_fa = nc.dram_tensor("y2_fa", [4 * ES, L // 2], BF16).ap()
    y2_fb = nc.dram_tensor("y2_fb", [4 * ES, L // 2], BF16).ap()

    u_sc_r = u_sc.rearrange("(dj p) l -> p dj l", p=128)
    g2_sc_r = g2_sc.rearrange("(dj p) l -> p dj l", p=128)
    xn_sc_r = xn_sc.rearrange("(j p) l -> p j l", p=128)
    y2_sla_r = y2_sla.rearrange("(dj p) l -> p dj l", p=128)
    y2_slb_r = y2_slb.rearrange("(dj p) l -> p dj l", p=128)
    y2_fa_r = y2_fa.rearrange("(kj p) l -> p kj l", p=128)
    y2_fb_r = y2_fb.rearrange("(kj p) l -> p kj l", p=128)

    RG = [[0, 1, 2, 3], [4, 5, 6, 7]]

    with contextlib.ExitStack() as _es:
        sb = lambda name, shape, dt=F32: _es.enter_context(nc.sbuf_tensor(name, shape, dt))
        ps = lambda name, shape: _es.enter_context(nc.psum_tensor(name, shape, F32))
        sem = lambda name: _es.enter_context(nc.semaphore(name))

        io_s = sb("io_s", [128, 128])
        id_s = sb("id_s", [128, 128])
        io16 = sb("io16", [16, 128])
        cst_s = sb("cst_s", [128, 2])

        winT = sb("winT", [128, 8 * 8 * 128], BF16)
        wxT = sb("wxT", [128, 4 * 96], BF16)
        cw_s = sb("cw_s", [128, 16])
        cb_s = sb("cb_s", [128, 4])
        x_sb = sb("x_sb", [128, 2, NT1, 1024])
        bst_s = sb("bst_s", [128, 2, 12])
        mva_s = sb("mva_s", [128, 64, 2])
        rst_s = sb("rst_s", [128, 64])
        xnt_s = sb("xnt_s", [128, 2, NT1, 1024])
        mrs_s = sb("mrs_s", [128, 64])
        xnf_s = sb("xnf_s", [128, 2, 8, LC1], BF16)
        xz_s = sb("xz_s", [128, 8, LC1])
        pad_s = sb("pad_s", [128, 4, LC1 + 3])
        cva_s = sb("cva_s", [128, LC1])
        cvb_s = sb("cvb_s", [128, LC1])
        conv_c = sb("conv_c", [128, 4, LC1])
        sig_s = sb("sig_s", [128, 8, LC1])
        u1_s = sb("u1_s", [128, 2, 4, LC1])
        ub4_s = sb("ub4_s", [128, 2, 4, LC1], BF16)
        g1_s = sb("g1_s", [128, 4, LC1])
        pt_s = sb("pt_s", [96, 2, LC1])
        wst2_s = sb("wst2_s", [96, 512])

        wdt_st = sb("wdt_st", [128, 4, R])
        wdtT = sb("wdtT", [64, 4, 128], BF16)
        a_sb = sb("a_sb", [128, 4, NST])
        dtb_s = sb("dtb_s", [128, 4])
        dsk_s = sb("dsk_s", [128, 4])
        state = sb("state", [128, 4, NST])
        u_sb = x_sb.ap().rearrange("p a b c -> p (a b c)").rearrange(
            "p (x y z) -> p x y z", x=2, y=4, z=LS)
        g2_sb = xnt_s.ap().rearrange("p a b c -> p (a b c)").rearrange(
            "p (x y z) -> p x y z", x=2, y=4, z=LS)
        z_sb = sig_s.ap().rearrange("p a b -> p (a b)").rearrange(
            "p (y z) -> p y z", y=4, z=LS)
        nz_sb = xz_s.ap().rearrange("p a b -> p (a b)").rearrange(
            "p (y z) -> p y z", y=4, z=LS)
        sp_sb = u1_s.ap().rearrange("p a b c -> p (a b c)").rearrange(
            "p (y z) -> p y z", y=4, z=LS)
        dtl_sb = sb("dtl_sb", [64, 2, LS])
        bps_sb = sb("bps_sb", [16, 2, LS])
        cps_sb = sb("cps_sb", [16, 2, LS])
        dtb16 = sb("dtb16", [64, 2, LS], BF16)
        del_sb = sb("del_sb", [128, 4, LS])
        du_sb = sb("du_sb", [128, 4, LS])
        da_sb = sb("da_sb", [128, 2, LS])
        dbu_sb = sb("dbu_sb", [128, LS], BF16)
        h_sb = sb("h_sb", [128, LS], BF16)
        hc_sb = sb("hc_sb", [128, 2, 4, LS], BF16)
        du_bf = sb("du_bf", [128, 4, LS], BF16)
        rep_bf = sb("rep_bf", [128, 2, 2, LS], BF16)
        en_bf = sb("en_bf", [16, 16 * 128], BF16)
        bps_bf = sb("bps_bf", [16, LS], BF16)
        cps_bf = sb("cps_bf", [16, LS], BF16)
        id_bf = sb("id_bf", [128, 128], BF16)
        ydu_sb = sb("ydu_sb", [128, LS])
        y2_sb = sb("y2_sb", [128, 2, 4, LS], BF16)

        woT = winT.ap()[:, 0:16 * 2 * 128]
        cbT = winT.ap()[:, 16 * 2 * 128:16 * 2 * 128 + 8 * 2 * 128]
        scb_s = sb("scb_s", [128, 2])
        bias_s = sb("bias_s", [128, 2])
        y2c_s = sb("y2c_s", [128, 2, 16, LC3], BF16)
        xnc_s = sb("xnc_s", [128, 2, 8, LC3], BF16)
        xcol_s = sb("xcol_s", [128, 2, 2, FS])
        ffn_s = pad_s.ap().rearrange("p a b -> p (a b)")[:, 0:2 * LC3].rearrange(
            "p (y z) -> p y z", y=2, z=LC3)
        af_s = conv_c.ap().rearrange("p a b -> p (a b)")[:, 0:2 * LC3].rearrange(
            "p (y z) -> p y z", y=2, z=LC3)
        out_s = sb("out_s", [128, 2, FS])

        ps_tp = ps("ps_tp", [128, 2, 512])
        ps_xz = ps("ps_xz", [128, 2, 512])
        ps_dp = ps("ps_dp", [128, 2, 512])
        ps_pt = ps_dp.ap()[0:96, 0, :]
        ps_rep = ps("ps_rep", [128, 2, LS])
        ps_w2 = ps_xz.ap()[0:64, 0, 0:128]

        s_io = sem("s_io")
        s_wdma = sem("s_wdma")
        s_px = [sem("s_px0"), sem("s_px1")]
        s_pst = sem("s_pst")
        s_rsl = sem("s_rsl")
        s_mrs = sem("s_mrs")
        s3_lds = [sem("s3_lds0"), sem("s3_lds1")]
        s1_x = [sem("s1_x0"), sem("s1_x1")]
        s1_aff = sem("s1_aff")
        s1_tp = sem("s1_tp")
        s1_tpe = sem("s1_tpe")
        s1_xz = sem("s1_xz")
        s1_xze = sem("s1_xze")
        s1_conv = sem("s1_conv")
        s1_sig = sem("s1_sig")
        s1_mul = sem("s1_mul")
        s1_pt = sem("s1_pt")
        s1_pte = sem("s1_pte")
        s1_out = [sem("s1_out0"), sem("s1_out1")]
        s1_ptd = [sem("s1_ptd0"), sem("s1_ptd1")]
        s_cc = sem("s_cc")
        s2_wdma = sem("s2_wdma")
        s2_ax = sem("s2_ax")
        s2_ld = [sem("s2_ld0"), sem("s2_ld1")]
        s2_used = sem("s2_used")
        s2_sum = sem("s2_sum")
        s2_dp = sem("s2_dp")
        s2_dpu = sem("s2_dpu")
        s2_nz = sem("s2_nz")
        s2_sp = sem("s2_sp")
        s2_del = sem("s2_del")
        s2_rep = sem("s2_rep")
        s2_repe = sem("s2_repe")
        s2_rbu = sem("s2_rbu")
        s2_hc = sem("s2_hc")
        s2_hcu = sem("s2_hcu")
        s2_yd = sem("s2_yd")
        s2_ydu = sem("s2_ydu")
        s2_da = sem("s2_da")
        s2_dau = sem("s2_dau")
        s2_y2 = sem("s2_y2")
        s2_st = [sem("s2_st0"), sem("s2_st1")]
        s3_w = sem("s3_w")
        s3_ld = [sem("s3_ld0"), sem("s3_ld1")]
        s3_xc = [sem("s3_xc0"), sem("s3_xc1")]
        s3_tp = sem("s3_tp")
        s3_ev = sem("s3_ev")
        s3_mm = sem("s3_mm")
        s3_ff = sem("s3_ff")
        s3_sum = sem("s3_sum")
        s3_tt = sem("s3_tt")
        s3_add = sem("s3_add")
        s3_st = sem("s3_st")
        block = _es.enter_context(nc.Block())

        nc.const_aps.register(F32, EPS, cst_s[:, 0:1])
        nc.const_aps.register(F32, 1.0, cst_s[:, 1:2])

        @block.sync
        def _(sync):
            sync.dma_start(out=x_sb.ap().rearrange("p b t f -> p (b t) f"),
                           in_=w_in[0:512, :].rearrange("(mi p) k -> p mi k", p=128)).then_inc(s_wdma, 16)
            sync.dma_start(out=wst2_s[:], in_=w_x).then_inc(s_wdma, 16)
            sync.dma_start(out=cw_s.ap().rearrange("p (dj k) -> p dj k", k=4),
                           in_=conv_w.rearrange("(dj p) k -> p dj k", p=128)).then_inc(s_wdma, 16)
            sync.dma_start(out=cb_s[:], in_=conv_b).then_inc(s_wdma, 16)
            sync.wait_ge(s1_tp, 32)
            sync.dma_start(out=x_sb.ap().rearrange("p b t f -> p (b t) f"),
                           in_=w_in[512:1024, :].rearrange("(mi p) k -> p mi k", p=128)).then_inc(s_wdma, 16)
            sync.dma_start(out=wdt_st[:], in_=w_dt.rearrange("(dj p) r -> p dj r", p=128)).then_inc(s2_wdma, 16)
            sync.dma_start(out=a_sb[:], in_=a_log.rearrange("(dj p) n -> p dj n", p=128)).then_inc(s2_wdma, 16)
            sync.dma_start(out=dtb_s[:], in_=dt_bias).then_inc(s2_wdma, 16)
            sync.dma_start(out=dsk_s[:], in_=d_skip).then_inc(s2_wdma, 16)
            sync.dma_start(out=scb_s[:], in_=scb_v).then_inc(s2_wdma, 16)
            sync.dma_start(out=bias_s[:], in_=bias_v).then_inc(s2_wdma, 16)
            for c in range(N1):
                if c < 2:
                    sync.wait_ge(s1_tp, 68)
                else:
                    _w(sync, s_pst, c - 1)
                sync.dma_start(out=x_sb[:, c % 2],
                               in_=x[c * LC1:(c + 1) * LC1, :].rearrange("(i p) f -> p i f", p=128)
                               ).then_inc(s_px[c % 2], 16)
            for c in range(N1):
                if c < 2:
                    sync.wait_ge(s_pst, N1)
                else:
                    sync.wait_ge(s1_aff, c - 1)
                sync.dma_start(out=x_sb[:, c % 2],
                               in_=x[c * LC1:(c + 1) * LC1, :].rearrange("(i p) f -> p i f", p=128)
                               ).then_inc(s1_x[c % 2], 16)
            sync.wait_ge(s1_aff, N1)
            sync.wait_ge(s1_tp, 68 + 8 * N1)
            sync.wait_ge(s_cc, 1)
            for c in range(N2):
                if c == N2 // 2:
                    sync.wait_ge(s_cc, 2)
                _w(sync, s2_used, c - 1)
                sl = slice(c * LS, (c + 1) * LS)
                pts_h = pt_suma if c < N2 // 2 else pt_sumb
                hsl = slice((c % (N2 // 2)) * LS, (c % (N2 // 2) + 1) * LS)
                sync.dma_start(out=u_sb[:, c % 2], in_=u_sc_r[:, :, sl]).then_inc(s2_ld[c % 2], 16)
                sync.dma_start(out=g2_sb[:, c % 2], in_=g2_sc_r[:, :, sl]).then_inc(s2_ld[c % 2], 16)
                sync.dma_start(out=dtl_sb[:, c % 2, :], in_=pts_h[0:64, hsl]).then_inc(s2_ld[c % 2], 16)
                sync.dma_start(out=bps_sb[:, c % 2, :], in_=pts_h[64:80, hsl]).then_inc(s2_ld[c % 2], 16)
                sync.dma_start(out=cps_sb[:, c % 2, :], in_=pts_h[80:96, hsl]).then_inc(s2_ld[c % 2], 16)
            sync.wait_ge(s2_used, N2)
            xstf = x_sb.ap().rearrange("p b t f -> p (b t f)")
            for m2 in range(2):
                sync.dma_start(out=xstf[:, m2 * 2048:(m2 + 1) * 2048],
                               in_=w_out_s[m2 * 128:(m2 + 1) * 128, :]).then_inc(s3_w, 16)
            sync.wait_ge(s3_tp, 32)
            for m2 in range(2):
                sync.dma_start(out=xstf[:, m2 * 1024:(m2 + 1) * 1024],
                               in_=cb_sl[m2 * 128:(m2 + 1) * 128, :]).then_inc(s3_w, 16)
            sync.wait_ge(s_cc, 3)
            for c in range(N3):
                if c == N3 // 2:
                    sync.wait_ge(s_cc, 4)
                _w(sync, s3_mm, 2 * (c - 1))
                if c < N3 // 2:
                    sync.dma_start(out=y2c_s[:, c % 2, 0:8],
                                   in_=y2_fa_r[:, 0:8, c * LC3:(c + 1) * LC3]).then_inc(s3_ld[c % 2], 16)
                else:
                    cc = c - N3 // 2
                    sync.dma_start(out=y2c_s[:, c % 2, 0:8],
                                   in_=y2_fb_r[:, 0:8, cc * LC3:(cc + 1) * LC3]).then_inc(s3_ld[c % 2], 16)
                sl = slice(c * LC3, (c + 1) * LC3)
                sync.dma_start(out=xnc_s[:, c % 2], in_=xn_sc_r[:, :, sl]).then_inc(s3_ld[c % 2], 16)

        @block.gpsimd
        def _(gpsimd):
            gpsimd.iota(out=io_s[:], pattern=[[1, 128]], base=0, channel_multiplier=-1,
                        allow_small_or_imprecise_dtypes=True)
            gpsimd.iota(out=io16[:], pattern=[[0, 128]], base=0, channel_multiplier=1,
                        allow_small_or_imprecise_dtypes=True).then_inc(s_io, 1)
            def p1_ptdma(c):
                gpsimd.wait_ge(s1_pte, c + 1)
                pth = pt_ina if c < N1 // 2 else pt_inb
                gpsimd.dma_start(out=pth[:, (c % (N1 // 2)) * LC1:(c % (N1 // 2) + 1) * LC1],
                                 in_=pt_s[:, c % 2]).then_inc(s1_ptd[c % 2], 16)

            for c in range(N1):
                gpsimd.wait_ge(s1_mul, c + 1)
                sl = slice(c * LC1, (c + 1) * LC1)
                gpsimd.dma_start(out=u_sc_r[:, :, sl], in_=u1_s[:, c % 2]).then_inc(s1_out[c % 2], 16)
                gpsimd.dma_start(out=g2_sc_r[:, :, sl], in_=g1_s[:]).then_inc(s1_out[c % 2], 16)
                gpsimd.dma_start(out=xn_sc_r[:, :, sl], in_=xnf_s[:, c % 2]).then_inc(s1_out[c % 2], 16)
                if c >= 2:
                    p1_ptdma(c - 2)
            p1_ptdma(N1 - 2)
            p1_ptdma(N1 - 1)
            gpsimd.wait_ge(s1_out[0], N1 // 4 * 48)
            gpsimd.wait_ge(s1_out[1], N1 // 4 * 48)
            gpsimd.wait_ge(s1_ptd[0], N1 // 4 * 16)
            gpsimd.wait_ge(s1_ptd[1], N1 // 4 * 16)
            gpsimd.collective_compute("AllReduce", ALU.add, replica_groups=RG,
                                      ins=[pt_ina.opt()], outs=[pt_suma.opt()]).then_inc(s_cc, 1)
            gpsimd.wait_ge(s1_out[0], N1 // 2 * 48)
            gpsimd.wait_ge(s1_out[1], N1 // 2 * 48)
            gpsimd.wait_ge(s1_ptd[0], N1 // 2 * 16)
            gpsimd.wait_ge(s1_ptd[1], N1 // 2 * 16)
            gpsimd.collective_compute("AllReduce", ALU.add, replica_groups=RG,
                                      ins=[pt_inb.opt()], outs=[pt_sumb.opt()]).then_inc(s_cc, 1)
            for c in range(N2 // 2):
                gpsimd.wait_ge(s2_y2, c + 1)
                gpsimd.dma_start(out=y2_sla_r[:, :, c * LS:(c + 1) * LS],
                                 in_=y2_sb[:, c % 2]).then_inc(s2_st[c % 2], 16)
            gpsimd.wait_ge(s2_st[0], N2 // 4 * 16)
            gpsimd.wait_ge(s2_st[1], N2 // 4 * 16)
            gpsimd.collective_compute("AllGather", ALU.bypass, replica_groups=RG,
                                      ins=[y2_sla.opt()], outs=[y2_fa.opt()]).then_inc(s_cc, 1)
            for c in range(N2 // 2, N2):
                gpsimd.wait_ge(s2_y2, c + 1)
                cc = c - N2 // 2
                gpsimd.dma_start(out=y2_slb_r[:, :, cc * LS:(cc + 1) * LS],
                                 in_=y2_sb[:, c % 2]).then_inc(s2_st[c % 2], 16)
            gpsimd.wait_ge(s2_st[0], N2 // 2 * 16)
            gpsimd.wait_ge(s2_st[1], N2 // 2 * 16)
            gpsimd.collective_compute("AllGather", ALU.bypass, replica_groups=RG,
                                      ins=[y2_slb.opt()], outs=[y2_fb.opt()]).then_inc(s_cc, 1)
            gpsimd.wait_ge(s_cc, 3)

            def p3_pref(c):
                if c == N3 // 2:
                    gpsimd.wait_ge(s_cc, 4)
                _w(gpsimd, s3_mm, 2 * (c - 1))
                if c < N3 // 2:
                    gpsimd.dma_start(out=y2c_s[:, c % 2, 8:16],
                                     in_=y2_fa_r[:, 8:16, c * LC3:(c + 1) * LC3]).then_inc(s3_lds[c % 2], 16)
                else:
                    cc = c - N3 // 2
                    gpsimd.dma_start(out=y2c_s[:, c % 2, 8:16],
                                     in_=y2_fb_r[:, 8:16, cc * LC3:(cc + 1) * LC3]).then_inc(s3_lds[c % 2], 16)
                sl = slice(c * LC3, (c + 1) * LC3)
                _w(gpsimd, s3_add, 2 * (c - 1))
                gpsimd.dma_start(out=xcol_s[:, c % 2],
                                 in_=xcol[sl, :].rearrange("(i p) f -> p i f", p=128)).then_inc(s3_xc[c % 2], 16)

            for c in range(N3):
                p3_pref(c)
                if c > 0:
                    gpsimd.wait_ge(s3_add, 2 * c)
                    gpsimd.dma_start(out=out[(c - 1) * LC3:c * LC3, :].rearrange("(i p) f -> p i f", p=128),
                                     in_=out_s[:, :, :]).then_inc(s3_st, 16)
            gpsimd.wait_ge(s3_add, 2 * N3)
            gpsimd.dma_start(out=out[(N3 - 1) * LC3:N3 * LC3, :].rearrange("(i p) f -> p i f", p=128),
                             in_=out_s[:, :, :]).then_inc(s3_st, 16)
            gpsimd.wait_ge(s3_st, N3 * 16)

        @block.tensor
        def _(tensor):
            tensor.wait_ge(s_io, 2)
            xst = x_sb.ap().rearrange("p b t f -> p (b t f)")
            tensor.wait_ge(s_wdma, 64)
            for half in range(2):
                if half == 1:
                    tensor.wait_ge(s_wdma, 80)
                for mi4 in range(4):
                    mi = half * 4 + mi4
                    for kj in range(8):
                        idx = mi * 8 + kj
                        _w(tensor, s1_tpe, idx - 1)
                        tensor.transpose(out=ps_tp[:, idx % 2, 0:128],
                                         in_=xst[:, mi4 * 1024 + kj * 128: mi4 * 1024 + (kj + 1) * 128],
                                         identity=id_s[:]).then_inc(s1_tp, 1)
            for dj in range(4):
                idx = 64 + dj
                _w(tensor, s1_tpe, idx - 1)
                tensor.transpose(out=ps_tp[:, idx % 2, 0:96], in_=wst2_s[:, dj * 128:(dj + 1) * 128],
                                 identity=id_s[0:96, 0:96]).then_inc(s1_tp, 1)
            def p1_tp_mm(c):
                base = 68 + 8 * c
                tensor.wait_ge(s1_aff, c + 1)
                for j in range(8):
                    _w(tensor, s1_tpe, base + j - 1)
                    for i in range(NT1):
                        ii = tensor.transpose(out=ps_tp[:, j % 2, i * 128:(i + 1) * 128],
                                              in_=xnt_s[:, c % 2, i, j * 128:(j + 1) * 128],
                                              identity=id_s[:])
                    ii.then_inc(s1_tp, 1)
                tensor.wait_ge(s1_tpe, base + 8)
                for mi in range(8):
                    gidx = 8 * c + mi
                    _w(tensor, s1_xze, gidx - 1)
                    for kj in range(8):
                        ii = tensor.matmul(out=ps_xz[:, mi % 2, 0:LC1],
                                           lhsT=winT[:, (kj * 8 + mi) * 128:(kj * 8 + mi + 1) * 128],
                                           rhs=xnf_s[:, c % 2, kj, :],
                                           start=(kj == 0), stop=(kj == 7))
                    ii.then_inc(s1_xz, 1)

            def p1_wx(c):
                tensor.wait_ge(s1_mul, c + 1)
                _w(tensor, s1_pte, c)
                for dj in range(4):
                    ii = tensor.matmul(out=ps_pt[:, 0:LC1], lhsT=wxT[:, dj * 96:(dj + 1) * 96],
                                       rhs=ub4_s[:, c % 2, dj, :], start=(dj == 0), stop=(dj == 3))
                ii.then_inc(s1_pt, 1)

            for c in range(N1):
                p1_tp_mm(c)
                if c > 0:
                    p1_wx(c - 1)
            p1_wx(N1 - 1)
            tensor.wait_ge(s2_wdma, 96)
            tensor.wait_ge(s1_xze, 8 * N1)
            for dj in range(4):
                _w(tensor, s2_dpu, dj)
                tensor.transpose(out=ps_w2[:, :], in_=wdt_st[:, dj, :], identity=id_s[:]
                                 ).then_inc(s2_dp, 1)
            ps_y4 = [ps_tp.ap()[:, 0, 0:LS], ps_tp.ap()[:, 1, 0:LS],
                     ps_xz.ap()[:, 0, 0:LS], ps_xz.ap()[:, 1, 0:LS]]
            for c in range(N2):
                tensor.wait_ge(s2_sum, 2 * c + 1)
                for dj in range(4):
                    gidx = 4 * c + dj
                    _w(tensor, s2_dpu, 4 + gidx - 1)
                    tensor.matmul(out=ps_dp[:, dj % 2, 0:LS], lhsT=wdtT[:, dj, :],
                                  rhs=dtb16[:, c % 2, :], start=True, stop=True).then_inc(s2_dp, 1)
                tensor.wait_ge(s2_sum, 2 * c + 2)
                _w(tensor, s2_ydu, c)

                def rep(n):
                    ridx = NST * c + n
                    _w(tensor, s2_repe, ridx)
                    tensor.matmul(out=ps_rep[:, 0, :], lhsT=en_bf[:, n * 128:(n + 1) * 128],
                                  rhs=bps_bf[:], start=True, stop=True)
                    tensor.matmul(out=ps_rep[:, 1, :], lhsT=en_bf[:, n * 128:(n + 1) * 128],
                                  rhs=cps_bf[:], start=True, stop=True).then_inc(s2_rep, 1)
                rep(0)
                rep(1)
                for n in range(NST):
                    if n + 2 < NST:
                        rep(n + 2)
                    for dj in range(4):
                        tensor.wait_ge(s2_hc, 64 * c + n * 4 + dj + 1)
                        ii = tensor.matmul(out=ps_y4[dj], lhsT=id_bf[:],
                                           rhs=hc_sb[:, n % 2, dj, :],
                                           start=(n == 0), stop=(n == NST - 1))
                    ii.then_inc(s2_hcu, 1)
                tensor.nop().then_inc(s2_yd, 1)
            tensor.wait_ge(s2_ydu, N2)
            tensor.wait_ge(s3_w, 32)
            for m2 in range(2):
                for kj in range(16):
                    idx = m2 * 16 + kj
                    _w(tensor, s3_ev, idx - 1)
                    tensor.transpose(out=ps_tp[:, idx % 2, 0:128],
                                     in_=xst[:, m2 * 2048 + kj * 128: m2 * 2048 + (kj + 1) * 128],
                                     identity=id_s[:]).then_inc(s3_tp, 1)
            tensor.wait_ge(s3_w, 64)
            for m2 in range(2):
                for kj in range(8):
                    idx = 32 + m2 * 8 + kj
                    _w(tensor, s3_ev, idx - 1)
                    tensor.transpose(out=ps_tp[:, idx % 2, 0:128],
                                     in_=xst[:, m2 * 1024 + kj * 128: m2 * 1024 + (kj + 1) * 128],
                                     identity=id_s[:]).then_inc(s3_tp, 1)
            tensor.wait_ge(s3_ev, 48)
            for c in range(N3):
                tensor.wait_ge(s3_ld[c % 2], 32 * (c // 2 + 1))
                tensor.wait_ge(s3_lds[c % 2], 16 * (c // 2 + 1))
                for fi in range(2):
                    gidx = 2 * c + fi
                    _w(tensor, s3_sum, gidx - 1)
                    _w(tensor, s3_ff, gidx - 1)
                    for kj in range(16):
                        tensor.matmul(out=ps_xz[:, fi % 2, 0:LC3],
                                      lhsT=woT[:, (kj * 2 + fi) * 128:(kj * 2 + fi + 1) * 128],
                                      rhs=y2c_s[:, c % 2, kj, :],
                                      start=(kj == 0), stop=(kj == 15))
                    for kj in range(8):
                        ii = tensor.matmul(out=ps_dp[:, fi % 2, 0:LC3],
                                           lhsT=cbT[:, (kj * 2 + fi) * 128:(kj * 2 + fi + 1) * 128],
                                           rhs=xnc_s[:, c % 2, kj, :],
                                           start=(kj == 0), stop=(kj == 7))
                    ii.then_inc(s3_mm, 1)
                tensor.wait_ge(s3_sum, 2 * (c + 1))
                for i in range(2):
                    tidx = 2 * c + i
                    _w(tensor, s3_add, tidx - 1)
                    for fi in range(2):
                        ii = tensor.transpose(out=ps_tp[:, i % 2, fi * 128:(fi + 1) * 128],
                                              in_=af_s[:, fi, i * 128:(i + 1) * 128],
                                              identity=id_s[:])
                    ii.then_inc(s3_tt, 1)

        @block.scalar
        def _(scalar):
            for idx in range(64):
                mi, kj = idx // 8, idx % 8
                scalar.wait_ge(s1_tp, idx + 1)
                scalar.copy(out=winT[:, (kj * 8 + mi) * 128:(kj * 8 + mi + 1) * 128],
                            in_=ps_tp[:, idx % 2, 0:128]).then_inc(s1_tpe, 1)
            for dj in range(4):
                idx = 64 + dj
                scalar.wait_ge(s1_tp, idx + 1)
                scalar.copy(out=wxT[:, dj * 96:(dj + 1) * 96],
                            in_=ps_tp[:, idx % 2, 0:96]).then_inc(s1_tpe, 1)
            scalar.wait_ge(s_pst, N1)
            scalar.activation(out=rst_s[:], in_=mva_s[:, :, 1:2], func=AF.Ln, bias=EPS)
            scalar.activation(out=rst_s[:], in_=rst_s[:], func=AF.Exp, scale=-0.5).then_inc(s_rsl, 1)
            scalar.wait_ge(s_mrs, 1)

            def p1_affine(c):
                scalar.wait_ge(s1_x[c % 2], 16 * (c // 2 + 1))
                if c >= 2:
                    scalar.wait_ge(s1_tp, 68 + 8 * (c - 1))
                for i in range(NT1):
                    tg = c * NT1 + i
                    ii = scalar.activation(out=xnt_s[:, c % 2, i, :], in_=x_sb[:, c % 2, i, :],
                                           func=AF.Identity, scale=rst_s[:, tg:tg + 1],
                                           bias=mrs_s[:, tg:tg + 1])
                ii.then_inc(s1_aff, 1)

            def p1_block(c):
                base = 68 + 8 * c
                _w(scalar, s1_xz, 8 * (c - 1))
                if c >= 2:
                    scalar.wait_ge(s1_out[c % 2], 48 * (c // 2))
                for j in range(8):
                    scalar.wait_ge(s1_tp, base + j + 1)
                    scalar.copy(out=xnf_s[:, c % 2, j, :], in_=ps_tp[:, j % 2, 0:LC1]).then_inc(s1_tpe, 1)
                _w(scalar, s1_mul, c)
                for mi in range(8):
                    gidx = 8 * c + mi
                    scalar.wait_ge(s1_xz, gidx + 1)
                    scalar.copy(out=xz_s[:, mi, :], in_=ps_xz[:, mi % 2, 0:LC1]).then_inc(s1_xze, 1)
                scalar.wait_ge(s1_conv, c + 1)
                for dj in range(4):
                    scalar.activation(out=sig_s[:, dj, :], in_=conv_c[:, dj, :], func=AF.Sigmoid)
                for dj in range(4):
                    ii = scalar.activation(out=sig_s[:, 4 + dj, :], in_=xz_s[:, 4 + dj, :],
                                           func=AF.Sigmoid)
                ii.then_inc(s1_sig, 1)
            def p1_pt(c):
                scalar.wait_ge(s1_pt, c + 1)
                if c >= 2:
                    scalar.wait_ge(s1_ptd[c % 2], 16 * (c // 2))
                scalar.copy(out=pt_s[:, c % 2, :], in_=ps_pt[:, 0:LC1]).then_inc(s1_pte, 1)

            for c in range(N1):
                p1_affine(c)
                if c > 0:
                    p1_block(c - 1)
                if c > 1:
                    p1_pt(c - 2)
            p1_block(N1 - 1)
            p1_pt(N1 - 2)
            p1_pt(N1 - 1)
            scalar.wait_ge(s2_wdma, 96)
            scalar.activation(out=a_sb.ap().rearrange("p a b -> p (a b)"),
                              in_=a_sb.ap().rearrange("p a b -> p (a b)"),
                              func=AF.Exp).then_inc(s2_ax, 1)
            for dj in range(4):
                scalar.wait_ge(s2_dp, dj + 1)
                scalar.copy(out=wdtT[:, dj, :], in_=ps_w2[:, :]).then_inc(s2_dpu, 1)
            scalar.wait_ge(s1_out[0], 48 * (N1 // 2))
            scalar.wait_ge(s1_out[1], 48 * (N1 // 2))
            for c in range(N2):
                scalar.wait_ge(s2_nz, c + 1)
                for dj in range(4):
                    scalar.activation(out=sp_sb[:, dj, :], in_=nz_sb[:, dj, :], func=AF.Exp)
                for dj in range(4):
                    ii = scalar.activation(out=sp_sb[:, dj, :], in_=sp_sb[:, dj, :], func=AF.Ln,
                                           bias=1.0)
                ii.then_inc(s2_sp, 1)
                scalar.wait_ge(s2_del, c + 1)
                for n in range(NST):
                    ridx = NST * c + n
                    scalar.wait_ge(s2_rep, ridx + 1)
                    _w(scalar, s2_rbu, ridx - 1)
                    scalar.copy(out=rep_bf[:, n % 2, 0, :], in_=ps_rep[:, 0, :])
                    scalar.copy(out=rep_bf[:, n % 2, 1, :], in_=ps_rep[:, 1, :]).then_inc(s2_repe, 1)
                    for dj in range(4):
                        idx = (c * NST + n) * 4 + dj
                        _w(scalar, s2_dau, idx - 1)
                        scalar.activation(out=da_sb[:, idx % 2, :], in_=del_sb[:, dj, :],
                                          func=AF.Exp, scale=a_sb[:, dj, n:n + 1]).then_inc(s2_da, 1)
            for idx in range(32):
                m2, kj = idx // 16, idx % 16
                scalar.wait_ge(s3_tp, idx + 1)
                scalar.copy(out=woT[:, (kj * 2 + m2) * 128:(kj * 2 + m2 + 1) * 128],
                            in_=ps_tp[:, idx % 2, 0:128]).then_inc(s3_ev, 1)
            for idx in range(32, 48):
                m2, kj = (idx - 32) // 8, (idx - 32) % 8
                scalar.wait_ge(s3_tp, idx + 1)
                scalar.copy(out=cbT[:, (kj * 2 + m2) * 128:(kj * 2 + m2 + 1) * 128],
                            in_=ps_tp[:, idx % 2, 0:128]).then_inc(s3_ev, 1)
            scalar.wait_ge(s2_wdma, 96)
            scalar.mul(out=scb_s[:], in_=scb_s[:], mul=1.0 / 127.0)
            for c in range(N3):
                for fi in range(2):
                    gidx = 2 * c + fi
                    scalar.wait_ge(s3_mm, gidx + 1)
                    _w(scalar, s3_sum, gidx - 1)
                    scalar.activation(out=ffn_s[:, fi % 2, :], in_=ps_dp[:, fi % 2, 0:LC3],
                                      func=AF.Identity, scale=scb_s[:, fi:fi + 1]).then_inc(s3_ff, 1)

        @block.vector
        def _(vector):
            vector.wait_ge(s_io, 1)
            vector.tensor_scalar(out=id_s[:], in0=io_s[:], scalar1=0.0, op0=ALU.is_equal,
                                 scalar2=None)
            for n in range(NST):
                ii = vector.tensor_scalar(out=en_bf[:, n * 128:(n + 1) * 128], in0=io16[:],
                                          scalar1=float(n), op0=ALU.is_equal, scalar2=None)
            ii.then_inc(s_io, 1)
            vector.tensor_copy(out=id_bf[:], in_=id_s[:])
            vector.memset(cst_s[:, 0:1], EPS)
            vector.memset(cst_s[:, 1:2], 1.0)
            vector.memset(pad_s.ap().rearrange("p a b -> p (a b)"), 0.0)
            vector.memset(state.ap().rearrange("p a b -> p (a b)"), 0.0)
            for c in range(N1):
                vector.wait_ge(s_px[c % 2], 16 * (c // 2 + 1))
                for i in range(NT1):
                    vector.bn_stats(out=bst_s[:, i, 0:6], in_=x_sb[:, c % 2, i, 0:512])
                    vector.bn_stats(out=bst_s[:, i, 6:12], in_=x_sb[:, c % 2, i, 512:1024])
                for i in range(NT1):
                    tg = c * NT1 + i
                    ii = vector.bn_aggr(out=mva_s[:, tg, :],
                                        in_=bst_s.ap()[:, i].rearrange("p (a b) -> p a b", b=3))
                ii.then_inc(s_pst, 1)
            vector.wait_ge(s_rsl, 1)
            vector.scalar_tensor_tensor(out=mrs_s[:], in0=mva_s[:, :, 0:1].rearrange("p a b -> p (a b)"),
                                        scalar=-1.0, in1=rst_s[:],
                                        op0=ALU.mult, op1=ALU.mult).then_inc(s_mrs, 1)
            for c in range(N1):
                vector.wait_ge(s1_xze, 8 * c + 4)
                for dj in range(4):
                    vector.tensor_copy(out=pad_s[:, dj, 3:3 + LC1], in_=xz_s[:, dj, :])
                for dj in range(4):
                    vector.tensor_scalar(out=cva_s[:], in0=pad_s[:, dj, 0:LC1],
                                         scalar1=cw_s[:, 4 * dj:4 * dj + 1], op0=ALU.mult,
                                         scalar2=cb_s[:, dj:dj + 1], op1=ALU.add)
                    vector.scalar_tensor_tensor(out=cvb_s[:], in0=pad_s[:, dj, 1:1 + LC1],
                                                scalar=cw_s[:, 4 * dj + 1:4 * dj + 2],
                                                in1=cva_s[:], op0=ALU.mult, op1=ALU.add)
                    vector.scalar_tensor_tensor(out=cva_s[:], in0=pad_s[:, dj, 2:2 + LC1],
                                                scalar=cw_s[:, 4 * dj + 2:4 * dj + 3],
                                                in1=cvb_s[:], op0=ALU.mult, op1=ALU.add)
                    vector.scalar_tensor_tensor(out=conv_c[:, dj, :], in0=pad_s[:, dj, 3:3 + LC1],
                                                scalar=cw_s[:, 4 * dj + 3:4 * dj + 4],
                                                in1=cva_s[:], op0=ALU.mult, op1=ALU.add)
                    ii = vector.tensor_copy(out=pad_s[:, dj, 0:3], in_=pad_s[:, dj, LC1:LC1 + 3])
                ii.then_inc(s1_conv, 1)
                vector.wait_ge(s1_sig, c + 1)
                if c >= 2:
                    vector.wait_ge(s1_out[c % 2], 48 * (c // 2))
                for dj in range(4):
                    vector.tensor_mul(out=u1_s[:, c % 2, dj, :], in0=conv_c[:, dj, :], in1=sig_s[:, dj, :])
                _w(vector, s1_pt, c - 1)
                for dj in range(4):
                    vector.tensor_copy(out=ub4_s[:, c % 2, dj, :], in_=u1_s[:, c % 2, dj, :])
                if c >= 1:
                    vector.wait_ge(s1_out[(c - 1) % 2], 48 * ((c - 1) // 2 + 1))
                for dj in range(4):
                    ii = vector.tensor_mul(out=g1_s[:, dj, :], in0=xz_s[:, 4 + dj, :],
                                           in1=sig_s[:, 4 + dj, :])
                ii.then_inc(s1_mul, 1)
            vector.wait_ge(s2_ax, 1)
            vector.tensor_scalar(out=a_sb.ap().rearrange("p a b -> p (a b)"),
                                 in0=a_sb.ap().rearrange("p a b -> p (a b)"),
                                 scalar1=-1.0, op0=ALU.mult, scalar2=None)
            for c in range(N2):
                vector.wait_ge(s2_ld[c % 2], 80 * (c // 2 + 1))
                _w(vector, s2_dp, 4 + 4 * c)
                vector.tensor_copy(out=dtb16[:, c % 2, :], in_=dtl_sb[:, c % 2, :]).then_inc(s2_sum, 1)
                vector.tensor_copy(out=bps_bf[:], in_=bps_sb[:, c % 2, :])
                vector.tensor_copy(out=cps_bf[:], in_=cps_sb[:, c % 2, :]).then_inc(s2_sum, 1)
                for dj in range(4):
                    vector.wait_ge(s2_dp, 4 + 4 * c + dj + 1)
                    vector.tensor_scalar(out=z_sb[:, dj, :], in0=ps_dp[:, dj % 2, 0:LS],
                                         scalar1=dtb_s[:, dj:dj + 1], op0=ALU.add,
                                         scalar2=None).then_inc(s2_dpu, 1)
                for dj in range(4):
                    ii = vector.scalar_tensor_tensor(out=nz_sb[:, dj, :], in0=z_sb[:, dj, :],
                                                     scalar=-1.0, in1=z_sb[:, dj, :],
                                                     op0=ALU.mult, op1=ALU.min)
                ii.then_inc(s2_nz, 1)
                vector.wait_ge(s2_sp, c + 1)
                for dj in range(4):
                    ii = vector.scalar_tensor_tensor(out=del_sb[:, dj, :], in0=z_sb[:, dj, :],
                                                     scalar=0.0, in1=sp_sb[:, dj, :],
                                                     op0=ALU.max, op1=ALU.add)
                ii.then_inc(s2_del, 1)
                for dj in range(4):
                    vector.tensor_mul(out=du_sb[:, dj, :], in0=del_sb[:, dj, :], in1=u_sb[:, c % 2, dj, :])
                for dj in range(4):
                    vector.tensor_copy(out=du_bf[:, dj, :], in_=du_sb[:, dj, :])
                for n in range(NST):
                    vector.wait_ge(s2_repe, NST * c + n + 1)
                    for dj in range(4):
                        idx = (c * NST + n) * 4 + dj
                        vector.wait_ge(s2_da, idx + 1)
                        vector.tensor_mul(out=dbu_sb[:], in0=du_bf[:, dj, :], in1=rep_bf[:, n % 2, 0, :])
                        vector.tensor_tensor_scan(out=h_sb[:], data0=da_sb[:, idx % 2, :], data1=dbu_sb[:],
                                                  initial=state[:, dj, n:n + 1],
                                                  op0=ALU.mult, op1=ALU.add).then_inc(s2_dau, 1)
                        vector.tensor_copy(out=state[:, dj, n:n + 1], in_=h_sb[:, LS - 1:LS])
                        if n >= 2:
                            vector.wait_ge(s2_hcu, 16 * c + n - 1)
                        vector.tensor_mul(out=hc_sb[:, n % 2, dj, :], in0=h_sb[:],
                                           in1=rep_bf[:, n % 2, 1, :]).then_inc(s2_hc, 1)
                        if dj == 3:
                            vector.tensor_copy(out=dbu_sb[:, 0:1],
                                               in_=dbu_sb[:, 0:1]).then_inc(s2_rbu, 1)
                if c >= 2:
                    vector.wait_ge(s2_st[c % 2], 16 * (c // 2))
                vector.wait_ge(s2_yd, c + 1)
                ps_y4v = [ps_tp.ap()[:, 0, 0:LS], ps_tp.ap()[:, 1, 0:LS],
                          ps_xz.ap()[:, 0, 0:LS], ps_xz.ap()[:, 1, 0:LS]]
                for dj in range(4):
                    vector.scalar_tensor_tensor(out=ydu_sb[:], in0=u_sb[:, c % 2, dj, :],
                                                scalar=dsk_s[:, dj:dj + 1],
                                                in1=ps_y4v[dj],
                                                op0=ALU.mult, op1=ALU.add)
                    ii = vector.tensor_mul(out=y2_sb[:, c % 2, dj, :], in0=ydu_sb[:],
                                           in1=g2_sb[:, c % 2, dj, :])
                ii.then_inc(s2_y2, 1)
                vector.nop().then_inc(s2_ydu, 1)
                vector.tensor_copy(out=dbu_sb[:, 0:1], in_=dbu_sb[:, 0:1]).then_inc(s2_used, 1)
            for c in range(N3):
                for fi in range(2):
                    gidx = 2 * c + fi
                    vector.wait_ge(s3_ff, gidx + 1)
                    vector.scalar_tensor_tensor(out=af_s[:, fi, :], in0=ps_xz[:, fi % 2, 0:LC3],
                                                scalar=bias_s[:, fi:fi + 1], in1=ffn_s[:, fi % 2, :],
                                                op0=ALU.add, op1=ALU.add).then_inc(s3_sum, 1)
                _w(vector, s3_st, 16 * c)
                vector.wait_ge(s3_xc[c % 2], 16 * (c // 2 + 1))
                for i in range(2):
                    tidx = 2 * c + i
                    vector.wait_ge(s3_tt, tidx + 1)
                    vector.tensor_add(out=out_s[:, i, :],
                                      in0=ps_tp[:, i % 2, 0:FS], in1=xcol_s[:, c % 2, i, :]
                                      ).then_inc(s3_add, 1)
    return nc


def kernel(L0_attn_hidden_states, L0_attn_W_in, L0_attn_conv_w, L0_attn_conv_b,
           L0_attn_W_x, L0_attn_W_dt, L0_attn_dt_bias, L0_attn_A_log,
           L0_attn_D_skip, L0_attn_W_out, L0_ffn_CB, L0_ffn_SCB, L0_ffn_bias):
    x = np.ascontiguousarray(np.asarray(L0_attn_hidden_states, dtype=np.float32))
    W_in = np.asarray(L0_attn_W_in, dtype=np.float32)
    conv_w = np.asarray(L0_attn_conv_w, dtype=np.float32)
    conv_b = np.asarray(L0_attn_conv_b, dtype=np.float32)
    W_x = np.asarray(L0_attn_W_x, dtype=np.float32)
    W_dt = np.asarray(L0_attn_W_dt, dtype=np.float32)
    dt_bias = np.asarray(L0_attn_dt_bias, dtype=np.float32)
    A_log = np.asarray(L0_attn_A_log, dtype=np.float32)
    D_skip = np.asarray(L0_attn_D_skip, dtype=np.float32)
    W_out = np.asarray(L0_attn_W_out, dtype=np.float32)
    CB = np.asarray(L0_ffn_CB, dtype=np.float32)
    SCB = np.asarray(L0_ffn_SCB, dtype=np.float32)
    fbias = np.asarray(L0_ffn_bias, dtype=np.float32)
    C = np.ascontiguousarray

    nc = build_fused()
    maps = []
    for cid in range(NC):
        b, s = cid // 4, cid % 4
        sl = slice(ES * s, ES * (s + 1))
        fs = slice(FS * s, FS * (s + 1))
        maps.append(dict(
            x=C(x[b]),
            w_in=C(np.concatenate([W_in[ES * s:ES * (s + 1)], W_in[E + ES * s:E + ES * (s + 1)]], axis=0)),
            conv_w=C(conv_w[sl]),
            conv_b=C(conv_b[sl].reshape(4, 128).T),
            w_x=C(W_x[:, sl]),
            w_dt=C(W_dt[sl]),
            dt_bias=C(dt_bias[sl].reshape(4, 128).T),
            a_log=C(A_log[sl]),
            d_skip=C(D_skip[sl].reshape(4, 128).T),
            w_out_s=C(W_out[fs]),
            cb_sl=C(CB[fs]),
            scb_v=C(SCB[fs].reshape(2, 128).T),
            bias_v=C(fbias[fs].reshape(2, 128).T),
            xcol=C(x[b][:, fs]),
        ))
    res = run_bass_kernel_spmd(nc, maps, core_ids=list(range(NC))).results

    out = np.empty((B, L, D), dtype=np.float32)
    for cid in range(NC):
        b, s = cid // 4, cid % 4
        out[b, :, FS * s:FS * (s + 1)] = res[cid]["out"]
    return out


# version 13
# speedup vs baseline: 1.3514x; 1.0179x over previous
import contextlib
import numpy as np
import ml_dtypes
import concourse.bass as bass
import concourse.mybir as mybir
from concourse.bass_utils import run_bass_kernel_spmd

F32 = mybir.dt.float32
BF16 = mybir.dt.bfloat16
AF = mybir.ActivationFunctionType
ALU = mybir.AluOpType
AX = mybir.AxisListType

B, L, D, E, NST, R = 2, 8192, 1024, 2048, 16, 64
ES = 512
NC = 8
EPS = 1e-5

LC1 = 256
N1 = L // LC1
NT1 = LC1 // 128
LS = 512
N2 = L // LS
LC3 = 256
N3 = L // LC3
FS = 256


def _w(eng, sem, v):
    if v > 0:
        eng.wait_ge(sem, v)


def build_fused():
    nc = bass.Bass("TRN2", target_bir_lowering=False, detect_race_conditions=False)
    x = nc.dram_tensor("x", [L, D], F32, kind="ExternalInput").ap()
    w_in = nc.dram_tensor("w_in", [1024, D], F32, kind="ExternalInput").ap()
    conv_w = nc.dram_tensor("conv_w", [ES, 4], F32, kind="ExternalInput").ap()
    conv_b = nc.dram_tensor("conv_b", [128, 4], F32, kind="ExternalInput").ap()
    w_x = nc.dram_tensor("w_x", [96, ES], F32, kind="ExternalInput").ap()
    w_dt = nc.dram_tensor("w_dt", [ES, R], F32, kind="ExternalInput").ap()
    dt_bias = nc.dram_tensor("dt_bias", [128, 4], F32, kind="ExternalInput").ap()
    a_log = nc.dram_tensor("a_log", [ES, NST], F32, kind="ExternalInput").ap()
    d_skip = nc.dram_tensor("d_skip", [128, 4], F32, kind="ExternalInput").ap()
    w_out_s = nc.dram_tensor("w_out_s", [FS, E], F32, kind="ExternalInput").ap()
    cb_sl = nc.dram_tensor("cb_sl", [FS, D], F32, kind="ExternalInput").ap()
    scb_v = nc.dram_tensor("scb_v", [128, 2], F32, kind="ExternalInput").ap()
    bias_v = nc.dram_tensor("bias_v", [128, 2], F32, kind="ExternalInput").ap()
    xcol = nc.dram_tensor("xcol", [L, FS], F32, kind="ExternalInput").ap()
    out = nc.dram_tensor("out", [L, FS], F32, kind="ExternalOutput").ap()

    u_sc = nc.dram_tensor("u_sc", [ES, L], F32).ap()
    g2_sc = nc.dram_tensor("g2_sc", [ES, L], F32).ap()
    xn_sc = nc.dram_tensor("xn_sc", [D, L], BF16).ap()
    pt_ina = nc.dram_tensor("pt_ina", [96, L // 2], F32).ap()
    pt_inb = nc.dram_tensor("pt_inb", [96, L // 2], F32).ap()
    pt_suma = nc.dram_tensor("pt_suma", [96, L // 2], F32).ap()
    pt_sumb = nc.dram_tensor("pt_sumb", [96, L // 2], F32).ap()
    y2_sla = nc.dram_tensor("y2_sla", [ES, L // 2], BF16).ap()
    y2_slb = nc.dram_tensor("y2_slb", [ES, L // 2], BF16).ap()
    y2_fa = nc.dram_tensor("y2_fa", [4 * ES, L // 2], BF16).ap()
    y2_fb = nc.dram_tensor("y2_fb", [4 * ES, L // 2], BF16).ap()

    u_sc_r = u_sc.rearrange("(dj p) l -> p dj l", p=128)
    g2_sc_r = g2_sc.rearrange("(dj p) l -> p dj l", p=128)
    xn_sc_r = xn_sc.rearrange("(j p) l -> p j l", p=128)
    y2_sla_r = y2_sla.rearrange("(dj p) l -> p dj l", p=128)
    y2_slb_r = y2_slb.rearrange("(dj p) l -> p dj l", p=128)
    y2_fa_r = y2_fa.rearrange("(kj p) l -> p kj l", p=128)
    y2_fb_r = y2_fb.rearrange("(kj p) l -> p kj l", p=128)

    RG = [[0, 1, 2, 3], [4, 5, 6, 7]]

    with contextlib.ExitStack() as _es:
        sb = lambda name, shape, dt=F32: _es.enter_context(nc.sbuf_tensor(name, shape, dt))
        ps = lambda name, shape: _es.enter_context(nc.psum_tensor(name, shape, F32))
        sem = lambda name: _es.enter_context(nc.semaphore(name))

        io_s = sb("io_s", [128, 128])
        id_s = sb("id_s", [128, 128])
        io16 = sb("io16", [16, 128])
        cst_s = sb("cst_s", [128, 2])

        winT = sb("winT", [128, 8 * 8 * 128], BF16)
        wxT = sb("wxT", [128, 4 * 96], BF16)
        cw_s = sb("cw_s", [128, 16])
        cb_s = sb("cb_s", [128, 4])
        x_sb = sb("x_sb", [128, 2, NT1, 1024])
        bst_s = sb("bst_s", [128, 2, 12])
        mva_s = sb("mva_s", [128, 64, 2])
        rst_s = sb("rst_s", [128, 64])
        xnt_s = sb("xnt_s", [128, 2, NT1, 1024])
        mrs_s = sb("mrs_s", [128, 64])
        xnf_s = sb("xnf_s", [128, 2, 8, LC1], BF16)
        xz_s = sb("xz_s", [128, 8, LC1])
        pad_s = sb("pad_s", [128, 4, LC1 + 3])
        cva_s = sb("cva_s", [128, LC1])
        cvb_s = sb("cvb_s", [128, LC1])
        conv_c = sb("conv_c", [128, 4, LC1])
        sig_s = sb("sig_s", [128, 8, LC1])
        u1_s = sb("u1_s", [128, 2, 4, LC1])
        ub4_s = sb("ub4_s", [128, 2, 4, LC1], BF16)
        g1_s = sb("g1_s", [128, 4, LC1])
        pt_s = sb("pt_s", [96, 2, LC1])
        wst2_s = sb("wst2_s", [96, 512])

        wdt_st = sb("wdt_st", [128, 4, R])
        wdtT = sb("wdtT", [64, 4, 128], BF16)
        a_sb = sb("a_sb", [128, 4, NST])
        dtb_s = sb("dtb_s", [128, 4])
        dsk_s = sb("dsk_s", [128, 4])
        state = sb("state", [128, 4, NST])
        u_sb = x_sb.ap().rearrange("p a b c -> p (a b c)").rearrange(
            "p (x y z) -> p x y z", x=2, y=4, z=LS)
        g2_sb = xnt_s.ap().rearrange("p a b c -> p (a b c)").rearrange(
            "p (x y z) -> p x y z", x=2, y=4, z=LS)
        z_sb = sig_s.ap().rearrange("p a b -> p (a b)").rearrange(
            "p (y z) -> p y z", y=4, z=LS)
        nz_sb = xz_s.ap().rearrange("p a b -> p (a b)").rearrange(
            "p (y z) -> p y z", y=4, z=LS)
        sp_sb = u1_s.ap().rearrange("p a b c -> p (a b c)").rearrange(
            "p (y z) -> p y z", y=4, z=LS)
        dtl_sb = sb("dtl_sb", [64, 2, LS])
        bps_sb = sb("bps_sb", [16, 2, LS])
        cps_sb = sb("cps_sb", [16, 2, LS])
        dtb16 = sb("dtb16", [64, 2, LS], BF16)
        del_sb = sb("del_sb", [128, 4, LS])
        du_sb = sb("du_sb", [128, 4, LS])
        da_sb = sb("da_sb", [128, 2, LS])
        dbu_sb = sb("dbu_sb", [128, LS], BF16)
        h_sb = sb("h_sb", [128, LS], BF16)
        hc_sb = sb("hc_sb", [128, 2, 4, LS], BF16)
        du_bf = sb("du_bf", [128, 4, LS], BF16)
        rep_bf = sb("rep_bf", [128, 2, 2, LS], BF16)
        en_bf = sb("en_bf", [16, 16 * 128], BF16)
        bps_bf = sb("bps_bf", [16, LS], BF16)
        cps_bf = sb("cps_bf", [16, LS], BF16)
        id_bf = sb("id_bf", [128, 128], BF16)
        ydu_sb = sb("ydu_sb", [128, LS])
        y2_sb = sb("y2_sb", [128, 2, 4, LS], BF16)

        woT = winT.ap()[:, 0:16 * 2 * 128]
        cbT = winT.ap()[:, 16 * 2 * 128:16 * 2 * 128 + 8 * 2 * 128]
        scb_s = sb("scb_s", [128, 2])
        bias_s = sb("bias_s", [128, 2])
        y2c_s = sb("y2c_s", [128, 2, 16, LC3], BF16)
        xnc_s = sb("xnc_s", [128, 2, 8, LC3], BF16)
        xcol_s = sb("xcol_s", [128, 2, 2, FS])
        ffn_s = pad_s.ap().rearrange("p a b -> p (a b)")[:, 0:2 * LC3].rearrange(
            "p (y z) -> p y z", y=2, z=LC3)
        af_s = conv_c.ap().rearrange("p a b -> p (a b)")[:, 0:2 * LC3].rearrange(
            "p (y z) -> p y z", y=2, z=LC3)
        out_s = sb("out_s", [128, 2, FS])

        ps_tp = ps("ps_tp", [128, 2, 512])
        ps_xz = ps("ps_xz", [128, 2, 512])
        ps_dp = ps("ps_dp", [128, 2, 512])
        ps_pt = ps_dp.ap()[0:96, 0, :]
        ps_rep = ps("ps_rep", [128, 2, LS])
        ps_w2 = ps_xz.ap()[0:64, 0, 0:128]

        s_io = sem("s_io")
        s_wdma = sem("s_wdma")
        s_px = [sem("s_px0"), sem("s_px1")]
        s_pst = sem("s_pst")
        s_rsl = sem("s_rsl")
        s_mrs = sem("s_mrs")
        s3_lds = [sem("s3_lds0"), sem("s3_lds1")]
        s1_x = [sem("s1_x0"), sem("s1_x1")]
        s1_aff = sem("s1_aff")
        s1_tp = sem("s1_tp")
        s1_tpe = sem("s1_tpe")
        s1_xz = sem("s1_xz")
        s1_xze = sem("s1_xze")
        s1_conv = sem("s1_conv")
        s1_sig = sem("s1_sig")
        s1_mul = sem("s1_mul")
        s1_pt = sem("s1_pt")
        s1_pte = sem("s1_pte")
        s1_out = [sem("s1_out0"), sem("s1_out1")]
        s1_ptd = [sem("s1_ptd0"), sem("s1_ptd1")]
        s_cc = sem("s_cc")
        s2_wdma = sem("s2_wdma")
        s2_ax = sem("s2_ax")
        s2_ld = [sem("s2_ld0"), sem("s2_ld1")]
        s2_used = sem("s2_used")
        s2_sum = sem("s2_sum")
        s2_dp = sem("s2_dp")
        s2_dpu = sem("s2_dpu")
        s2_nz = sem("s2_nz")
        s2_sp = sem("s2_sp")
        s2_del = sem("s2_del")
        s2_rep = sem("s2_rep")
        s2_repe = sem("s2_repe")
        s2_rbu = sem("s2_rbu")
        s2_hc = sem("s2_hc")
        s2_hcu = sem("s2_hcu")
        s2_yd = sem("s2_yd")
        s2_ydu = sem("s2_ydu")
        s2_da = sem("s2_da")
        s2_dau = sem("s2_dau")
        s2_y2 = sem("s2_y2")
        s2_st = [sem("s2_st0"), sem("s2_st1")]
        s3_w = sem("s3_w")
        s3_ld = [sem("s3_ld0"), sem("s3_ld1")]
        s3_xc = [sem("s3_xc0"), sem("s3_xc1")]
        s3_tp = sem("s3_tp")
        s3_ev = sem("s3_ev")
        s3_mm = sem("s3_mm")
        s3_ff = sem("s3_ff")
        s3_sum = sem("s3_sum")
        s3_tt = sem("s3_tt")
        s3_add = sem("s3_add")
        s3_st = sem("s3_st")
        block = _es.enter_context(nc.Block())

        nc.const_aps.register(F32, EPS, cst_s[:, 0:1])
        nc.const_aps.register(F32, 1.0, cst_s[:, 1:2])

        @block.sync
        def _(sync):
            sync.dma_start(out=x_sb.ap().rearrange("p b t f -> p (b t) f"),
                           in_=w_in[0:512, :].rearrange("(mi p) k -> p mi k", p=128)).then_inc(s_wdma, 16)
            sync.dma_start(out=wst2_s[:], in_=w_x).then_inc(s_wdma, 16)
            sync.dma_start(out=cw_s.ap().rearrange("p (dj k) -> p dj k", k=4),
                           in_=conv_w.rearrange("(dj p) k -> p dj k", p=128)).then_inc(s_wdma, 16)
            sync.dma_start(out=cb_s[:], in_=conv_b).then_inc(s_wdma, 16)
            sync.wait_ge(s1_tp, 32)
            sync.dma_start(out=x_sb.ap().rearrange("p b t f -> p (b t) f"),
                           in_=w_in[512:1024, :].rearrange("(mi p) k -> p mi k", p=128)).then_inc(s_wdma, 16)
            sync.dma_start(out=wdt_st[:], in_=w_dt.rearrange("(dj p) r -> p dj r", p=128)).then_inc(s2_wdma, 16)
            sync.dma_start(out=a_sb[:], in_=a_log.rearrange("(dj p) n -> p dj n", p=128)).then_inc(s2_wdma, 16)
            sync.dma_start(out=dtb_s[:], in_=dt_bias).then_inc(s2_wdma, 16)
            sync.dma_start(out=dsk_s[:], in_=d_skip).then_inc(s2_wdma, 16)
            sync.dma_start(out=scb_s[:], in_=scb_v).then_inc(s2_wdma, 16)
            sync.dma_start(out=bias_s[:], in_=bias_v).then_inc(s2_wdma, 16)
            for c in range(N1):
                if c < 2:
                    sync.wait_ge(s1_tp, 68)
                else:
                    _w(sync, s_pst, c - 1)
                sync.dma_start(out=x_sb[:, c % 2],
                               in_=x[c * LC1:(c + 1) * LC1, :].rearrange("(i p) f -> p i f", p=128)
                               ).then_inc(s_px[c % 2], 16)
            for c in range(N1):
                if c < 2:
                    sync.wait_ge(s_pst, N1)
                else:
                    sync.wait_ge(s1_aff, c - 1)
                sync.dma_start(out=x_sb[:, c % 2],
                               in_=x[c * LC1:(c + 1) * LC1, :].rearrange("(i p) f -> p i f", p=128)
                               ).then_inc(s1_x[c % 2], 16)
            sync.wait_ge(s1_aff, N1)
            sync.wait_ge(s1_tp, 68 + 8 * N1)
            sync.wait_ge(s_cc, 1)
            for c in range(N2):
                if c == N2 // 2:
                    sync.wait_ge(s_cc, 2)
                _w(sync, s2_used, c - 1)
                sl = slice(c * LS, (c + 1) * LS)
                pts_h = pt_suma if c < N2 // 2 else pt_sumb
                hsl = slice((c % (N2 // 2)) * LS, (c % (N2 // 2) + 1) * LS)
                sync.dma_start(out=u_sb[:, c % 2], in_=u_sc_r[:, :, sl]).then_inc(s2_ld[c % 2], 16)
                sync.dma_start(out=g2_sb[:, c % 2], in_=g2_sc_r[:, :, sl]).then_inc(s2_ld[c % 2], 16)
                sync.dma_start(out=dtl_sb[:, c % 2, :], in_=pts_h[0:64, hsl]).then_inc(s2_ld[c % 2], 16)
                sync.dma_start(out=bps_sb[:, c % 2, :], in_=pts_h[64:80, hsl]).then_inc(s2_ld[c % 2], 16)
                sync.dma_start(out=cps_sb[:, c % 2, :], in_=pts_h[80:96, hsl]).then_inc(s2_ld[c % 2], 16)
            sync.wait_ge(s2_used, N2)
            xstf = x_sb.ap().rearrange("p b t f -> p (b t f)")
            for m2 in range(2):
                sync.dma_start(out=xstf[:, m2 * 2048:(m2 + 1) * 2048],
                               in_=w_out_s[m2 * 128:(m2 + 1) * 128, :]).then_inc(s3_w, 16)
            sync.wait_ge(s3_tp, 32)
            for m2 in range(2):
                sync.dma_start(out=xstf[:, m2 * 1024:(m2 + 1) * 1024],
                               in_=cb_sl[m2 * 128:(m2 + 1) * 128, :]).then_inc(s3_w, 16)
            sync.wait_ge(s_cc, 3)
            for c in range(N3):
                if c == N3 // 2:
                    sync.wait_ge(s_cc, 4)
                _w(sync, s3_mm, 2 * (c - 1))
                if c < N3 // 2:
                    sync.dma_start(out=y2c_s[:, c % 2, 0:8],
                                   in_=y2_fa_r[:, 0:8, c * LC3:(c + 1) * LC3]).then_inc(s3_ld[c % 2], 16)
                else:
                    cc = c - N3 // 2
                    sync.dma_start(out=y2c_s[:, c % 2, 0:8],
                                   in_=y2_fb_r[:, 0:8, cc * LC3:(cc + 1) * LC3]).then_inc(s3_ld[c % 2], 16)
                sl = slice(c * LC3, (c + 1) * LC3)
                sync.dma_start(out=xnc_s[:, c % 2], in_=xn_sc_r[:, :, sl]).then_inc(s3_ld[c % 2], 16)

        @block.gpsimd
        def _(gpsimd):
            gpsimd.iota(out=io_s[:], pattern=[[1, 128]], base=0, channel_multiplier=-1,
                        allow_small_or_imprecise_dtypes=True)
            gpsimd.iota(out=io16[:], pattern=[[0, 128]], base=0, channel_multiplier=1,
                        allow_small_or_imprecise_dtypes=True).then_inc(s_io, 1)
            def p1_ptdma(c):
                gpsimd.wait_ge(s1_pte, c + 1)
                pth = pt_ina if c < N1 // 2 else pt_inb
                gpsimd.dma_start(out=pth[:, (c % (N1 // 2)) * LC1:(c % (N1 // 2) + 1) * LC1],
                                 in_=pt_s[:, c % 2]).then_inc(s1_ptd[c % 2], 16)

            for c in range(N1):
                gpsimd.wait_ge(s1_mul, c + 1)
                sl = slice(c * LC1, (c + 1) * LC1)
                gpsimd.dma_start(out=u_sc_r[:, :, sl], in_=u1_s[:, c % 2]).then_inc(s1_out[c % 2], 16)
                gpsimd.dma_start(out=g2_sc_r[:, :, sl], in_=g1_s[:]).then_inc(s1_out[c % 2], 16)
                gpsimd.dma_start(out=xn_sc_r[:, :, sl], in_=xnf_s[:, c % 2]).then_inc(s1_out[c % 2], 16)
                if c >= 2:
                    p1_ptdma(c - 2)
            p1_ptdma(N1 - 2)
            p1_ptdma(N1 - 1)
            gpsimd.wait_ge(s1_out[0], N1 // 4 * 48)
            gpsimd.wait_ge(s1_out[1], N1 // 4 * 48)
            gpsimd.wait_ge(s1_ptd[0], N1 // 4 * 16)
            gpsimd.wait_ge(s1_ptd[1], N1 // 4 * 16)
            gpsimd.collective_compute("AllReduce", ALU.add, replica_groups=RG,
                                      ins=[pt_ina.opt()], outs=[pt_suma.opt()]).then_inc(s_cc, 1)
            gpsimd.wait_ge(s1_out[0], N1 // 2 * 48)
            gpsimd.wait_ge(s1_out[1], N1 // 2 * 48)
            gpsimd.wait_ge(s1_ptd[0], N1 // 2 * 16)
            gpsimd.wait_ge(s1_ptd[1], N1 // 2 * 16)
            gpsimd.collective_compute("AllReduce", ALU.add, replica_groups=RG,
                                      ins=[pt_inb.opt()], outs=[pt_sumb.opt()]).then_inc(s_cc, 1)
            for c in range(N2 // 2):
                gpsimd.wait_ge(s2_y2, c + 1)
                gpsimd.dma_start(out=y2_sla_r[:, :, c * LS:(c + 1) * LS],
                                 in_=y2_sb[:, c % 2]).then_inc(s2_st[c % 2], 16)
            gpsimd.wait_ge(s2_st[0], N2 // 4 * 16)
            gpsimd.wait_ge(s2_st[1], N2 // 4 * 16)
            gpsimd.collective_compute("AllGather", ALU.bypass, replica_groups=RG,
                                      ins=[y2_sla.opt()], outs=[y2_fa.opt()]).then_inc(s_cc, 1)
            for c in range(N2 // 2, N2):
                gpsimd.wait_ge(s2_y2, c + 1)
                cc = c - N2 // 2
                gpsimd.dma_start(out=y2_slb_r[:, :, cc * LS:(cc + 1) * LS],
                                 in_=y2_sb[:, c % 2]).then_inc(s2_st[c % 2], 16)
            gpsimd.wait_ge(s2_st[0], N2 // 2 * 16)
            gpsimd.wait_ge(s2_st[1], N2 // 2 * 16)
            gpsimd.collective_compute("AllGather", ALU.bypass, replica_groups=RG,
                                      ins=[y2_slb.opt()], outs=[y2_fb.opt()]).then_inc(s_cc, 1)
            gpsimd.wait_ge(s_cc, 3)

            def p3_pref(c):
                if c == N3 // 2:
                    gpsimd.wait_ge(s_cc, 4)
                _w(gpsimd, s3_mm, 2 * (c - 1))
                if c < N3 // 2:
                    gpsimd.dma_start(out=y2c_s[:, c % 2, 8:16],
                                     in_=y2_fa_r[:, 8:16, c * LC3:(c + 1) * LC3]).then_inc(s3_lds[c % 2], 16)
                else:
                    cc = c - N3 // 2
                    gpsimd.dma_start(out=y2c_s[:, c % 2, 8:16],
                                     in_=y2_fb_r[:, 8:16, cc * LC3:(cc + 1) * LC3]).then_inc(s3_lds[c % 2], 16)
                sl = slice(c * LC3, (c + 1) * LC3)
                _w(gpsimd, s3_add, 2 * (c - 1))
                gpsimd.dma_start(out=xcol_s[:, c % 2],
                                 in_=xcol[sl, :].rearrange("(i p) f -> p i f", p=128)).then_inc(s3_xc[c % 2], 16)

            for c in range(N3):
                p3_pref(c)
                if c > 0:
                    gpsimd.wait_ge(s3_add, 2 * c)
                    gpsimd.dma_start(out=out[(c - 1) * LC3:c * LC3, :].rearrange("(i p) f -> p i f", p=128),
                                     in_=out_s[:, :, :]).then_inc(s3_st, 16)
            gpsimd.wait_ge(s3_add, 2 * N3)
            gpsimd.dma_start(out=out[(N3 - 1) * LC3:N3 * LC3, :].rearrange("(i p) f -> p i f", p=128),
                             in_=out_s[:, :, :]).then_inc(s3_st, 16)
            gpsimd.wait_ge(s3_st, N3 * 16)

        @block.tensor
        def _(tensor):
            tensor.wait_ge(s_io, 2)
            xst = x_sb.ap().rearrange("p b t f -> p (b t f)")
            tensor.wait_ge(s_wdma, 64)
            for half in range(2):
                if half == 1:
                    tensor.wait_ge(s_wdma, 80)
                for mi4 in range(4):
                    mi = half * 4 + mi4
                    for kj in range(8):
                        idx = mi * 8 + kj
                        _w(tensor, s1_tpe, idx - 1)
                        tensor.transpose(out=ps_tp[:, idx % 2, 0:128],
                                         in_=xst[:, mi4 * 1024 + kj * 128: mi4 * 1024 + (kj + 1) * 128],
                                         identity=id_s[:]).then_inc(s1_tp, 1)
            for dj in range(4):
                idx = 64 + dj
                _w(tensor, s1_tpe, idx - 1)
                tensor.transpose(out=ps_tp[:, idx % 2, 0:96], in_=wst2_s[:, dj * 128:(dj + 1) * 128],
                                 identity=id_s[0:96, 0:96]).then_inc(s1_tp, 1)
            def p1_tp_mm(c):
                base = 68 + 8 * c
                tensor.wait_ge(s1_aff, c + 1)
                for j in range(8):
                    _w(tensor, s1_tpe, base + j - 1)
                    for i in range(NT1):
                        ii = tensor.transpose(out=ps_tp[:, j % 2, i * 128:(i + 1) * 128],
                                              in_=xnt_s[:, c % 2, i, j * 128:(j + 1) * 128],
                                              identity=id_s[:])
                    ii.then_inc(s1_tp, 1)
                tensor.wait_ge(s1_tpe, base + 8)
                for mi in range(8):
                    gidx = 8 * c + mi
                    _w(tensor, s1_xze, gidx - 1)
                    for kj in range(8):
                        ii = tensor.matmul(out=ps_xz[:, mi % 2, 0:LC1],
                                           lhsT=winT[:, (kj * 8 + mi) * 128:(kj * 8 + mi + 1) * 128],
                                           rhs=xnf_s[:, c % 2, kj, :],
                                           start=(kj == 0), stop=(kj == 7))
                    ii.then_inc(s1_xz, 1)

            def p1_wx(c):
                tensor.wait_ge(s1_mul, c + 1)
                _w(tensor, s1_pte, c)
                for dj in range(4):
                    ii = tensor.matmul(out=ps_pt[:, 0:LC1], lhsT=wxT[:, dj * 96:(dj + 1) * 96],
                                       rhs=ub4_s[:, c % 2, dj, :], start=(dj == 0), stop=(dj == 3))
                ii.then_inc(s1_pt, 1)

            for c in range(N1):
                p1_tp_mm(c)
                if c > 0:
                    p1_wx(c - 1)
            p1_wx(N1 - 1)
            tensor.wait_ge(s2_wdma, 96)
            tensor.wait_ge(s1_xze, 8 * N1)
            for dj in range(4):
                _w(tensor, s2_dpu, dj)
                tensor.transpose(out=ps_w2[:, :], in_=wdt_st[:, dj, :], identity=id_s[:]
                                 ).then_inc(s2_dp, 1)
            ps_y4 = [ps_tp.ap()[:, 0, 0:LS], ps_tp.ap()[:, 1, 0:LS],
                     ps_xz.ap()[:, 0, 0:LS], ps_xz.ap()[:, 1, 0:LS]]
            for c in range(N2):
                tensor.wait_ge(s2_sum, 2 * c + 1)
                for dj in range(4):
                    gidx = 4 * c + dj
                    _w(tensor, s2_dpu, 4 + gidx - 1)
                    tensor.matmul(out=ps_dp[:, dj % 2, 0:LS], lhsT=wdtT[:, dj, :],
                                  rhs=dtb16[:, c % 2, :], start=True, stop=True).then_inc(s2_dp, 1)
                tensor.wait_ge(s2_sum, 2 * c + 2)
                _w(tensor, s2_ydu, c)

                def rep(n):
                    ridx = NST * c + n
                    _w(tensor, s2_repe, ridx)
                    tensor.matmul(out=ps_rep[:, 0, :], lhsT=en_bf[:, n * 128:(n + 1) * 128],
                                  rhs=bps_bf[:], start=True, stop=True)
                    tensor.matmul(out=ps_rep[:, 1, :], lhsT=en_bf[:, n * 128:(n + 1) * 128],
                                  rhs=cps_bf[:], start=True, stop=True).then_inc(s2_rep, 1)
                rep(0)
                rep(1)
                for n in range(NST):
                    if n + 2 < NST:
                        rep(n + 2)
                    for dj in range(4):
                        tensor.wait_ge(s2_hc, 64 * c + n * 4 + dj + 1)
                        ii = tensor.matmul(out=ps_y4[dj], lhsT=id_bf[:],
                                           rhs=hc_sb[:, n % 2, dj, :],
                                           start=(n == 0), stop=(n == NST - 1))
                    ii.then_inc(s2_hcu, 1)
                tensor.nop().then_inc(s2_yd, 1)
            tensor.wait_ge(s2_ydu, N2)
            tensor.wait_ge(s3_w, 32)
            for m2 in range(2):
                for kj in range(16):
                    idx = m2 * 16 + kj
                    _w(tensor, s3_ev, idx - 1)
                    tensor.transpose(out=ps_tp[:, idx % 2, 0:128],
                                     in_=xst[:, m2 * 2048 + kj * 128: m2 * 2048 + (kj + 1) * 128],
                                     identity=id_s[:]).then_inc(s3_tp, 1)
            tensor.wait_ge(s3_w, 64)
            for m2 in range(2):
                for kj in range(8):
                    idx = 32 + m2 * 8 + kj
                    _w(tensor, s3_ev, idx - 1)
                    tensor.transpose(out=ps_tp[:, idx % 2, 0:128],
                                     in_=xst[:, m2 * 1024 + kj * 128: m2 * 1024 + (kj + 1) * 128],
                                     identity=id_s[:]).then_inc(s3_tp, 1)
            tensor.wait_ge(s3_ev, 48)
            for c in range(N3):
                tensor.wait_ge(s3_ld[c % 2], 32 * (c // 2 + 1))
                tensor.wait_ge(s3_lds[c % 2], 16 * (c // 2 + 1))
                for fi in range(2):
                    gidx = 2 * c + fi
                    _w(tensor, s3_sum, gidx - 1)
                    _w(tensor, s3_ff, gidx - 1)
                    for kj in range(16):
                        tensor.matmul(out=ps_xz[:, fi % 2, 0:LC3],
                                      lhsT=woT[:, (kj * 2 + fi) * 128:(kj * 2 + fi + 1) * 128],
                                      rhs=y2c_s[:, c % 2, kj, :],
                                      start=(kj == 0), stop=(kj == 15))
                    for kj in range(8):
                        ii = tensor.matmul(out=ps_dp[:, fi % 2, 0:LC3],
                                           lhsT=cbT[:, (kj * 2 + fi) * 128:(kj * 2 + fi + 1) * 128],
                                           rhs=xnc_s[:, c % 2, kj, :],
                                           start=(kj == 0), stop=(kj == 7))
                    ii.then_inc(s3_mm, 1)
                tensor.wait_ge(s3_sum, 2 * (c + 1))
                for i in range(2):
                    tidx = 2 * c + i
                    _w(tensor, s3_add, tidx - 1)
                    for fi in range(2):
                        ii = tensor.transpose(out=ps_tp[:, i % 2, fi * 128:(fi + 1) * 128],
                                              in_=af_s[:, fi, i * 128:(i + 1) * 128],
                                              identity=id_s[:])
                    ii.then_inc(s3_tt, 1)

        @block.scalar
        def _(scalar):
            for idx in range(64):
                mi, kj = idx // 8, idx % 8
                scalar.wait_ge(s1_tp, idx + 1)
                scalar.copy(out=winT[:, (kj * 8 + mi) * 128:(kj * 8 + mi + 1) * 128],
                            in_=ps_tp[:, idx % 2, 0:128]).then_inc(s1_tpe, 1)
            for dj in range(4):
                idx = 64 + dj
                scalar.wait_ge(s1_tp, idx + 1)
                scalar.copy(out=wxT[:, dj * 96:(dj + 1) * 96],
                            in_=ps_tp[:, idx % 2, 0:96]).then_inc(s1_tpe, 1)
            scalar.wait_ge(s_pst, N1)
            scalar.activation(out=rst_s[:], in_=mva_s[:, :, 1:2], func=AF.Ln, bias=EPS)
            scalar.activation(out=rst_s[:], in_=rst_s[:], func=AF.Exp, scale=-0.5).then_inc(s_rsl, 1)
            def p1_block(c):
                base = 68 + 8 * c
                _w(scalar, s1_xz, 8 * (c - 1))
                if c >= 2:
                    scalar.wait_ge(s1_out[c % 2], 48 * (c // 2))
                for j in range(8):
                    scalar.wait_ge(s1_tp, base + j + 1)
                    scalar.copy(out=xnf_s[:, c % 2, j, :], in_=ps_tp[:, j % 2, 0:LC1]).then_inc(s1_tpe, 1)
                _w(scalar, s1_mul, c)
                for mi in range(8):
                    gidx = 8 * c + mi
                    scalar.wait_ge(s1_xz, gidx + 1)
                    scalar.copy(out=xz_s[:, mi, :], in_=ps_xz[:, mi % 2, 0:LC1]).then_inc(s1_xze, 1)
                scalar.wait_ge(s1_conv, c + 1)
                for dj in range(4):
                    scalar.activation(out=sig_s[:, dj, :], in_=conv_c[:, dj, :], func=AF.Sigmoid)
                for dj in range(4):
                    ii = scalar.activation(out=sig_s[:, 4 + dj, :], in_=xz_s[:, 4 + dj, :],
                                           func=AF.Sigmoid)
                ii.then_inc(s1_sig, 1)
            def p1_pt(c):
                scalar.wait_ge(s1_pt, c + 1)
                if c >= 2:
                    scalar.wait_ge(s1_ptd[c % 2], 16 * (c // 2))
                scalar.copy(out=pt_s[:, c % 2, :], in_=ps_pt[:, 0:LC1]).then_inc(s1_pte, 1)

            for c in range(N1):
                if c > 0:
                    p1_block(c - 1)
                if c > 1:
                    p1_pt(c - 2)
            p1_block(N1 - 1)
            p1_pt(N1 - 2)
            p1_pt(N1 - 1)
            scalar.wait_ge(s2_wdma, 96)
            scalar.activation(out=a_sb.ap().rearrange("p a b -> p (a b)"),
                              in_=a_sb.ap().rearrange("p a b -> p (a b)"),
                              func=AF.Exp).then_inc(s2_ax, 1)
            for dj in range(4):
                scalar.wait_ge(s2_dp, dj + 1)
                scalar.copy(out=wdtT[:, dj, :], in_=ps_w2[:, :]).then_inc(s2_dpu, 1)
            scalar.wait_ge(s1_out[0], 48 * (N1 // 2))
            scalar.wait_ge(s1_out[1], 48 * (N1 // 2))
            for c in range(N2):
                scalar.wait_ge(s2_nz, c + 1)
                for dj in range(4):
                    scalar.activation(out=sp_sb[:, dj, :], in_=nz_sb[:, dj, :], func=AF.Exp)
                for dj in range(4):
                    ii = scalar.activation(out=sp_sb[:, dj, :], in_=sp_sb[:, dj, :], func=AF.Ln,
                                           bias=1.0)
                ii.then_inc(s2_sp, 1)
                scalar.wait_ge(s2_del, c + 1)
                for n in range(NST):
                    ridx = NST * c + n
                    scalar.wait_ge(s2_rep, ridx + 1)
                    _w(scalar, s2_rbu, ridx - 1)
                    scalar.copy(out=rep_bf[:, n % 2, 0, :], in_=ps_rep[:, 0, :])
                    scalar.copy(out=rep_bf[:, n % 2, 1, :], in_=ps_rep[:, 1, :]).then_inc(s2_repe, 1)
                    for dj in range(4):
                        idx = (c * NST + n) * 4 + dj
                        _w(scalar, s2_dau, idx - 1)
                        scalar.activation(out=da_sb[:, idx % 2, :], in_=del_sb[:, dj, :],
                                          func=AF.Exp, scale=a_sb[:, dj, n:n + 1]).then_inc(s2_da, 1)
            for idx in range(32):
                m2, kj = idx // 16, idx % 16
                scalar.wait_ge(s3_tp, idx + 1)
                scalar.copy(out=woT[:, (kj * 2 + m2) * 128:(kj * 2 + m2 + 1) * 128],
                            in_=ps_tp[:, idx % 2, 0:128]).then_inc(s3_ev, 1)
            for idx in range(32, 48):
                m2, kj = (idx - 32) // 8, (idx - 32) % 8
                scalar.wait_ge(s3_tp, idx + 1)
                scalar.copy(out=cbT[:, (kj * 2 + m2) * 128:(kj * 2 + m2 + 1) * 128],
                            in_=ps_tp[:, idx % 2, 0:128]).then_inc(s3_ev, 1)
            scalar.wait_ge(s2_wdma, 96)
            scalar.mul(out=scb_s[:], in_=scb_s[:], mul=1.0 / 127.0)
            for c in range(N3):
                for fi in range(2):
                    gidx = 2 * c + fi
                    scalar.wait_ge(s3_mm, gidx + 1)
                    _w(scalar, s3_sum, gidx - 1)
                    scalar.activation(out=ffn_s[:, fi % 2, :], in_=ps_dp[:, fi % 2, 0:LC3],
                                      func=AF.Identity, scale=scb_s[:, fi:fi + 1]).then_inc(s3_ff, 1)

        @block.vector
        def _(vector):
            vector.wait_ge(s_io, 1)
            vector.tensor_scalar(out=id_s[:], in0=io_s[:], scalar1=0.0, op0=ALU.is_equal,
                                 scalar2=None)
            for n in range(NST):
                ii = vector.tensor_scalar(out=en_bf[:, n * 128:(n + 1) * 128], in0=io16[:],
                                          scalar1=float(n), op0=ALU.is_equal, scalar2=None)
            ii.then_inc(s_io, 1)
            vector.tensor_copy(out=id_bf[:], in_=id_s[:])
            vector.memset(cst_s[:, 0:1], EPS)
            vector.memset(cst_s[:, 1:2], 1.0)
            vector.memset(pad_s.ap().rearrange("p a b -> p (a b)"), 0.0)
            vector.memset(state.ap().rearrange("p a b -> p (a b)"), 0.0)
            for c in range(N1):
                vector.wait_ge(s_px[c % 2], 16 * (c // 2 + 1))
                for i in range(NT1):
                    vector.bn_stats(out=bst_s[:, i, 0:6], in_=x_sb[:, c % 2, i, 0:512])
                    vector.bn_stats(out=bst_s[:, i, 6:12], in_=x_sb[:, c % 2, i, 512:1024])
                for i in range(NT1):
                    tg = c * NT1 + i
                    ii = vector.bn_aggr(out=mva_s[:, tg, :],
                                        in_=bst_s.ap()[:, i].rearrange("p (a b) -> p a b", b=3))
                ii.then_inc(s_pst, 1)
            vector.wait_ge(s_rsl, 1)
            vector.scalar_tensor_tensor(out=mrs_s[:], in0=mva_s[:, :, 0:1].rearrange("p a b -> p (a b)"),
                                        scalar=-1.0, in1=rst_s[:],
                                        op0=ALU.mult, op1=ALU.mult).then_inc(s_mrs, 1)
            def p1_affine(c):
                vector.wait_ge(s1_x[c % 2], 16 * (c // 2 + 1))
                if c >= 2:
                    vector.wait_ge(s1_tp, 68 + 8 * (c - 1))
                for i in range(NT1):
                    tg = c * NT1 + i
                    ii = vector.tensor_scalar(out=xnt_s[:, c % 2, i, :], in0=x_sb[:, c % 2, i, :],
                                              scalar1=mva_s[:, tg, 0:1], op0=ALU.subtract,
                                              scalar2=rst_s[:, tg:tg + 1], op1=ALU.mult)
                ii.then_inc(s1_aff, 1)

            p1_affine(0)
            for c in range(N1):
                if c + 1 < N1:
                    p1_affine(c + 1)
                vector.wait_ge(s1_xze, 8 * c + 4)
                for dj in range(4):
                    vector.tensor_copy(out=pad_s[:, dj, 3:3 + LC1], in_=xz_s[:, dj, :])
                for dj in range(4):
                    vector.tensor_scalar(out=cva_s[:], in0=pad_s[:, dj, 0:LC1],
                                         scalar1=cw_s[:, 4 * dj:4 * dj + 1], op0=ALU.mult,
                                         scalar2=cb_s[:, dj:dj + 1], op1=ALU.add)
                    vector.scalar_tensor_tensor(out=cvb_s[:], in0=pad_s[:, dj, 1:1 + LC1],
                                                scalar=cw_s[:, 4 * dj + 1:4 * dj + 2],
                                                in1=cva_s[:], op0=ALU.mult, op1=ALU.add)
                    vector.scalar_tensor_tensor(out=cva_s[:], in0=pad_s[:, dj, 2:2 + LC1],
                                                scalar=cw_s[:, 4 * dj + 2:4 * dj + 3],
                                                in1=cvb_s[:], op0=ALU.mult, op1=ALU.add)
                    vector.scalar_tensor_tensor(out=conv_c[:, dj, :], in0=pad_s[:, dj, 3:3 + LC1],
                                                scalar=cw_s[:, 4 * dj + 3:4 * dj + 4],
                                                in1=cva_s[:], op0=ALU.mult, op1=ALU.add)
                    ii = vector.tensor_copy(out=pad_s[:, dj, 0:3], in_=pad_s[:, dj, LC1:LC1 + 3])
                ii.then_inc(s1_conv, 1)
                vector.wait_ge(s1_sig, c + 1)
                if c >= 2:
                    vector.wait_ge(s1_out[c % 2], 48 * (c // 2))
                for dj in range(4):
                    vector.tensor_mul(out=u1_s[:, c % 2, dj, :], in0=conv_c[:, dj, :], in1=sig_s[:, dj, :])
                _w(vector, s1_pt, c - 1)
                for dj in range(4):
                    vector.tensor_copy(out=ub4_s[:, c % 2, dj, :], in_=u1_s[:, c % 2, dj, :])
                if c >= 1:
                    vector.wait_ge(s1_out[(c - 1) % 2], 48 * ((c - 1) // 2 + 1))
                for dj in range(4):
                    ii = vector.tensor_mul(out=g1_s[:, dj, :], in0=xz_s[:, 4 + dj, :],
                                           in1=sig_s[:, 4 + dj, :])
                ii.then_inc(s1_mul, 1)
            vector.wait_ge(s2_ax, 1)
            vector.tensor_scalar(out=a_sb.ap().rearrange("p a b -> p (a b)"),
                                 in0=a_sb.ap().rearrange("p a b -> p (a b)"),
                                 scalar1=-1.0, op0=ALU.mult, scalar2=None)
            def p2_delta(c):
                vector.wait_ge(s2_ld[c % 2], 80 * (c // 2 + 1))
                _w(vector, s2_dp, 4 + 4 * c)
                vector.tensor_copy(out=dtb16[:, c % 2, :], in_=dtl_sb[:, c % 2, :]).then_inc(s2_sum, 1)
                vector.tensor_copy(out=bps_bf[:], in_=bps_sb[:, c % 2, :])
                vector.tensor_copy(out=cps_bf[:], in_=cps_sb[:, c % 2, :]).then_inc(s2_sum, 1)
                for dj in range(4):
                    vector.wait_ge(s2_dp, 4 + 4 * c + dj + 1)
                    vector.tensor_scalar(out=z_sb[:, dj, :], in0=ps_dp[:, dj % 2, 0:LS],
                                         scalar1=dtb_s[:, dj:dj + 1], op0=ALU.add,
                                         scalar2=None).then_inc(s2_dpu, 1)
                for dj in range(4):
                    ii = vector.scalar_tensor_tensor(out=nz_sb[:, dj, :], in0=z_sb[:, dj, :],
                                                     scalar=-1.0, in1=z_sb[:, dj, :],
                                                     op0=ALU.mult, op1=ALU.min)
                ii.then_inc(s2_nz, 1)
                vector.wait_ge(s2_sp, c + 1)
                for dj in range(4):
                    ii = vector.scalar_tensor_tensor(out=del_sb[:, dj, :], in0=z_sb[:, dj, :],
                                                     scalar=0.0, in1=sp_sb[:, dj, :],
                                                     op0=ALU.max, op1=ALU.add)
                ii.then_inc(s2_del, 1)
                for dj in range(4):
                    vector.tensor_mul(out=du_sb[:, dj, :], in0=del_sb[:, dj, :], in1=u_sb[:, c % 2, dj, :])
                for dj in range(4):
                    vector.tensor_copy(out=du_bf[:, dj, :], in_=du_sb[:, dj, :])

            p2_delta(0)
            for c in range(N2):
                for n in range(NST):
                    vector.wait_ge(s2_repe, NST * c + n + 1)
                    for dj in range(4):
                        idx = (c * NST + n) * 4 + dj
                        vector.wait_ge(s2_da, idx + 1)
                        vector.tensor_mul(out=dbu_sb[:], in0=du_bf[:, dj, :], in1=rep_bf[:, n % 2, 0, :])
                        vector.tensor_tensor_scan(out=h_sb[:], data0=da_sb[:, idx % 2, :], data1=dbu_sb[:],
                                                  initial=state[:, dj, n:n + 1],
                                                  op0=ALU.mult, op1=ALU.add).then_inc(s2_dau, 1)
                        vector.tensor_copy(out=state[:, dj, n:n + 1], in_=h_sb[:, LS - 1:LS])
                        _w(vector, s2_hcu, 16 * c + n - 1)
                        vector.tensor_mul(out=hc_sb[:, n % 2, dj, :], in0=h_sb[:],
                                           in1=rep_bf[:, n % 2, 1, :]).then_inc(s2_hc, 1)
                        if dj == 3:
                            vector.tensor_copy(out=dbu_sb[:, 0:1],
                                               in_=dbu_sb[:, 0:1]).then_inc(s2_rbu, 1)
                if c + 1 < N2:
                    vector.wait_ge(s2_rep, NST * (c + 1))
                    p2_delta(c + 1)
                if c >= 2:
                    vector.wait_ge(s2_st[c % 2], 16 * (c // 2))
                vector.wait_ge(s2_yd, c + 1)
                ps_y4v = [ps_tp.ap()[:, 0, 0:LS], ps_tp.ap()[:, 1, 0:LS],
                          ps_xz.ap()[:, 0, 0:LS], ps_xz.ap()[:, 1, 0:LS]]
                for dj in range(4):
                    vector.scalar_tensor_tensor(out=ydu_sb[:], in0=u_sb[:, c % 2, dj, :],
                                                scalar=dsk_s[:, dj:dj + 1],
                                                in1=ps_y4v[dj],
                                                op0=ALU.mult, op1=ALU.add)
                    ii = vector.tensor_mul(out=y2_sb[:, c % 2, dj, :], in0=ydu_sb[:],
                                           in1=g2_sb[:, c % 2, dj, :])
                ii.then_inc(s2_y2, 1)
                vector.nop().then_inc(s2_ydu, 1)
                vector.tensor_copy(out=dbu_sb[:, 0:1], in_=dbu_sb[:, 0:1]).then_inc(s2_used, 1)
            for c in range(N3):
                for fi in range(2):
                    gidx = 2 * c + fi
                    vector.wait_ge(s3_ff, gidx + 1)
                    vector.scalar_tensor_tensor(out=af_s[:, fi, :], in0=ps_xz[:, fi % 2, 0:LC3],
                                                scalar=bias_s[:, fi:fi + 1], in1=ffn_s[:, fi % 2, :],
                                                op0=ALU.add, op1=ALU.add).then_inc(s3_sum, 1)
                _w(vector, s3_st, 16 * c)
                vector.wait_ge(s3_xc[c % 2], 16 * (c // 2 + 1))
                for i in range(2):
                    tidx = 2 * c + i
                    vector.wait_ge(s3_tt, tidx + 1)
                    vector.tensor_add(out=out_s[:, i, :],
                                      in0=ps_tp[:, i % 2, 0:FS], in1=xcol_s[:, c % 2, i, :]
                                      ).then_inc(s3_add, 1)
    return nc


def kernel(L0_attn_hidden_states, L0_attn_W_in, L0_attn_conv_w, L0_attn_conv_b,
           L0_attn_W_x, L0_attn_W_dt, L0_attn_dt_bias, L0_attn_A_log,
           L0_attn_D_skip, L0_attn_W_out, L0_ffn_CB, L0_ffn_SCB, L0_ffn_bias):
    x = np.ascontiguousarray(np.asarray(L0_attn_hidden_states, dtype=np.float32))
    W_in = np.asarray(L0_attn_W_in, dtype=np.float32)
    conv_w = np.asarray(L0_attn_conv_w, dtype=np.float32)
    conv_b = np.asarray(L0_attn_conv_b, dtype=np.float32)
    W_x = np.asarray(L0_attn_W_x, dtype=np.float32)
    W_dt = np.asarray(L0_attn_W_dt, dtype=np.float32)
    dt_bias = np.asarray(L0_attn_dt_bias, dtype=np.float32)
    A_log = np.asarray(L0_attn_A_log, dtype=np.float32)
    D_skip = np.asarray(L0_attn_D_skip, dtype=np.float32)
    W_out = np.asarray(L0_attn_W_out, dtype=np.float32)
    CB = np.asarray(L0_ffn_CB, dtype=np.float32)
    SCB = np.asarray(L0_ffn_SCB, dtype=np.float32)
    fbias = np.asarray(L0_ffn_bias, dtype=np.float32)
    C = np.ascontiguousarray

    nc = build_fused()
    maps = []
    for cid in range(NC):
        b, s = cid // 4, cid % 4
        sl = slice(ES * s, ES * (s + 1))
        fs = slice(FS * s, FS * (s + 1))
        maps.append(dict(
            x=C(x[b]),
            w_in=C(np.concatenate([W_in[ES * s:ES * (s + 1)], W_in[E + ES * s:E + ES * (s + 1)]], axis=0)),
            conv_w=C(conv_w[sl]),
            conv_b=C(conv_b[sl].reshape(4, 128).T),
            w_x=C(W_x[:, sl]),
            w_dt=C(W_dt[sl]),
            dt_bias=C(dt_bias[sl].reshape(4, 128).T),
            a_log=C(A_log[sl]),
            d_skip=C(D_skip[sl].reshape(4, 128).T),
            w_out_s=C(W_out[fs]),
            cb_sl=C(CB[fs]),
            scb_v=C(SCB[fs].reshape(2, 128).T),
            bias_v=C(fbias[fs].reshape(2, 128).T),
            xcol=C(x[b][:, fs]),
        ))
    res = run_bass_kernel_spmd(nc, maps, core_ids=list(range(NC))).results

    out = np.empty((B, L, D), dtype=np.float32)
    for cid in range(NC):
        b, s = cid // 4, cid % 4
        out[b, :, FS * s:FS * (s + 1)] = res[cid]["out"]
    return out


# version 14
# speedup vs baseline: 1.3668x; 1.0114x over previous
import contextlib
import numpy as np
import ml_dtypes
import concourse.bass as bass
import concourse.mybir as mybir
from concourse.bass_utils import run_bass_kernel_spmd

F32 = mybir.dt.float32
BF16 = mybir.dt.bfloat16
AF = mybir.ActivationFunctionType
ALU = mybir.AluOpType
AX = mybir.AxisListType

B, L, D, E, NST, R = 2, 8192, 1024, 2048, 16, 64
ES = 512
NC = 8
EPS = 1e-5

LC1 = 256
N1 = L // LC1
NT1 = LC1 // 128
LS = 512
N2 = L // LS
LC3 = 256
N3 = L // LC3
FS = 256


def _w(eng, sem, v):
    if v > 0:
        eng.wait_ge(sem, v)


def build_fused():
    nc = bass.Bass("TRN2", target_bir_lowering=False, detect_race_conditions=False)
    x = nc.dram_tensor("x", [L, D], F32, kind="ExternalInput").ap()
    w_in = nc.dram_tensor("w_in", [1024, D], F32, kind="ExternalInput").ap()
    conv_w = nc.dram_tensor("conv_w", [ES, 4], F32, kind="ExternalInput").ap()
    conv_b = nc.dram_tensor("conv_b", [128, 4], F32, kind="ExternalInput").ap()
    w_x = nc.dram_tensor("w_x", [96, ES], F32, kind="ExternalInput").ap()
    w_dt = nc.dram_tensor("w_dt", [ES, R], F32, kind="ExternalInput").ap()
    dt_bias = nc.dram_tensor("dt_bias", [128, 4], F32, kind="ExternalInput").ap()
    a_log = nc.dram_tensor("a_log", [ES, NST], F32, kind="ExternalInput").ap()
    d_skip = nc.dram_tensor("d_skip", [128, 4], F32, kind="ExternalInput").ap()
    w_out_s = nc.dram_tensor("w_out_s", [FS, E], F32, kind="ExternalInput").ap()
    cb_sl = nc.dram_tensor("cb_sl", [FS, D], F32, kind="ExternalInput").ap()
    scb_v = nc.dram_tensor("scb_v", [128, 2], F32, kind="ExternalInput").ap()
    bias_v = nc.dram_tensor("bias_v", [128, 2], F32, kind="ExternalInput").ap()
    xcol = nc.dram_tensor("xcol", [L, FS], F32, kind="ExternalInput").ap()
    out = nc.dram_tensor("out", [L, FS], F32, kind="ExternalOutput").ap()

    u_sc = nc.dram_tensor("u_sc", [ES, L], F32).ap()
    g2_sc = nc.dram_tensor("g2_sc", [ES, L], F32).ap()
    xn_sc = nc.dram_tensor("xn_sc", [D, L], BF16).ap()
    pt_ina = nc.dram_tensor("pt_ina", [96, L // 2], F32).ap()
    pt_inb = nc.dram_tensor("pt_inb", [96, L // 2], F32).ap()
    pt_suma = nc.dram_tensor("pt_suma", [96, L // 2], F32).ap()
    pt_sumb = nc.dram_tensor("pt_sumb", [96, L // 2], F32).ap()
    y2_sla = nc.dram_tensor("y2_sla", [ES, L // 2], BF16).ap()
    y2_slb = nc.dram_tensor("y2_slb", [ES, L // 2], BF16).ap()
    y2_fa = nc.dram_tensor("y2_fa", [4 * ES, L // 2], BF16).ap()
    y2_fb = nc.dram_tensor("y2_fb", [4 * ES, L // 2], BF16).ap()

    u_sc_r = u_sc.rearrange("(dj p) l -> p dj l", p=128)
    g2_sc_r = g2_sc.rearrange("(dj p) l -> p dj l", p=128)
    xn_sc_r = xn_sc.rearrange("(j p) l -> p j l", p=128)
    y2_sla_r = y2_sla.rearrange("(dj p) l -> p dj l", p=128)
    y2_slb_r = y2_slb.rearrange("(dj p) l -> p dj l", p=128)
    y2_fa_r = y2_fa.rearrange("(kj p) l -> p kj l", p=128)
    y2_fb_r = y2_fb.rearrange("(kj p) l -> p kj l", p=128)

    RG = [[0, 1, 2, 3], [4, 5, 6, 7]]

    with contextlib.ExitStack() as _es:
        sb = lambda name, shape, dt=F32: _es.enter_context(nc.sbuf_tensor(name, shape, dt))
        ps = lambda name, shape: _es.enter_context(nc.psum_tensor(name, shape, F32))
        sem = lambda name: _es.enter_context(nc.semaphore(name))

        io_s = sb("io_s", [128, 128])
        id_s = sb("id_s", [128, 128])
        io16 = sb("io16", [16, 128])
        cst_s = sb("cst_s", [128, 2])

        winT = sb("winT", [128, 8 * 8 * 128], BF16)
        wxT = sb("wxT", [128, 4 * 96], BF16)
        cw_s = sb("cw_s", [128, 16])
        cb_s = sb("cb_s", [128, 4])
        x_sb = sb("x_sb", [128, 2, NT1, 1024])
        bst_s = sb("bst_s", [128, 2, 12])
        mva_s = sb("mva_s", [128, 64, 2])
        rst_s = sb("rst_s", [128, 64])
        xnt_s = sb("xnt_s", [128, 2, NT1, 1024])
        mrs_s = sb("mrs_s", [128, 64])
        xnf_s = sb("xnf_s", [128, 2, 8, LC1], BF16)
        xz_s = sb("xz_s", [128, 8, LC1])
        pad_s = sb("pad_s", [128, 4, LC1 + 3])
        cva_s = sb("cva_s", [128, LC1])
        cvb_s = sb("cvb_s", [128, LC1])
        conv_c = sb("conv_c", [128, 4, LC1])
        sig_s = sb("sig_s", [128, 8, LC1])
        u1_s = sb("u1_s", [128, 2, 4, LC1])
        ub4_s = sb("ub4_s", [128, 2, 4, LC1], BF16)
        g1_s = sb("g1_s", [128, 4, LC1])
        pt_s = sb("pt_s", [96, 2, LC1])
        wst2_s = sb("wst2_s", [96, 512])

        wdt_st = sb("wdt_st", [128, 4, R])
        wdtT = sb("wdtT", [64, 4, 128], BF16)
        a_sb = sb("a_sb", [128, 4, NST])
        dtb_s = sb("dtb_s", [128, 4])
        dsk_s = sb("dsk_s", [128, 4])
        state = sb("state", [128, 4, NST])
        u_sb = x_sb.ap().rearrange("p a b c -> p (a b c)").rearrange(
            "p (x y z) -> p x y z", x=2, y=4, z=LS)
        g2_sb = xnt_s.ap().rearrange("p a b c -> p (a b c)").rearrange(
            "p (x y z) -> p x y z", x=2, y=4, z=LS)
        z_sb = sig_s.ap().rearrange("p a b -> p (a b)").rearrange(
            "p (y z) -> p y z", y=4, z=LS)
        nz_sb = xz_s.ap().rearrange("p a b -> p (a b)").rearrange(
            "p (y z) -> p y z", y=4, z=LS)
        sp_sb = u1_s.ap().rearrange("p a b c -> p (a b c)").rearrange(
            "p (y z) -> p y z", y=4, z=LS)
        dtl_sb = sb("dtl_sb", [64, 2, LS])
        bps_sb = sb("bps_sb", [16, 2, LS])
        cps_sb = sb("cps_sb", [16, 2, LS])
        dtb16 = sb("dtb16", [64, 2, LS], BF16)
        del_sb = sb("del_sb", [128, 4, LS])
        du_sb = sb("du_sb", [128, 4, LS])
        da_sb = sb("da_sb", [128, 2, LS])
        dbu_sb = sb("dbu_sb", [128, LS], BF16)
        h_sb = sb("h_sb", [128, LS], BF16)
        hc_sb = sb("hc_sb", [128, 2, 4, LS], BF16)
        du_bf = sb("du_bf", [128, 4, LS], BF16)
        rep_bf = sb("rep_bf", [128, 2, 2, LS], BF16)
        en_bf = sb("en_bf", [16, 16 * 128], BF16)
        bps_bf = sb("bps_bf", [16, 2, LS], BF16)
        cps_bf = sb("cps_bf", [16, 2, LS], BF16)
        id_bf = sb("id_bf", [128, 128], BF16)
        ydu_sb = sb("ydu_sb", [128, LS])
        y2_sb = sb("y2_sb", [128, 2, 4, LS], BF16)

        woT = winT.ap()[:, 0:16 * 2 * 128]
        cbT = winT.ap()[:, 16 * 2 * 128:16 * 2 * 128 + 8 * 2 * 128]
        scb_s = sb("scb_s", [128, 2])
        bias_s = sb("bias_s", [128, 2])
        y2c_s = sb("y2c_s", [128, 2, 16, LC3], BF16)
        xnc_s = sb("xnc_s", [128, 2, 8, LC3], BF16)
        xcol_s = g1_s.ap().rearrange("p a b -> p (a b)").rearrange(
            "p (x y z) -> p x y z", x=2, y=2, z=FS)
        ffn_s = pad_s.ap().rearrange("p a b -> p (a b)")[:, 0:2 * LC3].rearrange(
            "p (y z) -> p y z", y=2, z=LC3)
        af_s = conv_c.ap().rearrange("p a b -> p (a b)")[:, 0:2 * LC3].rearrange(
            "p (y z) -> p y z", y=2, z=LC3)
        out_s = sb("out_s", [128, 2, FS])

        ps_tp = ps("ps_tp", [128, 2, 512])
        ps_xz = ps("ps_xz", [128, 2, 512])
        ps_dp = ps("ps_dp", [128, 2, 512])
        ps_pt = ps_dp.ap()[0:96, 0, :]
        ps_rep = ps("ps_rep", [128, 2, LS])
        ps_w2 = ps_xz.ap()[0:64, 0, 0:128]

        s_io = sem("s_io")
        s_wdma = sem("s_wdma")
        s_px = [sem("s_px0"), sem("s_px1")]
        s_pst = sem("s_pst")
        s_rsl = sem("s_rsl")
        s_mrs = sem("s_mrs")
        s3_lds = [sem("s3_lds0"), sem("s3_lds1")]
        s1_x = [sem("s1_x0"), sem("s1_x1")]
        s1_aff = sem("s1_aff")
        s1_tp = sem("s1_tp")
        s1_tpe = sem("s1_tpe")
        s1_xz = sem("s1_xz")
        s1_xze = sem("s1_xze")
        s1_conv = sem("s1_conv")
        s1_sig = sem("s1_sig")
        s1_mul = sem("s1_mul")
        s1_pt = sem("s1_pt")
        s1_pte = sem("s1_pte")
        s1_out = [sem("s1_out0"), sem("s1_out1")]
        s1_ptd = [sem("s1_ptd0"), sem("s1_ptd1")]
        s_cc = sem("s_cc")
        s2_wdma = sem("s2_wdma")
        s2_ax = sem("s2_ax")
        s2_ld = [sem("s2_ld0"), sem("s2_ld1")]
        s2_used = sem("s2_used")
        s2_sum = sem("s2_sum")
        s2_dp = sem("s2_dp")
        s2_dpu = sem("s2_dpu")
        s2_nz = sem("s2_nz")
        s2_sp = sem("s2_sp")
        s2_del = sem("s2_del")
        s2_rep = sem("s2_rep")
        s2_repe = sem("s2_repe")
        s2_rbu = sem("s2_rbu")
        s2_hc = sem("s2_hc")
        s2_hcu = sem("s2_hcu")
        s2_yd = sem("s2_yd")
        s2_ydu = sem("s2_ydu")
        s2_da = sem("s2_da")
        s2_dau = sem("s2_dau")
        s2_y2 = sem("s2_y2")
        s2_st = [sem("s2_st0"), sem("s2_st1")]
        s3_w = sem("s3_w")
        s3_ld = [sem("s3_ld0"), sem("s3_ld1")]
        s3_xc = [sem("s3_xc0"), sem("s3_xc1")]
        s3_tp = sem("s3_tp")
        s3_ev = sem("s3_ev")
        s3_mm = sem("s3_mm")
        s3_ff = sem("s3_ff")
        s3_sum = sem("s3_sum")
        s3_tt = sem("s3_tt")
        s3_add = sem("s3_add")
        s3_st = sem("s3_st")
        block = _es.enter_context(nc.Block())

        nc.const_aps.register(F32, EPS, cst_s[:, 0:1])
        nc.const_aps.register(F32, 1.0, cst_s[:, 1:2])

        @block.sync
        def _(sync):
            sync.dma_start(out=x_sb.ap().rearrange("p b t f -> p (b t) f"),
                           in_=w_in[0:512, :].rearrange("(mi p) k -> p mi k", p=128)).then_inc(s_wdma, 16)
            sync.dma_start(out=wst2_s[:], in_=w_x).then_inc(s_wdma, 16)
            sync.dma_start(out=cw_s.ap().rearrange("p (dj k) -> p dj k", k=4),
                           in_=conv_w.rearrange("(dj p) k -> p dj k", p=128)).then_inc(s_wdma, 16)
            sync.dma_start(out=cb_s[:], in_=conv_b).then_inc(s_wdma, 16)
            sync.wait_ge(s1_tp, 32)
            sync.dma_start(out=x_sb.ap().rearrange("p b t f -> p (b t) f"),
                           in_=w_in[512:1024, :].rearrange("(mi p) k -> p mi k", p=128)).then_inc(s_wdma, 16)
            sync.dma_start(out=wdt_st[:], in_=w_dt.rearrange("(dj p) r -> p dj r", p=128)).then_inc(s2_wdma, 16)
            sync.dma_start(out=a_sb[:], in_=a_log.rearrange("(dj p) n -> p dj n", p=128)).then_inc(s2_wdma, 16)
            sync.dma_start(out=dtb_s[:], in_=dt_bias).then_inc(s2_wdma, 16)
            sync.dma_start(out=dsk_s[:], in_=d_skip).then_inc(s2_wdma, 16)
            sync.dma_start(out=scb_s[:], in_=scb_v).then_inc(s2_wdma, 16)
            sync.dma_start(out=bias_s[:], in_=bias_v).then_inc(s2_wdma, 16)
            for c in range(N1):
                if c < 2:
                    sync.wait_ge(s1_tp, 68)
                else:
                    _w(sync, s_pst, c - 1)
                sync.dma_start(out=x_sb[:, c % 2],
                               in_=x[c * LC1:(c + 1) * LC1, :].rearrange("(i p) f -> p i f", p=128)
                               ).then_inc(s_px[c % 2], 16)
            for c in range(N1):
                if c < 2:
                    sync.wait_ge(s_pst, N1)
                else:
                    sync.wait_ge(s1_aff, c - 1)
                sync.dma_start(out=x_sb[:, c % 2],
                               in_=x[c * LC1:(c + 1) * LC1, :].rearrange("(i p) f -> p i f", p=128)
                               ).then_inc(s1_x[c % 2], 16)
            sync.wait_ge(s1_aff, N1)
            sync.wait_ge(s1_tp, 68 + 8 * N1)
            sync.wait_ge(s_cc, 1)
            for c in range(N2):
                if c == N2 // 2:
                    sync.wait_ge(s_cc, 2)
                _w(sync, s2_used, c - 1)
                sl = slice(c * LS, (c + 1) * LS)
                pts_h = pt_suma if c < N2 // 2 else pt_sumb
                hsl = slice((c % (N2 // 2)) * LS, (c % (N2 // 2) + 1) * LS)
                sync.dma_start(out=u_sb[:, c % 2], in_=u_sc_r[:, :, sl]).then_inc(s2_ld[c % 2], 16)
                sync.dma_start(out=g2_sb[:, c % 2], in_=g2_sc_r[:, :, sl]).then_inc(s2_ld[c % 2], 16)
                sync.dma_start(out=dtl_sb[:, c % 2, :], in_=pts_h[0:64, hsl]).then_inc(s2_ld[c % 2], 16)
                sync.dma_start(out=bps_sb[:, c % 2, :], in_=pts_h[64:80, hsl]).then_inc(s2_ld[c % 2], 16)
                sync.dma_start(out=cps_sb[:, c % 2, :], in_=pts_h[80:96, hsl]).then_inc(s2_ld[c % 2], 16)
            sync.wait_ge(s2_used, N2)
            xstf = x_sb.ap().rearrange("p b t f -> p (b t f)")
            for m2 in range(2):
                sync.dma_start(out=xstf[:, m2 * 2048:(m2 + 1) * 2048],
                               in_=w_out_s[m2 * 128:(m2 + 1) * 128, :]).then_inc(s3_w, 16)
            sync.wait_ge(s3_tp, 32)
            for m2 in range(2):
                sync.dma_start(out=xstf[:, m2 * 1024:(m2 + 1) * 1024],
                               in_=cb_sl[m2 * 128:(m2 + 1) * 128, :]).then_inc(s3_w, 16)
            sync.wait_ge(s_cc, 3)
            for c in range(N3):
                if c == N3 // 2:
                    sync.wait_ge(s_cc, 4)
                _w(sync, s3_mm, 2 * (c - 1))
                if c < N3 // 2:
                    sync.dma_start(out=y2c_s[:, c % 2, 0:8],
                                   in_=y2_fa_r[:, 0:8, c * LC3:(c + 1) * LC3]).then_inc(s3_ld[c % 2], 16)
                else:
                    cc = c - N3 // 2
                    sync.dma_start(out=y2c_s[:, c % 2, 0:8],
                                   in_=y2_fb_r[:, 0:8, cc * LC3:(cc + 1) * LC3]).then_inc(s3_ld[c % 2], 16)
                sl = slice(c * LC3, (c + 1) * LC3)
                sync.dma_start(out=xnc_s[:, c % 2], in_=xn_sc_r[:, :, sl]).then_inc(s3_ld[c % 2], 16)

        @block.gpsimd
        def _(gpsimd):
            gpsimd.iota(out=io_s[:], pattern=[[1, 128]], base=0, channel_multiplier=-1,
                        allow_small_or_imprecise_dtypes=True)
            gpsimd.iota(out=io16[:], pattern=[[0, 128]], base=0, channel_multiplier=1,
                        allow_small_or_imprecise_dtypes=True).then_inc(s_io, 1)
            def p1_ptdma(c):
                gpsimd.wait_ge(s1_pte, c + 1)
                pth = pt_ina if c < N1 // 2 else pt_inb
                gpsimd.dma_start(out=pth[:, (c % (N1 // 2)) * LC1:(c % (N1 // 2) + 1) * LC1],
                                 in_=pt_s[:, c % 2]).then_inc(s1_ptd[c % 2], 16)

            for c in range(N1):
                gpsimd.wait_ge(s1_mul, c + 1)
                sl = slice(c * LC1, (c + 1) * LC1)
                gpsimd.dma_start(out=u_sc_r[:, :, sl], in_=u1_s[:, c % 2]).then_inc(s1_out[c % 2], 16)
                gpsimd.dma_start(out=g2_sc_r[:, :, sl], in_=g1_s[:]).then_inc(s1_out[c % 2], 16)
                gpsimd.dma_start(out=xn_sc_r[:, :, sl], in_=xnf_s[:, c % 2]).then_inc(s1_out[c % 2], 16)
                if c >= 2:
                    p1_ptdma(c - 2)
            p1_ptdma(N1 - 2)
            p1_ptdma(N1 - 1)
            gpsimd.wait_ge(s1_out[0], N1 // 4 * 48)
            gpsimd.wait_ge(s1_out[1], N1 // 4 * 48)
            gpsimd.wait_ge(s1_ptd[0], N1 // 4 * 16)
            gpsimd.wait_ge(s1_ptd[1], N1 // 4 * 16)
            gpsimd.collective_compute("AllReduce", ALU.add, replica_groups=RG,
                                      ins=[pt_ina.opt()], outs=[pt_suma.opt()]).then_inc(s_cc, 1)
            gpsimd.wait_ge(s1_out[0], N1 // 2 * 48)
            gpsimd.wait_ge(s1_out[1], N1 // 2 * 48)
            gpsimd.wait_ge(s1_ptd[0], N1 // 2 * 16)
            gpsimd.wait_ge(s1_ptd[1], N1 // 2 * 16)
            gpsimd.collective_compute("AllReduce", ALU.add, replica_groups=RG,
                                      ins=[pt_inb.opt()], outs=[pt_sumb.opt()]).then_inc(s_cc, 1)
            for c in range(N2 // 2):
                gpsimd.wait_ge(s2_y2, c + 1)
                gpsimd.dma_start(out=y2_sla_r[:, :, c * LS:(c + 1) * LS],
                                 in_=y2_sb[:, c % 2]).then_inc(s2_st[c % 2], 16)
            gpsimd.wait_ge(s2_st[0], N2 // 4 * 16)
            gpsimd.wait_ge(s2_st[1], N2 // 4 * 16)
            gpsimd.collective_compute("AllGather", ALU.bypass, replica_groups=RG,
                                      ins=[y2_sla.opt()], outs=[y2_fa.opt()]).then_inc(s_cc, 1)
            for c in range(N2 // 2, N2):
                gpsimd.wait_ge(s2_y2, c + 1)
                cc = c - N2 // 2
                gpsimd.dma_start(out=y2_slb_r[:, :, cc * LS:(cc + 1) * LS],
                                 in_=y2_sb[:, c % 2]).then_inc(s2_st[c % 2], 16)
            gpsimd.wait_ge(s2_st[0], N2 // 2 * 16)
            gpsimd.wait_ge(s2_st[1], N2 // 2 * 16)
            gpsimd.collective_compute("AllGather", ALU.bypass, replica_groups=RG,
                                      ins=[y2_slb.opt()], outs=[y2_fb.opt()]).then_inc(s_cc, 1)
            gpsimd.wait_ge(s_cc, 3)

            def p3_pref(c):
                if c == N3 // 2:
                    gpsimd.wait_ge(s_cc, 4)
                _w(gpsimd, s3_mm, 2 * (c - 1))
                if c < N3 // 2:
                    gpsimd.dma_start(out=y2c_s[:, c % 2, 8:16],
                                     in_=y2_fa_r[:, 8:16, c * LC3:(c + 1) * LC3]).then_inc(s3_lds[c % 2], 16)
                else:
                    cc = c - N3 // 2
                    gpsimd.dma_start(out=y2c_s[:, c % 2, 8:16],
                                     in_=y2_fb_r[:, 8:16, cc * LC3:(cc + 1) * LC3]).then_inc(s3_lds[c % 2], 16)
                sl = slice(c * LC3, (c + 1) * LC3)
                _w(gpsimd, s3_add, 2 * (c - 1))
                gpsimd.dma_start(out=xcol_s[:, c % 2],
                                 in_=xcol[sl, :].rearrange("(i p) f -> p i f", p=128)).then_inc(s3_xc[c % 2], 16)

            for c in range(N3):
                p3_pref(c)
                if c > 0:
                    gpsimd.wait_ge(s3_add, 2 * c)
                    gpsimd.dma_start(out=out[(c - 1) * LC3:c * LC3, :].rearrange("(i p) f -> p i f", p=128),
                                     in_=out_s[:, :, :]).then_inc(s3_st, 16)
            gpsimd.wait_ge(s3_add, 2 * N3)
            gpsimd.dma_start(out=out[(N3 - 1) * LC3:N3 * LC3, :].rearrange("(i p) f -> p i f", p=128),
                             in_=out_s[:, :, :]).then_inc(s3_st, 16)
            gpsimd.wait_ge(s3_st, N3 * 16)

        @block.tensor
        def _(tensor):
            tensor.wait_ge(s_io, 2)
            xst = x_sb.ap().rearrange("p b t f -> p (b t f)")
            tensor.wait_ge(s_wdma, 64)
            for half in range(2):
                if half == 1:
                    tensor.wait_ge(s_wdma, 80)
                for mi4 in range(4):
                    mi = half * 4 + mi4
                    for kj in range(8):
                        idx = mi * 8 + kj
                        _w(tensor, s1_tpe, idx - 1)
                        tensor.transpose(out=ps_tp[:, idx % 2, 0:128],
                                         in_=xst[:, mi4 * 1024 + kj * 128: mi4 * 1024 + (kj + 1) * 128],
                                         identity=id_s[:]).then_inc(s1_tp, 1)
            for dj in range(4):
                idx = 64 + dj
                _w(tensor, s1_tpe, idx - 1)
                tensor.transpose(out=ps_tp[:, idx % 2, 0:96], in_=wst2_s[:, dj * 128:(dj + 1) * 128],
                                 identity=id_s[0:96, 0:96]).then_inc(s1_tp, 1)
            def p1_tp_mm(c):
                base = 68 + 8 * c
                tensor.wait_ge(s1_aff, c + 1)
                for j in range(8):
                    _w(tensor, s1_tpe, base + j - 1)
                    for i in range(NT1):
                        ii = tensor.transpose(out=ps_tp[:, j % 2, i * 128:(i + 1) * 128],
                                              in_=xnt_s[:, c % 2, i, j * 128:(j + 1) * 128],
                                              identity=id_s[:])
                    ii.then_inc(s1_tp, 1)
                tensor.wait_ge(s1_tpe, base + 8)
                for mi in range(8):
                    gidx = 8 * c + mi
                    _w(tensor, s1_xze, gidx - 1)
                    for kj in range(8):
                        ii = tensor.matmul(out=ps_xz[:, mi % 2, 0:LC1],
                                           lhsT=winT[:, (kj * 8 + mi) * 128:(kj * 8 + mi + 1) * 128],
                                           rhs=xnf_s[:, c % 2, kj, :],
                                           start=(kj == 0), stop=(kj == 7))
                    ii.then_inc(s1_xz, 1)

            def p1_wx(c):
                tensor.wait_ge(s1_mul, c + 1)
                _w(tensor, s1_pte, c)
                for dj in range(4):
                    ii = tensor.matmul(out=ps_pt[:, 0:LC1], lhsT=wxT[:, dj * 96:(dj + 1) * 96],
                                       rhs=ub4_s[:, c % 2, dj, :], start=(dj == 0), stop=(dj == 3))
                ii.then_inc(s1_pt, 1)

            for c in range(N1):
                p1_tp_mm(c)
                if c > 0:
                    p1_wx(c - 1)
            p1_wx(N1 - 1)
            tensor.wait_ge(s2_wdma, 96)
            tensor.wait_ge(s1_xze, 8 * N1)
            for dj in range(4):
                _w(tensor, s2_dpu, dj)
                tensor.transpose(out=ps_w2[:, :], in_=wdt_st[:, dj, :], identity=id_s[:]
                                 ).then_inc(s2_dp, 1)
            ps_y4 = [ps_tp.ap()[:, 0, 0:LS], ps_tp.ap()[:, 1, 0:LS],
                     ps_xz.ap()[:, 0, 0:LS], ps_xz.ap()[:, 1, 0:LS]]
            def p2_dpre(c):
                tensor.wait_ge(s2_sum, 2 * c + 1)
                for dj in range(4):
                    gidx = 4 * c + dj
                    _w(tensor, s2_dpu, 4 + gidx - 1)
                    tensor.matmul(out=ps_dp[:, dj % 2, 0:LS], lhsT=wdtT[:, dj, :],
                                  rhs=dtb16[:, c % 2, :], start=True, stop=True).then_inc(s2_dp, 1)

            p2_dpre(0)
            for c in range(N2):
                tensor.wait_ge(s2_sum, 2 * c + 2)
                _w(tensor, s2_ydu, c)

                def rep(n):
                    ridx = NST * c + n
                    _w(tensor, s2_repe, ridx)
                    tensor.matmul(out=ps_rep[:, 0, :], lhsT=en_bf[:, n * 128:(n + 1) * 128],
                                  rhs=bps_bf[:, c % 2, :], start=True, stop=True)
                    tensor.matmul(out=ps_rep[:, 1, :], lhsT=en_bf[:, n * 128:(n + 1) * 128],
                                  rhs=cps_bf[:, c % 2, :], start=True, stop=True).then_inc(s2_rep, 1)
                rep(0)
                rep(1)
                for n in range(NST):
                    if n == 11 and c + 1 < N2:
                        p2_dpre(c + 1)
                    if n + 2 < NST:
                        rep(n + 2)
                    for dj in range(4):
                        tensor.wait_ge(s2_hc, 64 * c + n * 4 + dj + 1)
                        ii = tensor.matmul(out=ps_y4[dj], lhsT=id_bf[:],
                                           rhs=hc_sb[:, n % 2, dj, :],
                                           start=(n == 0), stop=(n == NST - 1))
                    ii.then_inc(s2_hcu, 1)
                tensor.nop().then_inc(s2_yd, 1)
            tensor.wait_ge(s2_ydu, N2)
            tensor.wait_ge(s3_w, 32)
            for m2 in range(2):
                for kj in range(16):
                    idx = m2 * 16 + kj
                    _w(tensor, s3_ev, idx - 1)
                    tensor.transpose(out=ps_tp[:, idx % 2, 0:128],
                                     in_=xst[:, m2 * 2048 + kj * 128: m2 * 2048 + (kj + 1) * 128],
                                     identity=id_s[:]).then_inc(s3_tp, 1)
            tensor.wait_ge(s3_w, 64)
            for m2 in range(2):
                for kj in range(8):
                    idx = 32 + m2 * 8 + kj
                    _w(tensor, s3_ev, idx - 1)
                    tensor.transpose(out=ps_tp[:, idx % 2, 0:128],
                                     in_=xst[:, m2 * 1024 + kj * 128: m2 * 1024 + (kj + 1) * 128],
                                     identity=id_s[:]).then_inc(s3_tp, 1)
            tensor.wait_ge(s3_ev, 48)
            for c in range(N3):
                tensor.wait_ge(s3_ld[c % 2], 32 * (c // 2 + 1))
                tensor.wait_ge(s3_lds[c % 2], 16 * (c // 2 + 1))
                for fi in range(2):
                    gidx = 2 * c + fi
                    _w(tensor, s3_sum, gidx - 1)
                    _w(tensor, s3_ff, gidx - 1)
                    for kj in range(16):
                        tensor.matmul(out=ps_xz[:, fi % 2, 0:LC3],
                                      lhsT=woT[:, (kj * 2 + fi) * 128:(kj * 2 + fi + 1) * 128],
                                      rhs=y2c_s[:, c % 2, kj, :],
                                      start=(kj == 0), stop=(kj == 15))
                    for kj in range(8):
                        ii = tensor.matmul(out=ps_dp[:, fi % 2, 0:LC3],
                                           lhsT=cbT[:, (kj * 2 + fi) * 128:(kj * 2 + fi + 1) * 128],
                                           rhs=xnc_s[:, c % 2, kj, :],
                                           start=(kj == 0), stop=(kj == 7))
                    ii.then_inc(s3_mm, 1)
                tensor.wait_ge(s3_sum, 2 * (c + 1))
                for i in range(2):
                    tidx = 2 * c + i
                    _w(tensor, s3_add, tidx - 1)
                    for fi in range(2):
                        ii = tensor.transpose(out=ps_tp[:, i % 2, fi * 128:(fi + 1) * 128],
                                              in_=af_s[:, fi, i * 128:(i + 1) * 128],
                                              identity=id_s[:])
                    ii.then_inc(s3_tt, 1)

        @block.scalar
        def _(scalar):
            for idx in range(64):
                mi, kj = idx // 8, idx % 8
                scalar.wait_ge(s1_tp, idx + 1)
                scalar.copy(out=winT[:, (kj * 8 + mi) * 128:(kj * 8 + mi + 1) * 128],
                            in_=ps_tp[:, idx % 2, 0:128]).then_inc(s1_tpe, 1)
            for dj in range(4):
                idx = 64 + dj
                scalar.wait_ge(s1_tp, idx + 1)
                scalar.copy(out=wxT[:, dj * 96:(dj + 1) * 96],
                            in_=ps_tp[:, idx % 2, 0:96]).then_inc(s1_tpe, 1)
            scalar.wait_ge(s_pst, N1)
            scalar.activation(out=rst_s[:], in_=mva_s[:, :, 1:2], func=AF.Ln, bias=EPS)
            scalar.activation(out=rst_s[:], in_=rst_s[:], func=AF.Exp, scale=-0.5).then_inc(s_rsl, 1)
            def p1_block(c):
                base = 68 + 8 * c
                _w(scalar, s1_xz, 8 * (c - 1))
                if c >= 2:
                    scalar.wait_ge(s1_out[c % 2], 48 * (c // 2))
                for j in range(8):
                    scalar.wait_ge(s1_tp, base + j + 1)
                    scalar.copy(out=xnf_s[:, c % 2, j, :], in_=ps_tp[:, j % 2, 0:LC1]).then_inc(s1_tpe, 1)
                _w(scalar, s1_mul, c)
                for mi in range(8):
                    gidx = 8 * c + mi
                    scalar.wait_ge(s1_xz, gidx + 1)
                    scalar.copy(out=xz_s[:, mi, :], in_=ps_xz[:, mi % 2, 0:LC1]).then_inc(s1_xze, 1)
                scalar.wait_ge(s1_conv, c + 1)
                for dj in range(4):
                    scalar.activation(out=sig_s[:, dj, :], in_=conv_c[:, dj, :], func=AF.Sigmoid)
                for dj in range(4):
                    ii = scalar.activation(out=sig_s[:, 4 + dj, :], in_=xz_s[:, 4 + dj, :],
                                           func=AF.Sigmoid)
                ii.then_inc(s1_sig, 1)
            def p1_pt(c):
                scalar.wait_ge(s1_pt, c + 1)
                if c >= 2:
                    scalar.wait_ge(s1_ptd[c % 2], 16 * (c // 2))
                scalar.copy(out=pt_s[:, c % 2, :], in_=ps_pt[:, 0:LC1]).then_inc(s1_pte, 1)

            for c in range(N1):
                if c > 0:
                    p1_block(c - 1)
                if c > 1:
                    p1_pt(c - 2)
            p1_block(N1 - 1)
            p1_pt(N1 - 2)
            p1_pt(N1 - 1)
            scalar.wait_ge(s2_wdma, 96)
            scalar.activation(out=a_sb.ap().rearrange("p a b -> p (a b)"),
                              in_=a_sb.ap().rearrange("p a b -> p (a b)"),
                              func=AF.Exp).then_inc(s2_ax, 1)
            for dj in range(4):
                scalar.wait_ge(s2_dp, dj + 1)
                scalar.copy(out=wdtT[:, dj, :], in_=ps_w2[:, :]).then_inc(s2_dpu, 1)
            scalar.wait_ge(s1_out[0], 48 * (N1 // 2))
            scalar.wait_ge(s1_out[1], 48 * (N1 // 2))
            for c in range(N2):
                scalar.wait_ge(s2_nz, c + 1)
                for dj in range(4):
                    scalar.activation(out=sp_sb[:, dj, :], in_=nz_sb[:, dj, :], func=AF.Exp)
                for dj in range(4):
                    ii = scalar.activation(out=sp_sb[:, dj, :], in_=sp_sb[:, dj, :], func=AF.Ln,
                                           bias=1.0)
                ii.then_inc(s2_sp, 1)
                scalar.wait_ge(s2_del, c + 1)
                for n in range(NST):
                    ridx = NST * c + n
                    scalar.wait_ge(s2_rep, ridx + 1)
                    _w(scalar, s2_rbu, ridx - 1)
                    scalar.copy(out=rep_bf[:, n % 2, 0, :], in_=ps_rep[:, 0, :])
                    scalar.copy(out=rep_bf[:, n % 2, 1, :], in_=ps_rep[:, 1, :]).then_inc(s2_repe, 1)
                    for dj in range(4):
                        idx = (c * NST + n) * 4 + dj
                        _w(scalar, s2_dau, idx - 1)
                        scalar.activation(out=da_sb[:, idx % 2, :], in_=del_sb[:, dj, :],
                                          func=AF.Exp, scale=a_sb[:, dj, n:n + 1]).then_inc(s2_da, 1)
            for idx in range(32):
                m2, kj = idx // 16, idx % 16
                scalar.wait_ge(s3_tp, idx + 1)
                scalar.copy(out=woT[:, (kj * 2 + m2) * 128:(kj * 2 + m2 + 1) * 128],
                            in_=ps_tp[:, idx % 2, 0:128]).then_inc(s3_ev, 1)
            for idx in range(32, 48):
                m2, kj = (idx - 32) // 8, (idx - 32) % 8
                scalar.wait_ge(s3_tp, idx + 1)
                scalar.copy(out=cbT[:, (kj * 2 + m2) * 128:(kj * 2 + m2 + 1) * 128],
                            in_=ps_tp[:, idx % 2, 0:128]).then_inc(s3_ev, 1)
            scalar.wait_ge(s2_wdma, 96)
            scalar.mul(out=scb_s[:], in_=scb_s[:], mul=1.0 / 127.0)
            for c in range(N3):
                for fi in range(2):
                    gidx = 2 * c + fi
                    scalar.wait_ge(s3_mm, gidx + 1)
                    _w(scalar, s3_sum, gidx - 1)
                    scalar.activation(out=ffn_s[:, fi % 2, :], in_=ps_dp[:, fi % 2, 0:LC3],
                                      func=AF.Identity, scale=scb_s[:, fi:fi + 1]).then_inc(s3_ff, 1)

        @block.vector
        def _(vector):
            vector.wait_ge(s_io, 1)
            vector.tensor_scalar(out=id_s[:], in0=io_s[:], scalar1=0.0, op0=ALU.is_equal,
                                 scalar2=None)
            for n in range(NST):
                ii = vector.tensor_scalar(out=en_bf[:, n * 128:(n + 1) * 128], in0=io16[:],
                                          scalar1=float(n), op0=ALU.is_equal, scalar2=None)
            ii.then_inc(s_io, 1)
            vector.tensor_copy(out=id_bf[:], in_=id_s[:])
            vector.memset(cst_s[:, 0:1], EPS)
            vector.memset(cst_s[:, 1:2], 1.0)
            vector.memset(pad_s.ap().rearrange("p a b -> p (a b)"), 0.0)
            vector.memset(state.ap().rearrange("p a b -> p (a b)"), 0.0)
            for c in range(N1):
                vector.wait_ge(s_px[c % 2], 16 * (c // 2 + 1))
                for i in range(NT1):
                    vector.bn_stats(out=bst_s[:, i, 0:6], in_=x_sb[:, c % 2, i, 0:512])
                    vector.bn_stats(out=bst_s[:, i, 6:12], in_=x_sb[:, c % 2, i, 512:1024])
                for i in range(NT1):
                    tg = c * NT1 + i
                    ii = vector.bn_aggr(out=mva_s[:, tg, :],
                                        in_=bst_s.ap()[:, i].rearrange("p (a b) -> p a b", b=3))
                ii.then_inc(s_pst, 1)
            vector.wait_ge(s_rsl, 1)
            vector.scalar_tensor_tensor(out=mrs_s[:], in0=mva_s[:, :, 0:1].rearrange("p a b -> p (a b)"),
                                        scalar=-1.0, in1=rst_s[:],
                                        op0=ALU.mult, op1=ALU.mult).then_inc(s_mrs, 1)
            def p1_affine(c):
                vector.wait_ge(s1_x[c % 2], 16 * (c // 2 + 1))
                if c >= 2:
                    vector.wait_ge(s1_tp, 68 + 8 * (c - 1))
                for i in range(NT1):
                    tg = c * NT1 + i
                    ii = vector.tensor_scalar(out=xnt_s[:, c % 2, i, :], in0=x_sb[:, c % 2, i, :],
                                              scalar1=mva_s[:, tg, 0:1], op0=ALU.subtract,
                                              scalar2=rst_s[:, tg:tg + 1], op1=ALU.mult)
                ii.then_inc(s1_aff, 1)

            p1_affine(0)
            for c in range(N1):
                if c + 1 < N1:
                    p1_affine(c + 1)
                vector.wait_ge(s1_xze, 8 * c + 4)
                for dj in range(4):
                    vector.tensor_copy(out=pad_s[:, dj, 3:3 + LC1], in_=xz_s[:, dj, :])
                for dj in range(4):
                    vector.tensor_scalar(out=cva_s[:], in0=pad_s[:, dj, 0:LC1],
                                         scalar1=cw_s[:, 4 * dj:4 * dj + 1], op0=ALU.mult,
                                         scalar2=cb_s[:, dj:dj + 1], op1=ALU.add)
                    vector.scalar_tensor_tensor(out=cvb_s[:], in0=pad_s[:, dj, 1:1 + LC1],
                                                scalar=cw_s[:, 4 * dj + 1:4 * dj + 2],
                                                in1=cva_s[:], op0=ALU.mult, op1=ALU.add)
                    vector.scalar_tensor_tensor(out=cva_s[:], in0=pad_s[:, dj, 2:2 + LC1],
                                                scalar=cw_s[:, 4 * dj + 2:4 * dj + 3],
                                                in1=cvb_s[:], op0=ALU.mult, op1=ALU.add)
                    vector.scalar_tensor_tensor(out=conv_c[:, dj, :], in0=pad_s[:, dj, 3:3 + LC1],
                                                scalar=cw_s[:, 4 * dj + 3:4 * dj + 4],
                                                in1=cva_s[:], op0=ALU.mult, op1=ALU.add)
                    ii = vector.tensor_copy(out=pad_s[:, dj, 0:3], in_=pad_s[:, dj, LC1:LC1 + 3])
                ii.then_inc(s1_conv, 1)
                vector.wait_ge(s1_sig, c + 1)
                if c >= 2:
                    vector.wait_ge(s1_out[c % 2], 48 * (c // 2))
                for dj in range(4):
                    vector.tensor_mul(out=u1_s[:, c % 2, dj, :], in0=conv_c[:, dj, :], in1=sig_s[:, dj, :])
                _w(vector, s1_pt, c - 1)
                for dj in range(4):
                    vector.tensor_copy(out=ub4_s[:, c % 2, dj, :], in_=u1_s[:, c % 2, dj, :])
                if c >= 1:
                    vector.wait_ge(s1_out[(c - 1) % 2], 48 * ((c - 1) // 2 + 1))
                for dj in range(4):
                    ii = vector.tensor_mul(out=g1_s[:, dj, :], in0=xz_s[:, 4 + dj, :],
                                           in1=sig_s[:, 4 + dj, :])
                ii.then_inc(s1_mul, 1)
            vector.wait_ge(s2_ax, 1)
            vector.tensor_scalar(out=a_sb.ap().rearrange("p a b -> p (a b)"),
                                 in0=a_sb.ap().rearrange("p a b -> p (a b)"),
                                 scalar1=-1.0, op0=ALU.mult, scalar2=None)
            def p2_delta_a(c):
                vector.wait_ge(s2_ld[c % 2], 80 * (c // 2 + 1))
                _w(vector, s2_sp, c)
                _w(vector, s2_dp, 4 + 4 * c)
                vector.tensor_copy(out=dtb16[:, c % 2, :], in_=dtl_sb[:, c % 2, :]).then_inc(s2_sum, 1)
                _w(vector, s2_rep, NST * (c - 1))
                vector.tensor_copy(out=bps_bf[:, c % 2, :], in_=bps_sb[:, c % 2, :])
                vector.tensor_copy(out=cps_bf[:, c % 2, :], in_=cps_sb[:, c % 2, :]).then_inc(s2_sum, 1)
                for dj in range(4):
                    vector.wait_ge(s2_dp, 4 + 4 * c + dj + 1)
                    vector.tensor_scalar(out=z_sb[:, dj, :], in0=ps_dp[:, dj % 2, 0:LS],
                                         scalar1=dtb_s[:, dj:dj + 1], op0=ALU.add,
                                         scalar2=None).then_inc(s2_dpu, 1)
                for dj in range(4):
                    ii = vector.scalar_tensor_tensor(out=nz_sb[:, dj, :], in0=z_sb[:, dj, :],
                                                     scalar=-1.0, in1=z_sb[:, dj, :],
                                                     op0=ALU.mult, op1=ALU.min)
                ii.then_inc(s2_nz, 1)

            def p2_delta_b(c):
                vector.wait_ge(s2_sp, c + 1)
                for dj in range(4):
                    ii = vector.scalar_tensor_tensor(out=del_sb[:, dj, :], in0=z_sb[:, dj, :],
                                                     scalar=0.0, in1=sp_sb[:, dj, :],
                                                     op0=ALU.max, op1=ALU.add)
                ii.then_inc(s2_del, 1)
                for dj in range(4):
                    vector.tensor_mul(out=du_sb[:, dj, :], in0=del_sb[:, dj, :], in1=u_sb[:, c % 2, dj, :])
                for dj in range(4):
                    vector.tensor_copy(out=du_bf[:, dj, :], in_=du_sb[:, dj, :])

            def p2_delta(c):
                p2_delta_a(c)
                p2_delta_b(c)

            p2_delta(0)
            for c in range(N2):
                for n in range(NST):
                    vector.wait_ge(s2_repe, NST * c + n + 1)
                    for dj in range(4):
                        idx = (c * NST + n) * 4 + dj
                        vector.wait_ge(s2_da, idx + 1)
                        vector.tensor_mul(out=dbu_sb[:], in0=du_bf[:, dj, :], in1=rep_bf[:, n % 2, 0, :])
                        vector.tensor_tensor_scan(out=h_sb[:], data0=da_sb[:, idx % 2, :], data1=dbu_sb[:],
                                                  initial=state[:, dj, n:n + 1],
                                                  op0=ALU.mult, op1=ALU.add).then_inc(s2_dau, 1)
                        vector.tensor_copy(out=state[:, dj, n:n + 1], in_=h_sb[:, LS - 1:LS])
                        _w(vector, s2_hcu, 16 * c + n - 1)
                        vector.tensor_mul(out=hc_sb[:, n % 2, dj, :], in0=h_sb[:],
                                           in1=rep_bf[:, n % 2, 1, :]).then_inc(s2_hc, 1)
                        if dj == 3:
                            vector.tensor_copy(out=dbu_sb[:, 0:1],
                                               in_=dbu_sb[:, 0:1]).then_inc(s2_rbu, 1)
                    if c + 1 < N2 and n == 10:
                        p2_delta_a(c + 1)
                if c + 1 < N2:
                    p2_delta_b(c + 1)
                if c >= 2:
                    vector.wait_ge(s2_st[c % 2], 16 * (c // 2))
                vector.wait_ge(s2_yd, c + 1)
                ps_y4v = [ps_tp.ap()[:, 0, 0:LS], ps_tp.ap()[:, 1, 0:LS],
                          ps_xz.ap()[:, 0, 0:LS], ps_xz.ap()[:, 1, 0:LS]]
                for dj in range(4):
                    vector.scalar_tensor_tensor(out=ydu_sb[:], in0=u_sb[:, c % 2, dj, :],
                                                scalar=dsk_s[:, dj:dj + 1],
                                                in1=ps_y4v[dj],
                                                op0=ALU.mult, op1=ALU.add)
                    ii = vector.tensor_mul(out=y2_sb[:, c % 2, dj, :], in0=ydu_sb[:],
                                           in1=g2_sb[:, c % 2, dj, :])
                ii.then_inc(s2_y2, 1)
                vector.nop().then_inc(s2_ydu, 1)
                vector.tensor_copy(out=dbu_sb[:, 0:1], in_=dbu_sb[:, 0:1]).then_inc(s2_used, 1)
            for c in range(N3):
                for fi in range(2):
                    gidx = 2 * c + fi
                    vector.wait_ge(s3_ff, gidx + 1)
                    vector.scalar_tensor_tensor(out=af_s[:, fi, :], in0=ps_xz[:, fi % 2, 0:LC3],
                                                scalar=bias_s[:, fi:fi + 1], in1=ffn_s[:, fi % 2, :],
                                                op0=ALU.add, op1=ALU.add).then_inc(s3_sum, 1)
                _w(vector, s3_st, 16 * c)
                vector.wait_ge(s3_xc[c % 2], 16 * (c // 2 + 1))
                for i in range(2):
                    tidx = 2 * c + i
                    vector.wait_ge(s3_tt, tidx + 1)
                    vector.tensor_add(out=out_s[:, i, :],
                                      in0=ps_tp[:, i % 2, 0:FS], in1=xcol_s[:, c % 2, i, :]
                                      ).then_inc(s3_add, 1)
    return nc


def kernel(L0_attn_hidden_states, L0_attn_W_in, L0_attn_conv_w, L0_attn_conv_b,
           L0_attn_W_x, L0_attn_W_dt, L0_attn_dt_bias, L0_attn_A_log,
           L0_attn_D_skip, L0_attn_W_out, L0_ffn_CB, L0_ffn_SCB, L0_ffn_bias):
    x = np.ascontiguousarray(np.asarray(L0_attn_hidden_states, dtype=np.float32))
    W_in = np.asarray(L0_attn_W_in, dtype=np.float32)
    conv_w = np.asarray(L0_attn_conv_w, dtype=np.float32)
    conv_b = np.asarray(L0_attn_conv_b, dtype=np.float32)
    W_x = np.asarray(L0_attn_W_x, dtype=np.float32)
    W_dt = np.asarray(L0_attn_W_dt, dtype=np.float32)
    dt_bias = np.asarray(L0_attn_dt_bias, dtype=np.float32)
    A_log = np.asarray(L0_attn_A_log, dtype=np.float32)
    D_skip = np.asarray(L0_attn_D_skip, dtype=np.float32)
    W_out = np.asarray(L0_attn_W_out, dtype=np.float32)
    CB = np.asarray(L0_ffn_CB, dtype=np.float32)
    SCB = np.asarray(L0_ffn_SCB, dtype=np.float32)
    fbias = np.asarray(L0_ffn_bias, dtype=np.float32)
    C = np.ascontiguousarray

    nc = build_fused()
    maps = []
    for cid in range(NC):
        b, s = cid // 4, cid % 4
        sl = slice(ES * s, ES * (s + 1))
        fs = slice(FS * s, FS * (s + 1))
        maps.append(dict(
            x=C(x[b]),
            w_in=C(np.concatenate([W_in[ES * s:ES * (s + 1)], W_in[E + ES * s:E + ES * (s + 1)]], axis=0)),
            conv_w=C(conv_w[sl]),
            conv_b=C(conv_b[sl].reshape(4, 128).T),
            w_x=C(W_x[:, sl]),
            w_dt=C(W_dt[sl]),
            dt_bias=C(dt_bias[sl].reshape(4, 128).T),
            a_log=C(A_log[sl]),
            d_skip=C(D_skip[sl].reshape(4, 128).T),
            w_out_s=C(W_out[fs]),
            cb_sl=C(CB[fs]),
            scb_v=C(SCB[fs].reshape(2, 128).T),
            bias_v=C(fbias[fs].reshape(2, 128).T),
            xcol=C(x[b][:, fs]),
        ))
    res = run_bass_kernel_spmd(nc, maps, core_ids=list(range(NC))).results

    out = np.empty((B, L, D), dtype=np.float32)
    for cid in range(NC):
        b, s = cid // 4, cid % 4
        out[b, :, FS * s:FS * (s + 1)] = res[cid]["out"]
    return out
